# version 13
# speedup vs baseline: 1.0325x; 1.0251x over previous
import contextlib
import math
import numpy as np

import concourse.bass as bass
import concourse.mybir as mybir
from concourse.bass_utils import run_bass_kernel_spmd

F32 = mybir.dt.float32
BF16 = mybir.dt.bfloat16
AF = mybir.ActivationFunctionType
OP = mybir.AluOpType
AX = mybir.AxisListType.X

TC = 2048
TT = 512
NTILE = TC // TT
L = 128
NCH = TC // L
D = 1024
G8 = 8
DIN = 2048
G16 = 16
DST = 16
HID = 4096
G32 = 32
DTR = 64
MAGIC = float(np.float32(1.5 * 2 ** 23))
LN_EPS = 1e-5
FCONSTS = [1e-5, -0.5, -1.0, 1.0, 0.5, 127.0, 1.0 / 127.0,
           float(np.log(127.0)), 1.5 * 2.0 ** 23]


class Ctr:

    def __init__(self, es, nc, name):
        self.h = es.enter_context(nc.semaphore(name))
        self.n = 0

    def inc(self, instr, k=1):
        instr.then_inc(self.h, k)
        self.n += k
        return self.n


class Builder:
    def __init__(self, nc, taps=None):
        self.nc = nc
        self.es = contextlib.ExitStack()
        self.taps = taps if taps is not None else []
        self.tap_tensors = {}
        self._dram = {}
        self._sb = {}

    def dram(self, name, shape, dtype=F32, kind="Internal"):
        t = self.nc.dram_tensor(name, list(shape), dtype, kind=kind)
        self._dram[name] = t
        return t.ap()

    def sb(self, name, shape, dtype):
        t = self.es.enter_context(self.nc.sbuf_tensor(name, list(shape), dtype))
        self._sb[name] = t
        return t

    def ps(self, name, shape=(128, 512), dtype=F32):
        t = self.es.enter_context(self.nc.psum_tensor(name, list(shape), dtype))
        return t

    def tap(self, name, shape, dtype=F32):
        if name in self.taps:
            ap = self.dram("tap_" + name, shape, dtype, kind="ExternalOutput")
            self.tap_tensors[name] = ap
            return ap
        return None


def build_program(taps=None, upto=99):
    nc = bass.Bass("TRN2", target_bir_lowering=False)
    b = Builder(nc, taps)
    b.upto = upto
    d = declare_io(b)
    with b.es:
        emit(b, d)
    return nc, b


def declare_io(b):
    d = {}
    I = lambda n, s, dt=F32: d.__setitem__(n, b.dram(n, s, dt, kind="ExternalInput"))
    I("x_fm", [G8, 128, TC])
    d["y_out"] = b.dram("y_out", [G8, 128, TC], F32, kind="ExternalOutput")
    I("qkvg_w", [4, G8, 128, 128])
    I("qkvg_b", [128, 4])
    I("ab_w", [G8, 128, 2])
    I("ab_b", [128, 1])
    I("wo_w", [128, D])
    I("bo_c", [128, G8])
    I("win_w", [G8, 128, HID])
    I("convw_c", [128, G16, 4])
    I("convb_c", [128, G16])
    I("wx_w", [G16, 128, 96])
    I("wdt_w", [DTR, DIN])
    I("bdt_c", [128, G16])
    I("alog_c", [128, G16, DST])
    I("dd_c", [128, G16])
    I("gamma_c", [128, G8])
    I("beta_c", [128, G8])
    I("wout_w", [G16, 128, D])
    I("w1t", [G8, 128, HID])
    I("w2t", [G32, 128, D])
    I("identf", [128, 128])
    I("identb", [128, 128], BF16)
    I("mask_sn", [128, 128])
    I("mask_snT", [128, 128])
    I("mask_incT", [128, 128])
    I("ones_k1", [1, 128], BF16)
    I("ones_k1f", [1, 128])
    I("ones128", [128, 128], BF16)
    I("sel127", [128, 128])
    I("ones128f", [128, 128])
    I("maskpred_c", [128, 8])
    I("maskprev_c", [128, 8])
    I("sel32", [128, 32 * 128], BF16)
    I("maskhalo_c", [128, 1])
    I("fconst_c", [128, len(FCONSTS)])
    d["w1q_s"] = b.dram("w1q_s", [G8, 128, HID], BF16)
    d["w2q_s"] = b.dram("w2q_s", [G32, 128, D], BF16)
    d["delta_s"] = b.dram("delta_s", [G16, 128, TC], BF16)
    d["m_s"] = b.dram("m_s", [G16, 128, TC], BF16)
    d["xcd_s"] = b.dram("xcd_s", [G16, 128, TC], BF16)
    d["sz_s"] = b.dram("sz_s", [G16, 128, TC], BF16)
    d["bc_s"] = b.dram("bc_s", [2 * DST, TC], BF16)
    d["row_s"] = b.dram("row_s", [16, TC])
    d["cc1_in"] = b.dram("cc1_in", [256, 128])
    d["cc1_out"] = b.dram("cc1_out", [8, 256, 128])
    d["cc2_in"] = b.dram("cc2_in", [128, 2 * G16 * DST])
    d["cc2_out"] = b.dram("cc2_out", [8, 128, 2 * G16 * DST])
    d["cch_in"] = b.dram("cch_in", [128, 48], BF16)
    d["cch_out"] = b.dram("cch_out", [8, 128, 48], BF16)
    d["xin_s"] = b.dram("xin_s", [G16, 128, TC], BF16)
    d["y_s"] = b.dram("y_s", [G16, 128, TC], BF16)
    d["lnx3_s"] = b.dram("lnx3_s", [G8, 128, TC], BF16)
    return d


def emit(b, d):
    nc = b.nc
    es = b.es
    upto = getattr(b, "upto", 99)
    resid = b.sb("resid", [128, G8, TC], F32)
    consts_f = b.sb("consts_f", [128, 6 * 128], F32)
    identb = b.sb("identb_sb", [128, 128], BF16)
    ones_k1 = b.sb("ones_k1_sb", [1, 128], BF16)
    ones_k1f = b.sb("ones_k1f_sb", [1, 128], F32)
    ones128 = b.sb("ones128_sb", [128, 128], BF16)
    biases = b.sb("biases", [128, 16], F32)
    gamma_sb = b.sb("gamma_sb", [128, G8], F32)
    beta_sb = b.sb("beta_sb", [128, G8], F32)
    maskpred = b.sb("maskpred_sb", [128, 8], F32)
    maskhalo = b.sb("maskhalo_sb", [128, 1], F32)
    maskprev_sb = b.sb("maskprev_sb2", [128, 8], F32)
    fconst = b.sb("fconst_sb", [128, len(FCONSTS)], F32)
    for _i, _v in enumerate(FCONSTS):
        nc.const_aps.register(F32, float(_v), fconst[:, _i:_i + 1])
    murep = b.sb("murep", [128, TT], F32)
    rsvrep = b.sb("rsvrep", [128, TT], F32)
    scr_a = b.sb("scr_a", [128, TT], F32)
    scr_c = b.sb("scr_c", [128, TT], F32)
    scr_u = b.sb("scr_u", [128, TT], F32)
    xbuf = b.sb("xbuf", [128, G8, TT], BF16)
    sqbuf = b.sb("sqbuf", [128, G8, TT], BF16)

    identf = consts_f[:, 0:128]
    mask_sn = consts_f[:, 128:256]
    mask_snT = consts_f[:, 256:384]
    mask_incT = consts_f[:, 384:512]
    sel127 = consts_f[:, 512:640]
    ones128f = consts_f[:, 640:768]

    PS = [b.ps(f"ps{i}") for i in range(8)]

    def PB(bank, slot=0):
        return PS[bank][:, slot * 64:(slot + 1) * 64].bitcast(BF16)

    sT = Ctr(es, nc, "sT")
    sV = Ctr(es, nc, "sV")
    sS = Ctr(es, nc, "sS")
    sP = Ctr(es, nc, "sP")
    dS = Ctr(es, nc, "dS")
    dG = Ctr(es, nc, "dG")
    dG2 = Ctr(es, nc, "dG2")
    dA = Ctr(es, nc, "dA")
    dA2 = Ctr(es, nc, "dA2")
    cS = Ctr(es, nc, "cS")
    ALL = [sT, sV, sS, sP, dS, dG, dG2, dA, dA2, cS]
    dW1 = Ctr(es, nc, "dW1")
    dW2 = Ctr(es, nc, "dW2")
    dW3 = Ctr(es, nc, "dW3")
    dW4 = Ctr(es, nc, "dW4")
    dW5 = Ctr(es, nc, "dW5")
    dW6 = Ctr(es, nc, "dW6")

    block = es.enter_context(nc.Block())

    def barrier():
        vals = [c.n for c in ALL]

        def mk():
            def f(e):
                for c, v in zip(ALL, vals):
                    if v:
                        e.wait_ge(c.h, v)
            return f
        block.tensor(mk())
        block.vector(mk())
        block.scalar(mk())
        block.gpsimd(mk())
        block.sync(mk())

    def tap_now(name, aps_shapes):
        if not b.taps or name not in b.taps:
            return
        barrier()
        for i, (ap, shape, dt) in enumerate(aps_shapes):
            t = b.dram(f"tap_{name}_{i}", shape, dt, kind="ExternalOutput")
            b.tap_tensors[f"{name}_{i}"] = t

            def tp(e, ap=ap, t=t):
                e.dma_start(out=t, in_=ap).then_inc(dS.h, 16)
            block.sync(tp)
            dS.n += 16
        barrier()

    def ln_fm(src_g, dst, store_stats=None, tiles=None, entry=None):
        recip_n = 1.0 / D
        tl = list(tiles) if tiles is not None else list(range(NTILE))
        t0, v0, s0 = sT.n, sV.n, sS.n
        NV, NS, NT_ = 6, 3, 1
        ew = entry or {}

        def lv(e):
            for hh, vv in ew.get("v", []):
                e.wait_ge(hh, vv)
            for it_i, itt in enumerate(tl):
                it = it_i
                ts = slice(itt * TT, (itt + 1) * TT)
                if it > 0:
                    e.wait_ge(sT.h, t0 + it)
                for g in range(G8 - 1):
                    e.tensor_copy(out=xbuf[:, g, :], in_=src_g(g, ts))
                e.tensor_copy(out=xbuf[:, G8 - 1, :],
                              in_=src_g(G8 - 1, ts)).then_inc(sV.h, 1)
                e.wait_ge(sT.h, t0 + it + 1)
                e.tensor_scalar(out=murep[:], in0=PS[6][:], scalar1=recip_n,
                                scalar2=None, op0=OP.mult)
                e.tensor_scalar(out=scr_c[:], in0=PS[7][:], scalar1=recip_n,
                                scalar2=None, op0=OP.mult).then_inc(sV.h, 1)
                e.wait_ge(sV.h, v0 + NV * it + 2)
                e.tensor_mul(out=scr_a[:], in0=murep[:],
                             in1=murep[:]).then_inc(sV.h, 1)
                e.wait_ge(sV.h, v0 + NV * it + 3)
                e.tensor_sub(out=scr_a[:], in0=scr_c[:],
                             in1=scr_a[:]).then_inc(sV.h, 1)
                e.wait_ge(sS.h, s0 + NS * it + 2)
                for g in range(G8 - 1):
                    e.tensor_sub(out=sqbuf[:, g, :], in0=src_g(g, ts), in1=murep[:])
                e.tensor_sub(out=sqbuf[:, G8 - 1, :], in0=src_g(G8 - 1, ts),
                             in1=murep[:]).then_inc(sV.h, 1)
                e.wait_ge(sS.h, s0 + NS * it + 3)
                e.wait_ge(sV.h, v0 + NV * it + 5)
                for g in range(G8 - 1):
                    e.tensor_mul(out=dst(g, ts), in0=sqbuf[:, g, :], in1=rsvrep[:])
                ii = e.tensor_mul(out=dst(G8 - 1, ts), in0=sqbuf[:, G8 - 1, :],
                                  in1=rsvrep[:])
                if store_stats is not None:
                    e.tensor_copy(out=store_stats[0](it), in_=murep[:])
                    ii = e.tensor_copy(out=store_stats[1](it), in_=rsvrep[:])
                ii.then_inc(sV.h, 1)
        block.vector(lv)

        def ls(e):
            for hh, vv in ew.get("s", []):
                e.wait_ge(hh, vv)
            for it_i, itt in enumerate(tl):
                it = it_i
                e.wait_ge(sV.h, v0 + NV * it + 1)
                if it > 0:
                    e.wait_ge(sT.h, t0 + it)
                for g in range(G8 - 1):
                    e.activation(out=sqbuf[:, g, :], in_=xbuf[:, g, :], func=AF.Square)
                e.activation(out=sqbuf[:, G8 - 1, :], in_=xbuf[:, G8 - 1, :],
                             func=AF.Square).then_inc(sS.h, 1)
                e.wait_ge(sV.h, v0 + NV * it + 4)
                e.activation(out=scr_u[:], in_=scr_a[:], func=AF.Ln,
                             bias=LN_EPS).then_inc(sS.h, 1)
                e.wait_ge(sS.h, s0 + NS * it + 2)
                e.activation(out=rsvrep[:], in_=scr_u[:], func=AF.Exp,
                             scale=-0.5).then_inc(sS.h, 1)
        block.scalar(ls)

        def lt(e):
            for hh, vv in ew.get("t", []):
                e.wait_ge(hh, vv)
            for it_i, itt in enumerate(tl):
                it = it_i
                e.wait_ge(sV.h, v0 + NV * it + 1)
                if it > 0:
                    e.wait_ge(sV.h, v0 + NV * (it - 1) + 2)
                for g in range(G8):
                    e.matmul(PS[6][:], ones128[:], xbuf[:, g, :],
                             start=(g == 0), stop=(g == G8 - 1))
                e.wait_ge(sS.h, s0 + NS * it + 1)
                for g in range(G8 - 1):
                    e.matmul(PS[7][:], ones128[:], sqbuf[:, g, :],
                             start=(g == 0), stop=False)
                e.matmul(PS[7][:], ones128[:], sqbuf[:, G8 - 1, :],
                         start=False, stop=True).then_inc(sT.h, 1)
        block.tensor(lt)

        sV.n += NV * len(tl)
        sS.n += NS * len(tl)
        sT.n += NT_ * len(tl)

    def p0_sync(e):
        for g in range(G8):
            e.dma_start(out=resid[:, g, :], in_=d["x_fm"][g]).then_inc(dS.h, 16)
        e.dma_start(out=consts_f[:, 0:128], in_=d["identf"]).then_inc(dS.h, 16)
        e.dma_start(out=consts_f[:, 128:256], in_=d["mask_sn"]).then_inc(dS.h, 16)
        e.dma_start(out=consts_f[:, 256:384], in_=d["mask_snT"]).then_inc(dS.h, 16)
        e.dma_start(out=consts_f[:, 384:512], in_=d["mask_incT"]).then_inc(dS.h, 16)
        e.dma_start(out=consts_f[:, 512:640], in_=d["sel127"]).then_inc(dS.h, 16)
        e.dma_start(out=consts_f[:, 640:768], in_=d["ones128f"]).then_inc(dS.h, 16)
        e.dma_start(out=biases[:, 5:13], in_=d["bo_c"]).then_inc(dS.h, 16)
        e.dma_start(out=identb[:], in_=d["identb"]).then_inc(dS.h, 16)
        e.dma_start(out=ones_k1[:], in_=d["ones_k1"]).then_inc(dS.h, 16)
        e.dma_start(out=ones_k1f[:], in_=d["ones_k1f"]).then_inc(dS.h, 16)
        e.dma_start(out=ones128[:], in_=d["ones128"]).then_inc(dS.h, 16)
        e.dma_start(out=biases[:, 0:4], in_=d["qkvg_b"]).then_inc(dS.h, 16)
        e.dma_start(out=biases[:, 4:5], in_=d["ab_b"]).then_inc(dS.h, 16)
        e.dma_start(out=gamma_sb[:], in_=d["gamma_c"]).then_inc(dS.h, 16)
        e.dma_start(out=beta_sb[:], in_=d["beta_c"]).then_inc(dS.h, 16)
        e.dma_start(out=maskpred[:], in_=d["maskpred_c"]).then_inc(dS.h, 16)
        e.dma_start(out=maskhalo[:], in_=d["maskhalo_c"]).then_inc(dS.h, 16)
        e.dma_start(out=maskprev_sb[:], in_=d["maskprev_c"]).then_inc(dS.h, 16)
        e.dma_start(out=fconst[:], in_=d["fconst_c"]).then_inc(dS.h, 16)
    block.sync(p0_sync)
    dS.n += 16 * (G8 + 19)

    bq_col = biases[:, 0:1]
    bk_col = biases[:, 1:2]
    bv_col = biases[:, 2:3]
    bg_col = biases[:, 3:4]
    ba_col = biases[0:1, 4:5]
    bb_col = biases[32:33, 4:5]

    barrier()

    with contextlib.ExitStack() as aes:
        asb = lambda n, s, dt: aes.enter_context(nc.sbuf_tensor(n, list(s), dt))
        rows = asb("rows", [128, TC], F32)
        qf = asb("qf", [128, TC], BF16)
        kf = asb("kf", [128, TC], BF16)
        vf = asb("vf", [128, TC], BF16)
        gatef = asb("gatef", [128, TC], BF16)
        wo_sb = asb("wo_sb", [128, D], BF16)
        pes = contextlib.ExitStack()
        psb = lambda n, s, dt: pes.enter_context(nc.sbuf_tensor(n, list(s), dt))
        lnx1 = psb("lnx1", [128, G8, TC], BF16)
        qkvg_sb = psb("qkvg_sb", [128, 4 * G8 * 128], BF16)
        ab_sb = psb("ab_sb", [128, G8 * 2], BF16)

        def pw_gpsimd(e):
            e.dma_start(out=qkvg_sb[:].rearrange("p (j g m) -> p j g m", j=4, g=G8),
                        in_=d["qkvg_w"].rearrange("j g p m -> p j g m")).then_inc(dG.h, 16)
            e.dma_start(out=ab_sb[:].rearrange("p (g m) -> p g m", g=G8),
                        in_=d["ab_w"].rearrange("g p m -> p g m")).then_inc(dG.h, 16)
            e.dma_start(out=wo_sb[:], in_=d["wo_w"]).then_inc(dG.h, 16)
        block.gpsimd(pw_gpsimd)
        dG.n += 16 * 3
        qkvg_v = qkvg_sb[:].rearrange("p (j g m) -> p j g m", j=4, g=G8)
        ab_v = ab_sb[:].rearrange("p (g m) -> p g m", g=G8)

        if upto >= 1:
            ln_fm(lambda g, ts: resid[:, g, ts], lambda g, ts: lnx1[:, g, ts])
            barrier()
            tap_now("lnx1", [(lnx1[:].rearrange("p g t -> p (g t)"),
                              [128, G8 * TC], BF16)])

        if upto >= 2:
            t0, v0, s0 = sT.n, sV.n, sS.n
            NV, NS, NT_ = 3, 5, 2

            def proj_t(e):
                for it in range(NTILE):
                    ts = slice(it * TT, (it + 1) * TT)
                    if it == 0:
                        e.wait_ge(dG.h, 16 * 3)
                    else:
                        e.wait_ge(sV.h, v0 + NV * it)
                        e.wait_ge(sS.h, s0 + NS * it)
                    for j in range(4):
                        for kc in range(G8):
                            e.matmul(PS[j][:], qkvg_v[:, j, kc, :], lnx1[:, kc, ts],
                                     start=(kc == 0), stop=(kc == G8 - 1))
                    for kc in range(G8):
                        e.matmul(PS[4][0:1, :], ab_v[:, kc, 0:1], lnx1[:, kc, ts],
                                 start=(kc == 0), stop=(kc == G8 - 1))
                    for kc in range(G8 - 1):
                        e.matmul(PS[4][32:33, :], ab_v[:, kc, 1:2], lnx1[:, kc, ts],
                                 start=(kc == 0), stop=False, tile_position=(0, 32))
                    e.matmul(PS[4][32:33, :], ab_v[:, G8 - 1, 1:2], lnx1[:, G8 - 1, ts],
                             start=False, stop=True,
                             tile_position=(0, 32)).then_inc(sT.h, 1)
                    e.wait_ge(sS.h, s0 + NS * it + 1)
                    e.matmul(PS[5][:], ones128[:], sqbuf[:, 0, :], start=True,
                             stop=True).then_inc(sT.h, 1)
            block.tensor(proj_t)

            def proj_s(e):
                for it in range(NTILE):
                    ts = slice(it * TT, (it + 1) * TT)
                    e.wait_ge(sT.h, t0 + NT_ * it + 1)
                    if it > 0:
                        e.wait_ge(sT.h, t0 + NT_ * it)
                    e.activation(out=sqbuf[:, 0, :], in_=PS[1][:], func=AF.Square,
                                 bias=bk_col)
                    e.activation(out=gatef[:, ts], in_=PS[3][:], func=AF.Sigmoid,
                                 bias=bg_col)
                    e.activation(out=rows[0:1, ts], in_=PS[4][0:1, :], func=AF.Sigmoid,
                                 bias=ba_col)
                    e.activation(out=rows[32:33, ts], in_=PS[4][32:33, :],
                                 func=AF.Sigmoid, bias=bb_col).then_inc(sS.h, 1)
                    e.wait_ge(sS.h, s0 + NS * it + 1)
                    e.activation(out=rows[64:65, ts], in_=rows[0:1, ts],
                                 func=AF.Ln).then_inc(sS.h, 1)
                    e.wait_ge(sT.h, t0 + NT_ * it + 2)
                    if it > 0:
                        e.wait_ge(sV.h, v0 + NV * it)
                    e.activation(out=scr_a[:], in_=PS[5][:],
                                 func=AF.Sqrt).then_inc(sS.h, 1)
                    e.wait_ge(sV.h, v0 + NV * it + 2)
                    e.activation(out=scr_c[:], in_=scr_a[:],
                                 func=AF.Ln).then_inc(sS.h, 1)
                    e.wait_ge(sS.h, s0 + NS * it + 4)
                    e.activation(out=scr_c[:], in_=scr_c[:], func=AF.Exp,
                                 scale=-1.0).then_inc(sS.h, 1)
            block.scalar(proj_s)

            def proj_v(e):
                for it in range(NTILE):
                    ts = slice(it * TT, (it + 1) * TT)
                    e.wait_ge(sT.h, t0 + NT_ * it + 1)
                    e.tensor_scalar(out=qf[:, ts], in0=PS[0][:], scalar1=bq_col,
                                    scalar2=None, op0=OP.add)
                    e.tensor_scalar(out=vf[:, ts], in0=PS[2][:], scalar1=bv_col,
                                    scalar2=None, op0=OP.add).then_inc(sV.h, 1)
                    e.wait_ge(sS.h, s0 + NS * it + 3)
                    e.tensor_scalar(out=scr_a[:], in0=scr_a[:], scalar1=1e-12,
                                    scalar2=None, op0=OP.max).then_inc(sV.h, 1)
                    e.wait_ge(sS.h, s0 + NS * it + 5)
                    e.scalar_tensor_tensor(out=kf[:, ts], in0=PS[1][:], scalar=bk_col,
                                           in1=scr_c[:], op0=OP.add,
                                           op1=OP.mult).then_inc(sV.h, 1)
            block.vector(proj_v)

            sT.n += NT_ * NTILE
            sS.n += NS * NTILE
            sV.n += NV * NTILE
            barrier()

            tap_now("proj", [
                (qf[:], [128, TC], BF16), (kf[:], [128, TC], BF16),
                (vf[:], [128, TC], BF16), (gatef[:], [128, TC], BF16),
                (rows[:, :], [128, TC], F32),
            ])
        pes.close()

        if upto >= 3:
            lgch = asb("lgch", [16, 128], F32)
            lgcum = asb("lgcum", [16, 128], F32)
            onesf16 = asb("onesf16", [16, 128], F32)
            dncols = asb("dncols", [128, 96], F32)
            lgrow = asb("lgrow", [1, TC], F32)
            grow = asb("grow", [1, TC], F32)
            brow = asb("brow", [1, TC], F32)
            lgc = dncols[:, 0:16]
            lgLb = dncols[:, 16:32]
            rc = dncols[:, 32:48]
            bcolv = dncols[:, 48:64]
            gLb = dncols[:, 64:80]
            gc = dncols[:, 80:96]

            def aux_sync(e):
                e.dma_start(out=d["row_s"][0:1, :], in_=rows[64:65, :]).then_inc(dS.h, 16)
                e.dma_start(out=d["row_s"][1:2, :], in_=rows[32:33, :]).then_inc(dS.h, 16)
                e.wait_ge(dS.h, dS.n + 32)
                e.dma_start(out=lgch[:],
                            in_=d["row_s"][0:1, :].rearrange("a (c p) -> (a c) p", c=16)
                            ).then_inc(dS.h, 16)
                e.dma_start(out=bcolv,
                            in_=d["row_s"][1:2, :].rearrange("a (c p) -> (a p) c", c=16),
                            allow_slow_non_contiguous=True).then_inc(dS.h, 16)
                e.dma_start(out=brow[:], in_=d["row_s"][1:2, :]).then_inc(dS.h, 16)
            block.sync(aux_sync)
            dS.n += 16 * 5
            aux_d0 = dS.n
            av0, as0, at0 = sV.n, sS.n, sT.n

            def aux_v(e):
                e.wait_ge(dS.h, aux_d0)
                e.memset(onesf16[:], 1.0).then_inc(sV.h, 1)
                e.wait_ge(sV.h, av0 + 1)
                e.tensor_tensor_scan(out=lgcum[:], data0=onesf16[:], data1=lgch[:],
                                     initial=0.0, op0=OP.mult,
                                     op1=OP.add).then_inc(sV.h, 1)
                e.wait_ge(sT.h, at0 + 1)
                e.tensor_copy(out=lgc, in_=PS[0][:, 0:16]).then_inc(sV.h, 1)
                e.wait_ge(sT.h, at0 + 2)
                e.tensor_copy(out=lgLb, in_=PS[1][:, 0:16]).then_inc(sV.h, 1)
                e.wait_ge(sV.h, av0 + 4)
                e.tensor_sub(out=rc, in0=lgLb, in1=lgc).then_inc(sV.h, 1)
            block.vector(aux_v)

            def aux_t(e):
                e.wait_ge(sV.h, av0 + 2)
                e.transpose(PS[0][:, 0:16], lgcum[:],
                            identf[0:16, 0:16]).then_inc(sT.h, 1)
                e.wait_ge(sV.h, av0 + 3)
                e.matmul(PS[1][:, 0:16], sel127, lgc, start=True,
                         stop=True).then_inc(sT.h, 1)
            block.tensor(aux_t)

            def aux_s(e):
                e.wait_ge(sV.h, av0 + 5)
                e.activation(out=rc, in_=rc, func=AF.Exp).then_inc(sS.h, 1)
                e.activation(out=gLb, in_=lgLb, func=AF.Exp)
                e.activation(out=gc, in_=lgc, func=AF.Exp).then_inc(sS.h, 1)
            block.scalar(aux_s)
            sV.n += 5
            sT.n += 2
            sS.n += 2

            def aux2_sync(e):
                e.wait_ge(sV.h, av0 + 2)
                e.dma_start(out=d["row_s"][2:3, :].rearrange("a (c p) -> (a c) p", c=16),
                            in_=lgcum[:]).then_inc(dS.h, 16)
                e.wait_ge(dS.h, dS.n + 16)
                e.dma_start(out=lgrow[:], in_=d["row_s"][2:3, :]).then_inc(dS.h, 16)
            block.sync(aux2_sync)
            dS.n += 32
            aux_d1 = dS.n

            def aux2_s(e):
                e.wait_ge(dS.h, aux_d1)
                e.activation(out=grow[:], in_=lgrow[:], func=AF.Exp).then_inc(sS.h, 1)
            block.scalar(aux2_s)
            sS.n += 1
            barrier()
            tap_now("aux", [(dncols[:], [128, 96], F32), (lgrow[:], [1, TC], F32)])

            chx = asb("chx", [128, NCH, 256], BF16)
            wpt = asb("wpt", [128, NCH, 128], BF16)
            kt = asb("ktb", [128, NCH, 128], BF16)
            gtin = asb("gtin", [128, NCH, 128], BF16)
            wpr = asb("wpr", [128, NCH, 128], BF16)
            qg = asb("qg", [128, TC], BF16)
            pcur = asb("pcur", [128, NCH, 128], BF16)
            tcur = asb("tcur", [128, NCH, 128], BF16)
            ogf = asb("ogf", [128, TC], BF16)
            wkd = asb("wkd", [128, 2, 128], F32)
            wkm = asb("wkm", [128, 2, 128], F32)
            wkmi = asb("wkmi", [128, 2, 128], F32)
            wkgT = asb("wkgT", [128, 2, 128], F32)
            wkA = asb("wkA", [128, 2, 128], BF16)
            gmF = asb("gmF", [128, 2, 128], F32)
            gmTm = asb("gmTm", [128, 2, 128], F32)
            wk1 = asb("wk1", [128, 2, 128], F32)
            wk2 = asb("wk2", [128, 2, 128], F32)
            Zf = asb("Zf", [128, 128], F32)
            Zb = asb("Zbb", [128, 128], BF16)
            S0b = asb("S0b", [128, 128], BF16)
            Baccb = asb("Baccb", [128, 128], BF16)
            BaccTb = asb("BaccTb", [128, 128], BF16)
            Baccf = asb("Baccf", [128, 128], F32)
            wkv = asb("wkv", [128, 128], BF16)
            wkv2 = asb("wkv2", [128, 128], BF16)
            wkv3 = asb("wkv3", [128, 128], BF16)
            wkf = asb("wkf", [128, 256], F32)

            pt0, pv0, ps0 = sT.n, sV.n, sS.n
            PNV = 4

            def prep_t(e):
                for j in range(NCH):
                    cs = slice(j * L, (j + 1) * L)
                    w = j % 2
                    if j >= 2:
                        e.wait_ge(sV.h, pv0 + PNV * (j - 1))
                    e.matmul(PS[0 + w][:, 0:128], kf[:, cs], kf[:, cs],
                             start=True, stop=True)
                    e.matmul(PS[2 + w][:, 0:128], ones_k1f[:], lgrow[0:1, cs],
                             start=True, stop=True)
                    e.matmul(PS[2 + w][:, 128:256], ones_k1f[:], brow[0:1, cs],
                             start=True, stop=True)
                    e.transpose(PB(4 + w, 0), kf[:, cs], identb[:])
                    e.transpose(PB(4 + w, 1), vf[:, cs],
                                identb[:]).then_inc(sT.h, 1)
            block.tensor(prep_t)

            def prep_v1(e):
                for j in range(NCH):
                    w = j % 2
                    e.wait_ge(sT.h, pt0 + j + 1)
                    if j >= 2:
                        e.wait_ge(sS.h, ps0 + (j - 1))
                    e.tensor_copy(out=wkA[:, w, :], in_=PS[0 + w][:, 0:128])
                    e.tensor_scalar(out=wkd[:, w, :], in0=PS[2 + w][:, 0:128],
                                    scalar1=lgc[:, j:j + 1], scalar2=None,
                                    op0=OP.subtract).then_inc(sV.h, 1)
                    e.wait_ge(sV.h, pv0 + PNV * j + 1)
                    e.tensor_scalar(out=wkm[:, w, :], in0=wkd[:, w, :], scalar1=0.0,
                                    scalar2=None, op0=OP.max)
                    e.tensor_scalar(out=wkmi[:, w, :], in0=wkd[:, w, :], scalar1=0.0,
                                    scalar2=None, op0=OP.min).then_inc(sV.h, 1)
                    e.wait_ge(sS.h, ps0 + j + 1)
                    e.tensor_mul(out=gtin[:, j, :], in0=wkgT[:, w, :], in1=mask_incT)
                    e.tensor_mul(out=gmTm[:, w, :], in0=wkgT[:, w, :], in1=mask_snT)
                    e.scalar_tensor_tensor(out=wk1[:, w, :], in0=wkA[:, w, :],
                                           scalar=bcolv[:, j:j + 1], in1=gmF[:, w, :],
                                           op0=OP.mult, op1=OP.mult)
                    e.tensor_mul(out=wk2[:, w, :], in0=wkA[:, w, :],
                                 in1=PS[2 + w][:, 128:256])
                    e.tensor_scalar(out=chx[:, j, 0:128], in0=PB(4 + w, 1),
                                    scalar1=bcolv[:, j:j + 1], scalar2=None,
                                    op0=OP.mult)
                    e.tensor_scalar(out=chx[:, j, 128:256], in0=PB(4 + w, 0),
                                    scalar1=bcolv[:, j:j + 1], scalar2=gc[:, j:j + 1],
                                    op0=OP.mult, op1=OP.mult)
                    e.tensor_copy(out=kt[:, j, :],
                                  in_=PB(4 + w, 0)).then_inc(sV.h, 1)
                    e.wait_ge(sV.h, pv0 + PNV * j + 3)
                    e.tensor_mul(out=pcur[:, j, :], in0=wk1[:, w, :], in1=mask_sn)
                    e.tensor_mul(out=tcur[:, j, :], in0=wk2[:, w, :],
                                 in1=gmTm[:, w, :]).then_inc(sV.h, 1)
            block.vector(prep_v1)

            def prep_s1(e):
                for j in range(NCH):
                    w = j % 2
                    e.wait_ge(sV.h, pv0 + PNV * j + 2)
                    e.activation(out=gmF[:, w, :], in_=wkm[:, w, :], func=AF.Exp,
                                 scale=-1.0)
                    e.activation(out=wkgT[:, w, :], in_=wkmi[:, w, :],
                                 func=AF.Exp).then_inc(sS.h, 1)
            block.scalar(prep_s1)

            sT.n += NCH
            sV.n += PNV * NCH
            sS.n += NCH
            barrier()

            NLVL = 7
            st0, sv0 = sT.n, sV.n

            def solve_t(e):
                for lvl in range(NLVL):
                    for j in range(NCH):
                        w = j % 2
                        if lvl > 0 and j < 2:
                            e.wait_ge(sV.h, sv0 + lvl * NCH)
                        if j >= 2:
                            e.wait_ge(sV.h, sv0 + lvl * NCH + (j - 1))
                        e.matmul(PS[0 + w][:, 0:256], identb[:], chx[:, j, :],
                                 start=True, stop=False)
                        if lvl < NLVL - 1:
                            e.matmul(PS[0 + w][:, 0:256], tcur[:, j, :], chx[:, j, :],
                                     start=False, stop=True)
                            e.matmul(PS[2 + w][:, 0:128], tcur[:, j, :], pcur[:, j, :],
                                     start=True, stop=True)
                            e.matmul(PS[2 + w][:, 128:256], pcur[:, j, :],
                                     tcur[:, j, :],
                                     start=True, stop=True).then_inc(sT.h, 1)
                        else:
                            e.matmul(PS[0 + w][:, 0:256], tcur[:, j, :], chx[:, j, :],
                                     start=False, stop=True).then_inc(sT.h, 1)
            block.tensor(solve_t)

            def solve_v(e):
                for lvl in range(NLVL):
                    for j in range(NCH):
                        w = j % 2
                        e.wait_ge(sT.h, st0 + lvl * NCH + j + 1)
                        ii = e.tensor_copy(out=chx[:, j, :], in_=PS[0 + w][:, 0:256])
                        if lvl < NLVL - 1:
                            e.tensor_copy(out=pcur[:, j, :], in_=PS[2 + w][:, 0:128])
                            ii = e.tensor_copy(out=tcur[:, j, :],
                                               in_=PS[2 + w][:, 128:256])
                        ii.then_inc(sV.h, 1)
            block.vector(solve_v)
            sT.n += NLVL * NCH
            sV.n += NLVL * NCH
            barrier()

            qt0, qv0 = sT.n, sV.n

            def post_t(e):
                for j in range(NCH):
                    w = j % 2
                    if j >= 2:
                        e.wait_ge(sV.h, qv0 + (j - 1))
                    e.transpose(PB(0 + w, 0), chx[:, j, 128:256],
                                identb[:]).then_inc(sT.h, 1)
                for it in range(NTILE):
                    ts = slice(it * TT, (it + 1) * TT)
                    if it >= 1:
                        e.wait_ge(sV.h, qv0 + NCH + it)
                    e.matmul(PS[4][:, 0:TT], ones_k1f[:], grow[0:1, ts],
                             start=True, stop=True).then_inc(sT.h, 1)
            block.tensor(post_t)

            def post_v(e):
                for j in range(NCH):
                    w = j % 2
                    e.wait_ge(sT.h, qt0 + j + 1)
                    e.tensor_copy(out=wpt[:, j, :], in_=PB(0 + w, 0))
                    e.tensor_scalar(out=wpr[:, j, :], in0=chx[:, j, 128:256],
                                    scalar1=rc[:, j:j + 1], scalar2=None,
                                    op0=OP.mult).then_inc(sV.h, 1)
                for it in range(NTILE):
                    ts = slice(it * TT, (it + 1) * TT)
                    e.wait_ge(sT.h, qt0 + NCH + it + 1)
                    e.tensor_mul(out=qg[:, ts], in0=qf[:, ts],
                                 in1=PS[4][:, 0:TT]).then_inc(sV.h, 1)
            block.vector(post_v)
            sT.n += NCH + NTILE
            sV.n += NCH + NTILE
            barrier()
            tap_now("solve", [
                (chx[:].rearrange("p c x -> p (c x)"), [128, NCH * 256], BF16),
                (kt[:].rearrange("p c x -> p (c x)"), [128, NCH * 128], BF16),
                (wpt[:].rearrange("p c x -> p (c x)"), [128, NCH * 128], BF16),
                (gtin[:].rearrange("p c x -> p (c x)"), [128, NCH * 128], BF16),
            ])

            def zs_v_init(e):
                e.memset(Zf[:], 0.0)
                e.memset(Zb[:], 0.0)
                e.memset(S0b[:], 0.0)
                e.tensor_copy(out=Baccb[:], in_=identb[:])
                e.tensor_copy(out=BaccTb[:], in_=identb[:])
                e.tensor_copy(out=Baccf[:], in_=identf).then_inc(sV.h, 1)
            block.vector(zs_v_init)
            sV.n += 1
            zt0, zv0 = sT.n, sV.n
            ZT, ZV = 4, 6

            def zs_t(e):
                for j in range(NCH):
                    e.wait_ge(sV.h, zv0 + ZV * j)
                    e.matmul(PS[0][:, 0:128], wpt[:, j, :], Zb[:],
                             start=True, stop=True).then_inc(sT.h, 1)
                    e.wait_ge(sV.h, zv0 + ZV * j + 2)
                    e.matmul(PS[1][:, 0:128], kt[:, j, :], wkv2[:],
                             start=True, stop=True)
                    e.matmul(PS[2][:, 0:128], kt[:, j, :], wpr[:, j, :],
                             start=True, stop=True).then_inc(sT.h, 1)
                    e.wait_ge(sV.h, zv0 + ZV * j + 4)
                    e.transpose(PB(3, 0), wkv[:], identb[:])
                    e.transpose(PB(4, 0), Zb[:], identb[:]).then_inc(sT.h, 1)
                    e.wait_ge(sV.h, zv0 + ZV * j + 5)
                    e.matmul(PS[5][:, 0:128], BaccTb[:], wkv3[:],
                             start=True, stop=True)
                    e.matmul(PS[6][:, 0:128], wkv3[:], BaccTb[:],
                             start=True, stop=True).then_inc(sT.h, 1)
            block.tensor(zs_t)

            def zs_v(e):
                for j in range(NCH):
                    e.wait_ge(sT.h, zt0 + ZT * j + 1)
                    e.tensor_sub(out=wkv2[:], in0=chx[:, j, 0:128],
                                 in1=PS[0][:, 0:128]).then_inc(sV.h, 1)
                    e.wait_ge(sV.h, zv0 + ZV * j + 1)
                    e.tensor_scalar(out=wkv2[:], in0=wkv2[:], scalar1=rc[:, j:j + 1],
                                    scalar2=None, op0=OP.mult).then_inc(sV.h, 1)
                    e.wait_ge(sT.h, zt0 + ZT * j + 2)
                    e.scalar_tensor_tensor(out=Zf[:], in0=Zf[:], scalar=gLb[:, j:j + 1],
                                           in1=PS[1][:, 0:128], op0=OP.mult,
                                           op1=OP.add)
                    e.scalar_tensor_tensor(out=wkv[:], in0=identb[:],
                                           scalar=gLb[:, j:j + 1], in1=PS[2][:, 0:128],
                                           op0=OP.mult,
                                           op1=OP.subtract).then_inc(sV.h, 1)
                    e.wait_ge(sV.h, zv0 + ZV * j + 3)
                    e.tensor_copy(out=Zb[:], in_=Zf[:]).then_inc(sV.h, 1)
                    e.wait_ge(sT.h, zt0 + ZT * j + 3)
                    e.tensor_copy(out=wkv3[:], in_=PB(3, 0))
                    e.tensor_copy(out=S0b[:], in_=PB(4, 0)).then_inc(sV.h, 1)
                    e.wait_ge(sT.h, zt0 + ZT * j + 4)
                    e.tensor_copy(out=Baccb[:], in_=PS[5][:, 0:128])
                    e.tensor_copy(out=Baccf[:], in_=PS[5][:, 0:128])
                    e.tensor_copy(out=BaccTb[:], in_=PS[6][:, 0:128]).then_inc(sV.h, 1)
            block.vector(zs_v)
            sT.n += ZT * NCH
            sV.n += ZV * NCH
            barrier()
            tap_now("agg", [(Zf[:], [128, 128], F32), (Baccf[:], [128, 128], F32)])

            def cc1_sync(e):
                e.dma_start(out=d["cc1_in"][0:128, :], in_=Zf[:]).then_inc(dS.h, 16)
                e.dma_start(out=d["cc1_in"][128:256, :], in_=Baccf[:]).then_inc(dS.h, 16)
            block.sync(cc1_sync)
            dS.n += 32
            cc1_d = dS.n

            def cc1_g(e):
                e.wait_ge(dS.h, cc1_d)
                e.collective_compute(
                    "AllGather", OP.bypass,
                    replica_groups=[list(range(8))],
                    ins=[d["cc1_in"]],
                    outs=[d["cc1_out"].rearrange("r a b -> (r a) b")],
                ).then_inc(cS.h, 1)
            block.gpsimd(cc1_g)
            cS.n += 1

            def fold_v_init(e):
                e.wait_ge(cS.h, cS.n)
                e.memset(Zf[:], 0.0)
                e.memset(Zb[:], 0.0).then_inc(sV.h, 1)
            block.vector(fold_v_init)
            sV.n += 1
            ft0, fv0, fd0 = sT.n, sV.n, dS.n

            def fold_sync(e):
                e.wait_ge(cS.h, cS.n)
                for s in range(8):
                    if s > 0:
                        e.wait_ge(sV.h, fv0 + 4 * s - 1)
                        e.wait_ge(dS.h, fd0 + 32 * s)
                    e.dma_start(out=wkf[:, 0:128],
                                in_=d["cc1_out"][s, 0:128, :]).then_inc(dS.h, 16)
                    e.dma_start(out=wkf[:, 128:256],
                                in_=d["cc1_out"][s, 128:256, :]).then_inc(dS.h, 16)
            block.sync(fold_sync)
            dS.n += 16 * 16

            def fold_v(e):
                for s in range(8):
                    e.wait_ge(dS.h, fd0 + 32 * (s + 1))
                    e.tensor_sub(out=wk1[:, 0, :], in0=wkf[:, 128:256],
                                 in1=identf).then_inc(sV.h, 1)
                    e.wait_ge(sV.h, fv0 + 4 * s + 1)
                    e.scalar_tensor_tensor(out=wkv[:], in0=wk1[:, 0, :],
                                           scalar=maskpred[:, s:s + 1], in1=identf,
                                           op0=OP.mult,
                                           op1=OP.add).then_inc(sV.h, 1)
                    e.wait_ge(sT.h, ft0 + s + 1)
                    e.scalar_tensor_tensor(out=Zf[:], in0=wkf[:, 0:128],
                                           scalar=maskpred[:, s:s + 1],
                                           in1=PS[0][:, 0:128], op0=OP.mult,
                                           op1=OP.add).then_inc(sV.h, 1)
                    e.wait_ge(sV.h, fv0 + 4 * s + 3)
                    e.tensor_copy(out=Zb[:], in_=Zf[:]).then_inc(sV.h, 1)
            block.vector(fold_v)

            def fold_t(e):
                for s in range(8):
                    e.wait_ge(sV.h, fv0 + 4 * s + 2)
                    e.matmul(PS[0][:, 0:128], wkv[:], Zb[:], start=True,
                             stop=True).then_inc(sT.h, 1)
            block.tensor(fold_t)
            sV.n += 4 * 8
            sT.n += 8
            barrier()

            def fold_fix_t(e):
                e.transpose(PB(1, 0), Zb[:], identb[:]).then_inc(sT.h, 1)
            block.tensor(fold_fix_t)
            sT.n += 1
            fxt = sT.n

            def fold_fix_v(e):
                e.wait_ge(sT.h, fxt)
                e.tensor_copy(out=S0b[:], in_=PB(1, 0)).then_inc(sV.h, 1)
            block.vector(fold_fix_v)
            sV.n += 1
            barrier()
            tap_now("zin", [(Zf[:], [128, 128], F32)])

            ot0, ov0 = sT.n, sV.n
            OT, OV = 5, 7

            def os_t(e):
                for j in range(NCH):
                    cs = slice(j * L, (j + 1) * L)
                    e.wait_ge(sV.h, ov0 + OV * j)
                    e.matmul(PS[0][:, 0:128], wpt[:, j, :], Zb[:],
                             start=True, stop=True).then_inc(sT.h, 1)
                    e.wait_ge(sV.h, ov0 + OV * j + 2)
                    e.transpose(PB(1, 0), wkv[:], identb[:]).then_inc(sT.h, 1)
                    e.wait_ge(sV.h, ov0 + OV * j + 3)
                    e.matmul(PS[2][:, 0:128], wkv3[:], qf[:, cs],
                             start=True, stop=True).then_inc(sT.h, 1)
                    e.wait_ge(sV.h, ov0 + OV * j + 4)
                    e.matmul(PS[3][:, 0:128], kt[:, j, :], wkv[:],
                             start=True, stop=False)
                    e.matmul(PS[3][:, 0:128], S0b[:], qg[:, cs],
                             start=False, stop=True)
                    e.matmul(PS[4][:, 0:128], kt[:, j, :], wkv2[:],
                             start=True, stop=True).then_inc(sT.h, 1)
                    e.wait_ge(sV.h, ov0 + OV * j + 6)
                    e.transpose(PB(5, 0), Zb[:], identb[:]).then_inc(sT.h, 1)
            block.tensor(os_t)

            def os_v(e):
                for j in range(NCH):
                    cs = slice(j * L, (j + 1) * L)
                    e.wait_ge(sT.h, ot0 + OT * j + 1)
                    e.tensor_sub(out=wkv[:], in0=chx[:, j, 0:128],
                                 in1=PS[0][:, 0:128]).then_inc(sV.h, 1)
                    e.wait_ge(sV.h, ov0 + OV * j + 1)
                    e.tensor_scalar(out=wkv2[:], in0=wkv[:], scalar1=rc[:, j:j + 1],
                                    scalar2=None, op0=OP.mult).then_inc(sV.h, 1)
                    e.wait_ge(sT.h, ot0 + OT * j + 2)
                    e.tensor_copy(out=wkv3[:], in_=PB(1, 0)).then_inc(sV.h, 1)
                    e.wait_ge(sT.h, ot0 + OT * j + 3)
                    e.tensor_mul(out=wkv[:], in0=PS[2][:, 0:128],
                                 in1=gtin[:, j, :]).then_inc(sV.h, 1)
                    e.wait_ge(sT.h, ot0 + OT * j + 4)
                    e.tensor_mul(out=ogf[:, cs], in0=PS[3][:, 0:128], in1=gatef[:, cs])
                    e.scalar_tensor_tensor(out=Zf[:], in0=Zf[:], scalar=gLb[:, j:j + 1],
                                           in1=PS[4][:, 0:128], op0=OP.mult,
                                           op1=OP.add).then_inc(sV.h, 1)
                    e.wait_ge(sV.h, ov0 + OV * j + 5)
                    e.tensor_copy(out=Zb[:], in_=Zf[:]).then_inc(sV.h, 1)
                    e.wait_ge(sT.h, ot0 + OT * j + 5)
                    e.tensor_copy(out=S0b[:], in_=PB(5, 0)).then_inc(sV.h, 1)
            block.vector(os_v)
            sT.n += OT * NCH
            sV.n += OV * NCH
            barrier()

            wt0, wv0 = sT.n, sV.n

            def wo_t(e):
                for it in range(NTILE):
                    ts = slice(it * TT, (it + 1) * TT)
                    for g in range(G8):
                        bk = 6 + (g % 2)
                        if it > 0 or g >= 2:
                            e.wait_ge(sV.h, wv0 + it * G8 + g - 1)
                        e.matmul(PS[bk][:], wo_sb[:, g * 128:(g + 1) * 128],
                                 ogf[:, ts], start=True, stop=True).then_inc(sT.h, 1)
            block.tensor(wo_t)

            def wo_v(e):
                for it in range(NTILE):
                    ts = slice(it * TT, (it + 1) * TT)
                    for g in range(G8):
                        bk = 6 + (g % 2)
                        e.wait_ge(sT.h, wt0 + it * G8 + g + 1)
                        e.scalar_tensor_tensor(out=resid[:, g, ts], in0=PS[bk][:],
                                               scalar=biases[:, 5 + g:6 + g],
                                               in1=resid[:, g, ts], op0=OP.add,
                                               op1=OP.add).then_inc(sV.h, 1)
            block.vector(wo_v)
            sT.n += NTILE * G8
            sV.n += NTILE * G8
            barrier()
            tap_now("attnout", [(resid[:].rearrange("p g t -> p (g t)"),
                                 [128, G8 * TC], F32)])

    LN127 = float(np.log(127.0))

    def emit_wquant():
        wes = contextlib.ExitStack()
        wsb = lambda n, s, dt: wes.enter_context(nc.sbuf_tensor(n, list(s), dt))
        wwk = wsb("wwk", [128, 2, HID], F32)
        wab = wsb("wab", [128, 2, HID], BF16)
        wq1 = wsb("wq1", [128, HID], F32)
        wqo = wsb("wqo", [128, 2, HID], BF16)
        acc = wsb("acc", [128, 48], F32)
        srow = wsb("srow", [1, 4], F32)
        scol = wsb("scol", [128, 4], F32)
        CH_ALL = [("w1t", "w1q_s", G8, HID), ("w2t", "w2q_s", G32, D)]
        NCHW = G8 + G32
        d0, v0, s0, t0 = dS.n, sV.n, sS.n, sT.n
        d0b = dG2.n

        def wq_dma1(e, par=0):
            i = 0
            for (src, dst, n, wd) in CH_ALL:
                for g in range(n):
                    if i % 2 == par:
                        if i >= 2:
                            e.wait_ge(sS.h, s0 + i - 1)
                            if par == 0:
                                e.wait_ge(dS.h, d0 + 16 * (i // 2))
                        qq = dS if i % 2 == 0 else dG2
                        e.dma_start(out=wwk[:, i % 2, 0:wd],
                                    in_=d[src][g]).then_inc(qq.h, 16)
                    i += 1
        block.sync(lambda e: wq_dma1(e, 0))
        block.gpsimd(lambda e: wq_dma1(e, 1))
        dS.n += 16 * ((NCHW + 1) // 2)
        dG2.n += 16 * (NCHW // 2)

        def wq_s1(e):
            i = 0
            for (src, dst, n, wd) in CH_ALL:
                for g in range(n):
                    qq, qb = (dS, d0) if i % 2 == 0 else (dG2, d0b)
                    e.wait_ge(qq.h, qb + 16 * (i // 2 + 1))
                    if i >= 2:
                        e.wait_ge(sS.h, s0 + i - 1)
                    e.activation(out=wab[:, i % 2, 0:wd], in_=wwk[:, i % 2, 0:wd],
                                 func=AF.Abs,
                                 accum_out=acc[:, i:i + 1]).then_inc(sS.h, 1)
                    i += 1
        block.scalar(wq_s1)
        sS.n += NCHW

        def wq_v1(e):
            e.wait_ge(sS.h, s0 + NCHW)
            e.tensor_reduce(out=acc[:, 40:41], in_=acc[:, 0:G8], axis=AX, op=OP.add)
            e.tensor_reduce(out=acc[:, 41:42], in_=acc[:, G8:NCHW], axis=AX,
                            op=OP.add).then_inc(sV.h, 1)
        block.vector(wq_v1)

        def wq_t1(e):
            e.wait_ge(sV.h, v0 + 1)
            e.matmul(PS[0][0:1, 0:2], ones128f[:, 0:1], acc[:, 40:42], start=True,
                     stop=True).then_inc(sT.h, 1)
        block.tensor(wq_t1)

        def wq_v2(e):
            e.wait_ge(sT.h, t0 + 1)
            e.tensor_scalar(out=srow[0:1, 0:2], in0=PS[0][0:1, 0:2],
                            scalar1=1.0 / 4194304.0, scalar2=1e-5, op0=OP.mult,
                            op1=OP.max).then_inc(sV.h, 1)
            e.wait_ge(sV.h, v0 + 2)
            e.reciprocal(out=srow[0:1, 2:4], in_=srow[0:1, 0:2]).then_inc(sV.h, 1)
        block.vector(wq_v2)

        def wq_t2(e):
            e.wait_ge(sV.h, v0 + 3)
            e.matmul(PS[1][:, 0:4], ones_k1f[:], srow[0:1, 0:4], start=True,
                     stop=True).then_inc(sT.h, 1)
        block.tensor(wq_t2)

        def wq_v3(e):
            e.wait_ge(sT.h, t0 + 2)
            e.tensor_copy(out=scol[:], in_=PS[1][:, 0:4]).then_inc(sV.h, 1)
        block.vector(wq_v3)
        sV.n += 4
        sT.n += 2
        vq0 = sV.n
        dq0 = dS.n
        dq0b = dG2.n
        daq0 = dA.n
        daq0b = dA2.n

        def wq_dma2(e, par=0):
            i = 0
            for (src, dst, n, wd) in CH_ALL:
                for g in range(n):
                    if i % 2 == par:
                        e.wait_ge(sV.h, vq0)
                        if i >= 2:
                            e.wait_ge(sV.h, vq0 + 3 * (i - 1))
                            if par == 0:
                                e.wait_ge(dS.h, dq0 + 16 * (i // 2))
                        qq = dS if i % 2 == 0 else dG2
                        e.dma_start(out=wwk[:, i % 2, 0:wd],
                                    in_=d[src][g]).then_inc(qq.h, 16)
                    i += 1
        block.sync(lambda e: wq_dma2(e, 0))
        block.gpsimd(lambda e: wq_dma2(e, 1))
        dS.n += 16 * ((NCHW + 1) // 2)
        dG2.n += 16 * (NCHW // 2)

        def wq_v4(e):
            i = 0
            for (src, dst, n, wd) in CH_ALL:
                si = 0 if src == "w1t" else 1
                for g in range(n):
                    qq, qb = (dS, dq0) if i % 2 == 0 else (dG2, dq0b)
                    e.wait_ge(qq.h, qb + 16 * (i // 2 + 1))
                    if i >= 2:
                        qa, qab = (dA, daq0) if i % 2 == 0 else (dA2, daq0b)
                        e.wait_ge(qa.h, qab + 16 * ((i - 2) // 2 + 1))
                    if i >= 1:
                        e.wait_ge(sV.h, vq0 + 3 * i)
                    e.tensor_scalar(out=wq1[:, 0:wd], in0=wwk[:, i % 2, 0:wd],
                                    scalar1=scol[:, 2 + si:3 + si], scalar2=MAGIC,
                                    op0=OP.mult, op1=OP.add).then_inc(sV.h, 1)
                    e.wait_ge(sV.h, vq0 + 3 * i + 1)
                    e.tensor_scalar(out=wq1[:, 0:wd], in0=wq1[:, 0:wd], scalar1=MAGIC,
                                    scalar2=1.0, op0=OP.subtract,
                                    op1=OP.min).then_inc(sV.h, 1)
                    e.wait_ge(sV.h, vq0 + 3 * i + 2)
                    e.tensor_scalar(out=wqo[:, i % 2, 0:wd], in0=wq1[:, 0:wd],
                                    scalar1=-1.0, scalar2=scol[:, si:si + 1],
                                    op0=OP.max, op1=OP.mult).then_inc(sV.h, 1)
                    i += 1
        block.vector(wq_v4)
        sV.n += 3 * NCHW

        def wq_dma3(e):
            i = 0
            for (src, dst, n, wd) in CH_ALL:
                for g in range(n):
                    e.wait_ge(sV.h, vq0 + 3 * i + 3)
                    qa = dA if i % 2 == 0 else dA2
                    e.dma_start(out=d[dst][g], in_=wqo[:, i % 2, 0:wd]).then_inc(qa.h, 16)
                    i += 1
        block.scalar(wq_dma3)
        dA.n += 16 * ((NCHW + 1) // 2)
        dA2.n += 16 * (NCHW // 2)
        barrier()
        wes.close()

    def ffn_call(tag, tapname=None):
        fes = contextlib.ExitStack()
        fsb = lambda n, s, dt: fes.enter_context(nc.sbuf_tensor(tag + n, list(s), dt))
        lnxt_b = fsb("lnxt", [128, G8, TT], BF16)
        xqfm = fsb("xqfm", [128, G8, TT], BF16)
        qf32 = fsb("qf32", [128, 1024], F32)
        w1b = fsb("w1b", [128, 2, G8, 512], BF16)
        htm = fsb("htm", [128, 4, HID], BF16)
        hqtm = fsb("hqtm", [128, HID], BF16)
        hqfm = fsb("hqfm", [128, G32, 512], BF16)
        w2b = fsb("w2b", [128, 2, D], BF16)
        qcols = fsb("qcols", [128, 8], F32)
        lnxT4 = lambda sub: hqtm[:][:, sub * 1024:(sub + 1) * 1024]
        W2S = [dW3, dW4, dW5, dW6]
        tb = {}
        itb = {}

        def emit_AB(j):
            A_T, A_V, A_S = sT.n, sV.n, sS.n
            entry = None
            if j > 0:
                entry = {"v": [(sT.h, tb[j - 1]["A_T"] + 1),
                               (sS.h, tb[j - 1]["A_S"] + 1)],
                         "t": [(sT.h, A_T)]}
            ln_fm(lambda g, tss: resid[:, g, tss],
                  lambda g, tss: lnxt_b[:, g, tss.start - j * TT:tss.stop - j * TT],
                  tiles=[j], entry=entry)
            for sub in range(4):
                bt0 = sT.n

                def b_t(e, sub=sub, A_V=A_V):
                    e.wait_ge(sV.h, A_V + 6 + (sub if sub else 0))
                    for g in range(G8 - 1):
                        e.transpose(PB(6, g), lnxt_b[:, g, sub * 128:(sub + 1) * 128],
                                    identb[:])
                    e.transpose(PB(6, G8 - 1),
                                lnxt_b[:, G8 - 1, sub * 128:(sub + 1) * 128],
                                identb[:]).then_inc(sT.h, 1)
                def b_v1(e, sub=sub, bt0=bt0):
                    e.wait_ge(sT.h, bt0 + 1)
                    for g in range(G8 - 1):
                        e.tensor_copy(out=lnxT4(sub)[:, g * 128:(g + 1) * 128],
                                      in_=PB(6, g))
                    e.tensor_copy(out=lnxT4(sub)[:, (G8 - 1) * 128:G8 * 128],
                                  in_=PB(6, G8 - 1)).then_inc(sV.h, 1)
                block.tensor(b_t)
                block.vector(b_v1)
                sT.n += 1
            tb[j] = dict(A_T=A_T, A_V=A_V, A_S=A_S, B1_end=A_V + 10)
            sV.n += 4

        def emit_B2(j):
            A_V, A_S = tb[j]["A_V"], tb[j]["A_S"]
            for sub in range(4):
                bv0, bs0 = sV.n, sS.n

                def b_v2(e, sub=sub, bv0=bv0, bs0=bs0, A_V=A_V):
                    e.wait_ge(sV.h, A_V + 7 + sub)
                    if sub:
                        e.wait_ge(sV.h, bv0)
                    e.tensor_reduce(out=qcols[:, 0:1], in_=lnxT4(sub), axis=AX,
                                    op=OP.max,
                                    apply_absolute_value=True).then_inc(sV.h, 1)
                    e.wait_ge(sV.h, bv0 + 1)
                    e.tensor_scalar(out=qcols[:, 1:2], in0=qcols[:, 0:1], scalar1=1e-5,
                                    scalar2=None, op0=OP.max).then_inc(sV.h, 1)
                    e.wait_ge(sS.h, bs0 + 2)
                    e.tensor_scalar(out=qcols[:, 3:4], in0=qcols[:, 1:2],
                                    scalar1=1.0 / 127.0, scalar2=None,
                                    op0=OP.mult).then_inc(sV.h, 1)
                    e.wait_ge(sS.h, bs0 + 3)
                    e.wait_ge(sV.h, bv0 + 3)
                    e.tensor_scalar(out=lnxT4(sub), in0=qf32[:, 0:D], scalar1=MAGIC,
                                    scalar2=qcols[:, 3:4], op0=OP.subtract,
                                    op1=OP.mult).then_inc(sV.h, 1)
                def b_s(e, sub=sub, bv0=bv0, bs0=bs0):
                    e.wait_ge(sV.h, bv0 + 2)
                    e.activation(out=qcols[:, 4:5], in_=qcols[:, 1:2],
                                 func=AF.Ln).then_inc(sS.h, 1)
                    e.wait_ge(sS.h, bs0 + 1)
                    e.activation(out=qcols[:, 2:3], in_=qcols[:, 4:5], func=AF.Exp,
                                 scale=-1.0, bias=LN127).then_inc(sS.h, 1)
                    e.wait_ge(sS.h, bs0 + 2)
                    e.activation(out=qf32[:, 0:D], in_=lnxT4(sub), func=AF.Copy,
                                 scale=qcols[:, 2:3], bias=MAGIC).then_inc(sS.h, 1)
                block.vector(b_v2)
                block.scalar(b_s)
                sV.n += 4
                sS.n += 3
            tb[j]["B2_end"] = sV.n

        def emit_B3(j, FV):
            XQ = []
            VB = sV.n
            for sub in range(4):
                bt0 = sT.n

                def b_t2(e, sub=sub, FV=FV, VB=VB):
                    e.wait_ge(sV.h, FV if sub == 0 else VB + sub)
                    for g in range(G8 - 1):
                        e.transpose(PB(5, g), lnxT4(sub)[:, g * 128:(g + 1) * 128],
                                    identb[:])
                    e.transpose(PB(5, G8 - 1),
                                lnxT4(sub)[:, (G8 - 1) * 128:G8 * 128],
                                identb[:]).then_inc(sT.h, 1)
                def b_v3(e, sub=sub, bt0=bt0):
                    e.wait_ge(sT.h, bt0 + 1)
                    for g in range(G8 - 1):
                        e.tensor_copy(out=xqfm[:, g, sub * 128:(sub + 1) * 128],
                                      in_=PB(5, g))
                    e.tensor_copy(out=xqfm[:, G8 - 1, sub * 128:(sub + 1) * 128],
                                  in_=PB(5, G8 - 1)).then_inc(sV.h, 1)
                block.tensor(b_t2)
                block.vector(b_v3)
                sT.n += 1
                XQ.append(VB + 1 + sub)
            sV.n += 4
            tb[j]["XQ"] = XQ

        emit_AB(0)
        emit_B2(0)
        emit_B3(0, sV.n)

        for it in range(NTILE):
            ts = slice(it * TT, (it + 1) * TT)
            IT_T, IT_V, IT_S = sT.n, sV.n, sS.n
            last = (it == NTILE - 1)
            mm1_T0 = IT_T
            gelu_S0 = IT_S
            mm2_T0 = IT_T + (48 if last else 53)
            dlast_V = IT_V + 44
            itb[it] = dict(IT_T=IT_T, IT_V=IT_V, mm1_T0=mm1_T0,
                           mm2_T0=mm2_T0)

            w1d0 = [dW1.n, dW2.n]
            w2d0 = [c.n for c in W2S]
            pv = itb.get(it - 1)

            def w_dma(e, mm1_T0=mm1_T0, mm2_T0=mm2_T0, pv=pv):
                for q in range(8):
                    if q >= 2:
                        e.wait_ge(sT.h, mm1_T0 + 4 * q - 4)
                    elif pv is not None:
                        e.wait_ge(sT.h, pv["mm1_T0"] + 4 * (q + 6) + 4)
                    qq = dW1 if q % 2 == 0 else dW2
                    e.dma_start(
                        out=w1b[:, q % 2, :, :],
                        in_=d["w1q_s"].rearrange("g p h -> p g h")[
                            :, :, q * 512:(q + 1) * 512]).then_inc(qq.h, 16)
                for hc in range(G32):
                    if hc >= 2:
                        e.wait_ge(sT.h, mm2_T0 + hc - 1)
                    elif pv is not None:
                        e.wait_ge(sT.h, pv["mm2_T0"] + 31 + hc)
                    e.dma_start(out=w2b[:, hc % 2, :],
                                in_=d["w2q_s"][hc]).then_inc(W2S[hc % 4].h, 16)
            block.gpsimd(w_dma)
            dW1.n += 16 * 4
            dW2.n += 16 * 4
            for c in W2S:
                c.n += 16 * 8

            ct0, cs0 = sT.n, sS.n
            assert ct0 == mm1_T0 and cs0 == gelu_S0
            XQ = tb[it]["XQ"]

            def c_t(e, ct0=ct0, cs0=cs0, w1d0=w1d0, XQ=XQ):
                for q in range(8):
                    qq, qb = (dW1, w1d0[0]) if q % 2 == 0 else (dW2, w1d0[1])
                    e.wait_ge(qq.h, qb + 16 * (q // 2 + 1))
                    for sub in range(4):
                        i = q * 4 + sub
                        bk = i % 2
                        if q == 0:
                            e.wait_ge(sV.h, XQ[sub])
                        if i >= 2:
                            e.wait_ge(sS.h, cs0 + i - 1)
                        for kc in range(G8 - 1):
                            e.matmul(PS[bk][:],
                                     xqfm[:, kc, sub * 128:(sub + 1) * 128],
                                     w1b[:, q % 2, kc, :],
                                     start=(kc == 0), stop=False)
                        e.matmul(PS[bk][:],
                                 xqfm[:, G8 - 1, sub * 128:(sub + 1) * 128],
                                 w1b[:, q % 2, G8 - 1, :],
                                 start=False, stop=True).then_inc(sT.h, 1)
            def c_s(e, ct0=ct0):
                i = 0
                for q in range(8):
                    for sub in range(4):
                        bk = i % 2
                        e.wait_ge(sT.h, ct0 + i + 1)
                        e.activation(out=htm[:, sub, q * 512:(q + 1) * 512],
                                     in_=PS[bk][:],
                                     func=AF.Gelu).then_inc(sS.h, 1)
                        i += 1
            block.tensor(c_t)
            block.scalar(c_s)
            sT.n += 32
            sS.n += 32

            D_T0 = sT.n
            assert D_T0 == IT_T + 32
            for sub in range(4):
                dt0, dv0, ds0 = sT.n, sV.n, sS.n

                def d_v(e, sub=sub, dv0=dv0, ds0=ds0, dt0=dt0, gelu_S0=gelu_S0):
                    e.wait_ge(sS.h, gelu_S0 + 29 if sub == 0
                              else gelu_S0 + 32 + 6 * sub)
                    if sub:
                        e.wait_ge(sT.h, dt0)
                    e.tensor_reduce(out=qcols[:, 4:5], in_=htm[:, sub, :], axis=AX,
                                    op=OP.max,
                                    apply_absolute_value=True).then_inc(sV.h, 1)
                    e.wait_ge(sV.h, dv0 + 1)
                    e.tensor_scalar(out=qcols[:, 5:6], in0=qcols[:, 4:5], scalar1=1e-5,
                                    scalar2=None, op0=OP.max).then_inc(sV.h, 1)
                    e.wait_ge(sS.h, ds0 + 2)
                    e.tensor_scalar(out=qcols[:, 7:8], in0=qcols[:, 5:6],
                                    scalar1=1.0 / 127.0, scalar2=None,
                                    op0=OP.mult).then_inc(sV.h, 1)
                    for hh in range(4):
                        hs = slice(hh * 1024, (hh + 1) * 1024)
                        e.wait_ge(sS.h, ds0 + 3 + hh)
                        e.wait_ge(sV.h, dv0 + 3 + hh)
                        e.tensor_scalar(out=hqtm[:, hs], in0=qf32[:], scalar1=MAGIC,
                                        scalar2=qcols[:, 7:8], op0=OP.subtract,
                                        op1=OP.mult).then_inc(sV.h, 1)
                    for blkz in range(4):
                        e.wait_ge(sT.h, dt0 + blkz + 1)
                        bk = 4 + (blkz % 2)
                        kk = None
                        for k in range(8):
                            hc = blkz * 8 + k
                            kk = e.tensor_copy(
                                out=hqfm[:, hc, sub * 128:(sub + 1) * 128],
                                in_=PB(bk, k))
                        kk.then_inc(sV.h, 1)
                def d_s(e, sub=sub, dv0=dv0, ds0=ds0):
                    e.wait_ge(sV.h, dv0 + 2)
                    e.activation(out=qcols[:, 6:7], in_=qcols[:, 5:6],
                                 func=AF.Ln).then_inc(sS.h, 1)
                    e.wait_ge(sS.h, ds0 + 1)
                    e.activation(out=qcols[:, 6:7], in_=qcols[:, 6:7], func=AF.Exp,
                                 scale=-1.0, bias=LN127).then_inc(sS.h, 1)
                    for hh in range(4):
                        hs = slice(hh * 1024, (hh + 1) * 1024)
                        e.wait_ge(sS.h, ds0 + 2 + hh)
                        if hh:
                            e.wait_ge(sV.h, dv0 + 3 + hh)
                        e.activation(out=qf32[:], in_=htm[:, sub, hs], func=AF.Copy,
                                     scale=qcols[:, 6:7],
                                     bias=MAGIC).then_inc(sS.h, 1)
                def d_t(e, dv0=dv0, dt0=dt0):
                    for blkz in range(4):
                        bk = 4 + (blkz % 2)
                        e.wait_ge(sV.h, dv0 + 4 + blkz)
                        if blkz >= 2:
                            e.wait_ge(sV.h, dv0 + 6 + blkz)
                        for k in range(8 - 1):
                            hc = blkz * 8 + k
                            e.transpose(PB(bk, k), hqtm[:, hc * 128:(hc + 1) * 128],
                                        identb[:])
                        hc = blkz * 8 + 7
                        e.transpose(PB(bk, 7), hqtm[:, hc * 128:(hc + 1) * 128],
                                    identb[:]).then_inc(sT.h, 1)
                block.vector(d_v)
                block.scalar(d_s)
                block.tensor(d_t)
                sV.n += 11
                sS.n += 6
                sT.n += 4
            assert sV.n == dlast_V

            if not last:
                emit_AB(it + 1)
                emit_B2(it + 1)

            ft0, fv0 = sT.n, sV.n
            assert ft0 == mm2_T0
            fwait = dlast_V if last else tb[it + 1]["B1_end"]

            def f_t(e, ft0=ft0, w2d0=w2d0, fwait=fwait):
                e.wait_ge(sV.h, fwait)
                for hc in range(G32):
                    e.wait_ge(W2S[hc % 4].h, w2d0[hc % 4] + 16 * (hc // 4 + 1))
                    sl = hc % 2
                    for g in range(G8 - 1):
                        e.matmul(PS[g][:], w2b[:, sl, g * 128:(g + 1) * 128],
                                 hqfm[:, hc, :], start=(hc == 0),
                                 stop=(hc == G32 - 1))
                    e.matmul(PS[G8 - 1][:], w2b[:, sl, (G8 - 1) * 128:G8 * 128],
                             hqfm[:, hc, :], start=(hc == 0),
                             stop=(hc == G32 - 1)).then_inc(sT.h, 1)
            def f_v(e, ts=ts, ft0=ft0):
                e.wait_ge(sT.h, ft0 + G32)
                for g in range(G8 - 1):
                    e.tensor_add(out=resid[:, g, ts], in0=PS[g][:],
                                 in1=resid[:, g, ts])
                e.tensor_add(out=resid[:, G8 - 1, ts], in0=PS[G8 - 1][:],
                             in1=resid[:, G8 - 1, ts]).then_inc(sV.h, 1)
            block.tensor(f_t)
            block.vector(f_v)
            sT.n += G32
            sV.n += 1
            if not last:
                emit_B3(it + 1, sV.n)
            assert sT.n == IT_T + (80 if last else 89)
            assert sV.n == IT_V + (45 if last else 75)

        barrier()
        fes.close()
        if tapname is not None:
            tap_now(tapname, [(resid[:].rearrange("p g t -> p (g t)"),
                               [128, G8 * TC], F32)])

    if upto >= 4:
        emit_wquant()
        if b.taps and "wq" in b.taps:
            t1 = b.dram("tap_wq_0", [128, HID], BF16, kind="ExternalOutput")
            t2 = b.dram("tap_wq_1", [128, D], BF16, kind="ExternalOutput")
            b.tap_tensors["wq_0"] = t1
            b.tap_tensors["wq_1"] = t2

            def tpw(e):
                e.dma_start(out=t1, in_=d["w1q_s"][0]).then_inc(dS.h, 16)
                e.dma_start(out=t2, in_=d["w2q_s"][0]).then_inc(dS.h, 16)
            block.sync(tpw)
            dS.n += 32
            barrier()
        ffn_call("f1", tapname="ffn1out")

    if upto >= 5:
      with contextlib.ExitStack() as mes:
        msb = lambda n, s, dt: mes.enter_context(nc.sbuf_tensor(n, list(s), dt))
        haloxin = msb("haloxin", [128, G16, 4], BF16)

        def mh_init(e):
            e.memset(haloxin[:], 0.0).then_inc(sV.h, 1)
        block.vector(mh_init)
        sV.n += 1
        barrier()

        with contextlib.ExitStack() as m1es:
            lnx3 = m1es.enter_context(nc.sbuf_tensor("lnx3", [128, G8, TC], BF16))
            win_sb = m1es.enter_context(nc.sbuf_tensor("win_sb", [128, G8, HID], BF16))
            xzev = m1es.enter_context(nc.sbuf_tensor("xzev", [128, 2, TT], BF16))
            chal = m1es.enter_context(nc.sbuf_tensor("chal", [128, 48], BF16))

            ln_fm(lambda g, ts: resid[:, g, ts], lambda g, ts: lnx3[:, g, ts])

            def m1_wdma(e):
                for g in range(G8):
                    qq = dG if g % 2 == 0 else dG2
                    e.dma_start(out=win_sb[:, g, :], in_=d["win_w"][g]).then_inc(qq.h, 16)
            block.gpsimd(m1_wdma)
            dG.n += 16 * 4
            dG2.n += 16 * 4
            barrier()

            def m1_lndma(e):
                for g in range(G8):
                    qa = dA if g % 2 == 0 else dA2
                    e.dma_start(out=d["lnx3_s"][g], in_=lnx3[:, g, :]).then_inc(qa.h, 16)
            block.scalar(m1_lndma)
            dA.n += 16 * 4
            dA2.n += 16 * 4

            m1t0, m1v0, m1s0, m1d0 = sT.n, sV.n, sS.n, dS.n
            m1d0b = dA2.n

            def _cntx(i):
                it_, mc_ = divmod(i, 32)
                return it_ * 16 + min(mc_, 16)

            def _cntz(i):
                it_, mc_ = divmod(i, 32)
                return it_ * 16 + max(mc_ - 16, 0)

            def m1_t(e):
                i = 0
                for it in range(NTILE):
                    ts = slice(it * TT, (it + 1) * TT)
                    for mc in range(32):
                        bk = i % 2
                        if i >= 2:
                            if (i - 2) % 32 < 16:
                                e.wait_ge(sV.h, m1v0 + _cntx(i - 1))
                            else:
                                e.wait_ge(sS.h, m1s0 + _cntz(i - 1))
                        for kc in range(G8 - 1):
                            e.matmul(PS[bk][:], win_sb[:, kc, mc * 128:(mc + 1) * 128],
                                     lnx3[:, kc, ts], start=(kc == 0), stop=False)
                        e.matmul(PS[bk][:], win_sb[:, G8 - 1, mc * 128:(mc + 1) * 128],
                                 lnx3[:, G8 - 1, ts], start=False,
                                 stop=True).then_inc(sT.h, 1)
                        i += 1
                for mc in range(16):
                    for kc in range(G8 - 1):
                        e.matmul(PS[2][:, mc * 3:(mc + 1) * 3],
                                 win_sb[:, kc, mc * 128:(mc + 1) * 128],
                                 lnx3[:, kc, TC - 3:TC], start=(kc == 0), stop=False)
                    e.matmul(PS[2][:, mc * 3:(mc + 1) * 3],
                             win_sb[:, G8 - 1, mc * 128:(mc + 1) * 128],
                             lnx3[:, G8 - 1, TC - 3:TC], start=False, stop=True)
                e.matmul(PS[3][0:1, 0:1], identb[:, 0:1], identb[:, 0:1],
                         start=True, stop=True).then_inc(sT.h, 1)
            block.tensor(m1_t)

            def m1_v(e):
                i = 0
                for it in range(NTILE):
                    for mc in range(32):
                        bk = i % 2
                        if mc < 16:
                            e.wait_ge(sT.h, m1t0 + i + 1)
                            if i >= 2:
                                qq, qb = (dS, m1d0) if bk == 0 else (dA2, m1d0b)
                                e.wait_ge(qq.h, qb + 16 * ((i - 2) // 2 + 1))
                            e.tensor_copy(out=xzev[:, bk, :],
                                          in_=PS[bk][:]).then_inc(sV.h, 1)
                        i += 1
            block.vector(m1_v)

            def m1_sil(e):
                i = 0
                for it in range(NTILE):
                    for mc in range(32):
                        bk = i % 2
                        if mc >= 16:
                            e.wait_ge(sT.h, m1t0 + i + 1)
                            if i >= 2:
                                qq, qb = (dS, m1d0) if bk == 0 else (dA2, m1d0b)
                                e.wait_ge(qq.h, qb + 16 * ((i - 2) // 2 + 1))
                            e.activation(out=xzev[:, bk, :], in_=PS[bk][:],
                                         func=AF.Silu).then_inc(sS.h, 1)
                        i += 1
            block.scalar(m1_sil)

            def m1_odma(e):
                i = 0
                for it in range(NTILE):
                    ts = slice(it * TT, (it + 1) * TT)
                    for mc in range(32):
                        bk = i % 2
                        qq, qb = (dS, m1d0) if bk == 0 else (dA2, m1d0b)
                        if i >= 2:
                            e.wait_ge(qq.h, qb + 16 * ((i - 2) // 2 + 1))
                        if mc < 16:
                            e.wait_ge(sV.h, m1v0 + _cntx(i) + 1)
                            e.dma_start(out=d["xin_s"][mc, :, ts],
                                        in_=xzev[:, bk, :]).then_inc(qq.h, 16)
                        else:
                            e.wait_ge(sS.h, m1s0 + _cntz(i) + 1)
                            e.dma_start(out=d["sz_s"][mc - 16, :, ts],
                                        in_=xzev[:, bk, :]).then_inc(qq.h, 16)
                        i += 1
            block.sync(m1_odma)
            sT.n += NTILE * 32 + 1
            sV.n += NTILE * 16
            sS.n += NTILE * 16
            dS.n += 16 * NTILE * 16
            dA2.n += 16 * NTILE * 16
            barrier()

            def mh_v(e):
                e.tensor_copy(out=chal[:], in_=PS[2][:, 0:48]).then_inc(sV.h, 1)
            block.vector(mh_v)
            sV.n += 1
            mhv_done = sV.n

            def mh_dma(e):
                e.wait_ge(sV.h, mhv_done)
                e.dma_start(out=d["cch_in"], in_=chal[:]).then_inc(dS.h, 16)
            block.sync(mh_dma)
            dS.n += 16
            barrier()

            def mh_cc(e):
                e.collective_compute(
                    "AllGather", OP.bypass,
                    replica_groups=[list(range(8))],
                    ins=[d["cch_in"]],
                    outs=[d["cch_out"].rearrange("r a b -> (r a) b")],
                ).then_inc(cS.h, 1)
            block.gpsimd(mh_cc)
            cS.n += 1

            mhd0, mhv0 = dS.n, sV.n

            def mh_dma2(e):
                e.wait_ge(cS.h, cS.n)
                for s in range(8):
                    if s > 0:
                        e.wait_ge(sV.h, mhv0 + s)
                    e.dma_start(out=chal[:], in_=d["cch_out"][s]).then_inc(dS.h, 16)
            block.sync(mh_dma2)
            dS.n += 16 * 8

            def mh_v2(e):
                for s in range(8):
                    e.wait_ge(dS.h, mhd0 + 16 * (s + 1))
                    e.scalar_tensor_tensor(
                        out=haloxin[:, :, 0:3],
                        in0=chal[:].rearrange("p (g r) -> p g r", g=G16),
                        scalar=maskprev_sb[:, s:s + 1],
                        in1=haloxin[:, :, 0:3],
                        op0=OP.mult, op1=OP.add).then_inc(sV.h, 1)
            block.vector(mh_v2)
            sV.n += 8
            barrier()
            tap_now("halo", [(haloxin[:].rearrange("p g r -> p (g r)"),
                              [128, G16 * 4], BF16),
                             (chal[:], [128, 48], BF16)])

        acols = msb("acols", [128, G16 * DST], F32)
        cwc = msb("cwc", [128, G16 * 4], F32)
        cbc = msb("cbc", [128, G16], F32)
        bdtc = msb("bdtc", [128, G16], F32)
        ddc = msb("ddc", [128, G16], F32)
        sel_sb = msb("sel_sb", [128, 32 * 128], BF16)
        wx_sb = msb("wx_sb", [128, G16 * 96], BF16)
        wdt_sb = msb("wdt_sb", [64, DIN], BF16)
        hcolA = msb("hcolA", [128, G16 * DST], F32)
        hcolC = msb("hcolC", [128, G16 * DST], F32)
        pcolC = msb("pcolC", [128, G16], F32)
        sumdp = msb("sumdp", [128, G16 * NTILE], F32)
        dcore = msb("dcore", [128, G16 * DST], F32)
        wkcc = msb("wkcc", [128, 512], F32)
        wkc2 = msb("wkc2", [128, 512], F32)

        def m0_dma(e):
            e.dma_start(out=cwc[:].rearrange("p (g k) -> p g k", g=G16),
                        in_=d["convw_c"]).then_inc(dS.h, 16)
            e.dma_start(out=cbc[:], in_=d["convb_c"]).then_inc(dS.h, 16)
            e.dma_start(out=bdtc[:], in_=d["bdt_c"]).then_inc(dS.h, 16)
            e.dma_start(out=ddc[:], in_=d["dd_c"]).then_inc(dS.h, 16)
            e.dma_start(out=sel_sb[:], in_=d["sel32"]).then_inc(dS.h, 16)
            e.dma_start(out=acols[:].rearrange("p (g s) -> p g s", g=G16),
                        in_=d["alog_c"]).then_inc(dS.h, 16)
        block.sync(m0_dma)
        dS.n += 16 * 6

        def m0_gdma(e):
            e.dma_start(out=wx_sb[:].rearrange("p (g w) -> p g w", g=G16),
                        in_=d["wx_w"].rearrange("g p w -> p g w")).then_inc(dG.h, 16)
            e.dma_start(out=wdt_sb[:], in_=d["wdt_w"]).then_inc(dG.h, 16)
        block.gpsimd(m0_gdma)
        dG.n += 32
        barrier()

        m0v, m0s = sV.n, sS.n

        def m0_s(e):
            e.activation(out=acols[:], in_=acols[:], func=AF.Exp).then_inc(sS.h, 1)
        block.scalar(m0_s)
        sS.n += 1

        def m0_v(e):
            e.wait_ge(sS.h, m0s + 1)
            e.tensor_scalar(out=acols[:], in0=acols[:], scalar1=-1.0, scalar2=None,
                            op0=OP.mult).then_inc(sV.h, 1)
            e.memset(hcolA[:], 0.0)
            e.memset(hcolC[:], 0.0)
            e.memset(pcolC[:], 0.0)
            e.memset(sumdp[:], 0.0).then_inc(sV.h, 1)
        block.vector(m0_v)
        sV.n += 2
        barrier()

        cvin = msb("cvin", [128, G16, 520], BF16)
        xct = msb("xct", [128, G16, TT], BF16)
        delt = msb("delt", [128, G16, TT], BF16)
        mt = msb("mt", [128, G16, TT], BF16)

        dbcsb = msb("dbcsb", [128, TT], BF16)
        stg = msb("stg", [128, 2, TT], BF16)
        brep = msb("brep", [128, 2, TT], BF16)
        crep = msb("crep", [128, 2, TT], BF16)
        dab = msb("dab", [128, 2, TT], BF16)
        wbu = msb("wbu", [128, 2, TT], BF16)
        scno = msb("scno", [128, TT], F32)
        wbuP = msb("wbuP", [128, 2, TT], BF16)
        scnoP = msb("scnoP", [128, 2, TT], F32)
        accv2 = lambda g: scnoP[:][:, 0, :] if g % 2 == 0 else scno[:]
        sps = lambda mc: scno[:] if mc % 2 == 0 else scnoP[:][:, 1, :]
        woutb = msb("woutb", [128, 2, D], BF16)
        lnh = xct[:][:, 0:G8, :]
        yac = cvin[:].rearrange("p g r -> p (g r)")[:, 0:G16 * TT].rearrange(
            "p (g t) -> p g t", g=G16)
        ymix = mt[:]
        h4t = delt[:].rearrange("p g t -> p (g t)").bitcast(F32).rearrange(
            "p (g t) -> p g t", g=G8)

        def emit_tile_prep(it):
            ts = slice(it * TT, (it + 1) * TT)
            pd0 = dS.n
            do_tap = (it == 0)

            def p_dma(e):
                for g in range(G16):
                    e.dma_start(out=cvin[:, g, 3:515],
                                in_=d["xin_s"][g, :, ts]).then_inc(dS.h, 16)
                if it > 0:
                    for g in range(G16):
                        e.dma_start(out=cvin[:, g, 0:3],
                                    in_=d["xin_s"][g, :, it * TT - 3:it * TT]
                                    ).then_inc(dS.h, 16)
            block.sync(p_dma)
            dS.n += 16 * (G16 * (2 if it > 0 else 1))

            if it == 0:
                vh0 = sV.n

                def p_v0(e):
                    e.wait_ge(dS.h, dS.n)
                    e.tensor_copy(out=cvin[:, :, 0:3],
                                  in_=haloxin[:, :, 0:3]).then_inc(sV.h, 1)
                block.vector(p_v0)
                sV.n += 1
            barrier()

            pv1, ps1, pa1, pa1b = sV.n, sS.n, dA.n, dA2.n

            def p_cv(e):
                for g in range(G16):
                    if g >= 2:
                        e.wait_ge(sS.h, ps1 + g - 1)
                    e.tensor_scalar(out=accv2(g), in0=cvin[:, g, 3:515],
                                    scalar1=cwc[:, g * 4 + 3:g * 4 + 4],
                                    scalar2=cbc[:, g:g + 1], op0=OP.mult,
                                    op1=OP.add).then_inc(sV.h, 1)
                    for k in range(3):
                        e.wait_ge(sV.h, pv1 + 4 * g + 1 + k)
                        e.scalar_tensor_tensor(out=accv2(g), in0=cvin[:, g, k:512 + k],
                                               scalar=cwc[:, g * 4 + k:g * 4 + k + 1],
                                               in1=accv2(g), op0=OP.mult,
                                               op1=OP.add).then_inc(sV.h, 1)
            block.vector(p_cv)

            def p_cs(e):
                for g in range(G16):
                    e.wait_ge(sV.h, pv1 + 4 * g + 4)
                    e.activation(out=xct[:, g, :], in_=accv2(g),
                                 func=AF.Silu).then_inc(sS.h, 1)
            block.scalar(p_cs)
            sV.n += 4 * G16
            sS.n += G16
            pv2 = sV.n

            def p_cd(e):
                for g in range(G16):
                    e.wait_ge(sS.h, ps1 + g + 1)
                    if g >= 2:
                        qa, qb = (dA, pa1) if g % 2 == 0 else (dA2, pa1b)
                        e.wait_ge(qa.h, qb + 16 * ((g - 2) // 2 + 1))
                    e.tensor_scalar(out=stg[:, g % 2, :], in0=xct[:, g, :],
                                    scalar1=ddc[:, g:g + 1], scalar2=None,
                                    op0=OP.mult).then_inc(sV.h, 1)
            block.vector(p_cd)

            def p_cddma(e):
                for g in range(G16):
                    e.wait_ge(sV.h, pv2 + g + 1)
                    qa, qb2 = (dA, pa1) if g % 2 == 0 else (dA2, pa1b)
                    if g >= 2:
                        e.wait_ge(qa.h, qb2 + 16 * ((g - 2) // 2 + 1))
                    e.dma_start(out=d["xcd_s"][g, :, ts],
                                in_=stg[:, g % 2, :]).then_inc(qa.h, 16)
            block.scalar(p_cddma)
            sV.n += G16
            dA.n += 16 * (G16 // 2)
            dA2.n += 16 * (G16 // 2)
            barrier()

            qt0, qv0 = sT.n, sV.n

            def p_qt(e):
                for kc in range(G16 - 1):
                    e.matmul(PS[0][0:96, 0:TT], wx_sb[:, kc * 96:(kc + 1) * 96],
                             xct[:, kc, :], start=(kc == 0), stop=False)
                e.matmul(PS[0][0:96, 0:TT], wx_sb[:, (G16 - 1) * 96:G16 * 96],
                         xct[:, G16 - 1, :], start=False, stop=True).then_inc(sT.h, 1)
            block.tensor(p_qt)

            def p_qv(e):
                e.wait_ge(sT.h, qt0 + 1)
                e.tensor_copy(out=dbcsb[0:96, :],
                              in_=PS[0][0:96, 0:TT]).then_inc(sV.h, 1)
            block.vector(p_qv)

            def p_qdma(e):
                e.wait_ge(sV.h, qv0 + 1)
                e.dma_start(out=d["bc_s"][:, ts], in_=dbcsb[64:96, :]).then_inc(dS.h, 16)
            block.sync(p_qdma)
            sT.n += 1
            sV.n += 1
            dS.n += 16
            barrier()

            rt0, rv0, rs0 = sT.n, sV.n, sS.n

            def p_rt(e):
                for mc in range(G16):
                    bk = mc % 2
                    if mc >= 2:
                        e.wait_ge(sS.h, rs0 + 2 * (mc - 1))
                    e.matmul(PS[bk][:, 0:TT], wdt_sb[:, mc * 128:(mc + 1) * 128],
                             dbcsb[0:64, :], start=True, stop=True).then_inc(sT.h, 1)
            block.tensor(p_rt)

            def p_rs(e):
                for mc in range(G16):
                    bk = mc % 2
                    e.wait_ge(sT.h, rt0 + mc + 1)
                    if mc >= 1:
                        e.wait_ge(sV.h, rv0 + 2 * mc - 1)
                    e.activation(out=sps(mc), in_=PS[bk][:, 0:TT], func=AF.Exp,
                                 bias=bdtc[:, mc:mc + 1]).then_inc(sS.h, 1)
                    e.wait_ge(sV.h, rv0 + 2 * mc + 1)
                    e.activation(out=delt[:, mc, :], in_=sps(mc), func=AF.Ln,
                                 accum_out=sumdp[:, it * G16 + mc:it * G16 + mc + 1]
                                 ).then_inc(sS.h, 1)
            block.scalar(p_rs)

            def p_rv(e):
                for mc in range(G16):
                    e.wait_ge(sS.h, rs0 + 2 * mc + 1)
                    e.tensor_scalar(out=sps(mc), in0=sps(mc),
                                    scalar1=1.0, scalar2=None,
                                    op0=OP.add).then_inc(sV.h, 1)
                    e.wait_ge(sS.h, rs0 + 2 * mc + 2)
                    e.tensor_mul(out=mt[:, mc, :], in0=delt[:, mc, :],
                                 in1=xct[:, mc, :]).then_inc(sV.h, 1)
            block.vector(p_rv)
            sT.n += G16
            sS.n += 2 * G16
            sV.n += 2 * G16
            barrier()

            if do_tap:
                tap_now("xct0", [(xct[:].rearrange("p g t -> p (g t)"),
                                  [128, G16 * TT], BF16),
                                 (cvin[:].rearrange("p g t -> p (g t)"),
                                  [128, G16 * 520], BF16),
                                 (delt[:].rearrange("p g t -> p (g t)"),
                                  [128, G16 * TT], BF16)])

            dd0 = dS.n

            def p_sdma(e):
                for g in range(G16):
                    e.dma_start(out=d["delta_s"][g, :, ts],
                                in_=delt[:, g, :]).then_inc(dS.h, 16)
            block.sync(p_sdma)
            dS.n += 16 * G16
            barrier()

        dabPR = [dab[:].rearrange("p a t -> p (a t)"),
                 wbuP[:].rearrange("p a t -> p (a t)")]
        wPR = [wbu[:].rearrange("p a t -> p (a t)"),
               stg[:].rearrange("p a t -> p (a t)")]
        scnoF = scnoP[:].rearrange("p a t -> p (a t)")
        NPAIR = G16 // 2
        NVP = 6

        def emit_scan_tile(it, passC, hcol_ov=None):
            assert passC
            ts = slice(it * TT, (it + 1) * TT)
            hcol = hcol_ov if hcol_ov is not None else hcolC
            hview = hcol.rearrange("p (g d) -> p g d", g=G16)
            st0, sv0, ss0 = sT.n, sV.n, sS.n
            PV = 1 + NPAIR * NVP

            def scan_pos(s, j):
                return sv0 + s * PV + 1 + j * NVP + 4

            def s_t(e):
                for s in range(DST):
                    bs = s % 2
                    if s >= 2:
                        e.wait_ge(sV.h, sv0 + (s - 1) * PV)
                    e.matmul(PS[4 + bs][:, 0:TT],
                             sel_sb[64:96, s * 128:(s + 1) * 128],
                             dbcsb[64:96, :], start=True, stop=True)
                    e.matmul(PS[6 + bs][:, 0:TT],
                             sel_sb[64:96, (16 + s) * 128:(17 + s) * 128],
                             dbcsb[64:96, :], start=True,
                             stop=True).then_inc(sT.h, 1)
            block.tensor(s_t)

            def s_s(e):
                for s in range(DST):
                    for dg in range(G16):
                        j, h = divmod(dg, 2)
                        if h == 0 and (j >= 2 or s >= 1):
                            s2, j2 = (s, j - 2) if j >= 2 else (s - 1, j + NPAIR - 2)
                            e.wait_ge(sV.h, scan_pos(s2, j2))
                        dslot = dabPR[j % 2].rearrange("p (a t) -> p a t", a=2)
                        e.activation(out=dslot[:, h, :], in_=delt[:, dg, :],
                                     func=AF.Exp,
                                     scale=acols[:, dg * DST + s:dg * DST + s + 1]
                                     ).then_inc(sS.h, 1)
            block.scalar(s_s)

            def s_v(e):
                v = sv0
                for s in range(DST):
                    bs = s % 2
                    e.wait_ge(sT.h, st0 + s + 1)
                    if s >= 1:
                        e.wait_ge(sV.h, v)
                    e.tensor_copy(out=brep[:, bs, :], in_=PS[4 + bs][:, 0:TT])
                    e.tensor_copy(out=crep[:, bs, :],
                                  in_=PS[6 + bs][:, 0:TT]).then_inc(sV.h, 1)
                    v += 1
                    for j in range(NPAIR):
                        dg0, dg1 = 2 * j, 2 * j + 1
                        dslot = dabPR[j % 2].rearrange("p (a t) -> p a t", a=2)
                        wslot = wPR[j % 2].rearrange("p (a t) -> p a t", a=2)
                        e.wait_ge(sS.h, ss0 + s * 16 + dg1 + 1)
                        e.wait_ge(sV.h, v)
                        e.tensor_mul(out=wslot[:, 0, :], in0=mt[:, dg0, :],
                                     in1=brep[:, bs, :])
                        e.tensor_mul(out=wslot[:, 1, :], in0=mt[:, dg1, :],
                                     in1=brep[:, bs, :]).then_inc(sV.h, 1)
                        v += 1
                        e.wait_ge(sV.h, v)
                        e.scalar_tensor_tensor(
                            out=wslot[:, 1, 0:1], in0=dslot[:, 1, 0:1],
                            scalar=hview[:, dg1, s:s + 1], in1=wslot[:, 1, 0:1],
                            op0=OP.mult, op1=OP.add).then_inc(sV.h, 1)
                        v += 1
                        e.wait_ge(sV.h, v)
                        e.tensor_scalar(out=dslot[:, 1, 0:1], in0=dslot[:, 1, 0:1],
                                        scalar1=0.0, scalar2=None,
                                        op0=OP.mult).then_inc(sV.h, 1)
                        v += 1
                        e.wait_ge(sV.h, v)
                        e.tensor_tensor_scan(
                            out=scnoF, data0=dabPR[j % 2], data1=wPR[j % 2],
                            initial=hview[:, dg0, s:s + 1],
                            op0=OP.mult, op1=OP.add).then_inc(sV.h, 1)
                        v += 1
                        e.wait_ge(sV.h, v)
                        e.tensor_copy(out=hview[:, 2 * j:2 * j + 2, s:s + 1],
                                      in_=scnoP[:][:, :, TT - 1:TT])
                        e.tensor_mul(
                            out=wslot[:],
                            in0=scnoP[:],
                            in1=crep[:, bs:bs + 1, :].broadcast_to([128, 2, TT])
                            ).then_inc(sV.h, 1)
                        v += 1
                        e.wait_ge(sV.h, v)
                        if s == 0:
                            e.tensor_copy(out=yac[:, 2 * j:2 * j + 2, :],
                                          in_=wslot[:]).then_inc(sV.h, 1)
                        else:
                            e.tensor_add(out=yac[:, 2 * j:2 * j + 2, :],
                                         in0=yac[:, 2 * j:2 * j + 2, :],
                                         in1=wslot[:]).then_inc(sV.h, 1)
                        v += 1
            block.vector(s_v)
            sT.n += DST
            sS.n += DST * G16
            sV.n += DST * PV
            barrier()

        for it in range(NTILE):
            emit_tile_prep(it)
            emit_scan_tile(it, passC=True, hcol_ov=hcolA)
            ts_z = slice(it * TT, (it + 1) * TT)

            def yz_dma(e, ts=ts_z):
                for g in range(G16):
                    e.dma_start(out=d["y_s"][g, :, ts],
                                in_=yac[:, g, :]).then_inc(dS.h, 16)
            block.sync(yz_dma)
            dS.n += 16 * G16
            barrier()

        g3v0, g3s0 = sV.n, sS.n

        def m3_v(e):
            for dg in range(G16):
                if dg > 0:
                    e.wait_ge(sV.h, g3v0 + 2 * dg)
                e.tensor_reduce(out=wkcc[:, dg:dg + 1],
                                in_=sumdp[:].rearrange("p (t g) -> p t g", g=G16)[:, :, dg],
                                axis=AX, op=OP.add).then_inc(sV.h, 1)
                e.wait_ge(sV.h, g3v0 + 2 * dg + 1)
                e.tensor_scalar(out=dcore[:, dg * DST:(dg + 1) * DST],
                                in0=acols[:, dg * DST:(dg + 1) * DST],
                                scalar1=wkcc[:, dg:dg + 1], scalar2=None,
                                op0=OP.mult).then_inc(sV.h, 1)
        block.vector(m3_v)
        sV.n += 2 * G16

        def m3_s(e):
            e.wait_ge(sV.h, g3v0 + 2 * G16)
            e.activation(out=dcore[:], in_=dcore[:], func=AF.Exp).then_inc(sS.h, 1)
        block.scalar(m3_s)
        sS.n += 1
        barrier()

        def m3_dma(e):
            e.dma_start(out=d["cc2_in"][:, 0:256], in_=dcore[:]).then_inc(dS.h, 16)
            e.dma_start(out=d["cc2_in"][:, 256:512], in_=hcolA[:]).then_inc(dS.h, 16)
        block.sync(m3_dma)
        dS.n += 32
        barrier()

        def m3_cc(e):
            e.collective_compute(
                "AllGather", OP.bypass,
                replica_groups=[list(range(8))],
                ins=[d["cc2_in"]],
                outs=[d["cc2_out"].rearrange("r a b -> (r a) b")],
            ).then_inc(cS.h, 1)
        block.gpsimd(m3_cc)
        cS.n += 1

        f2d0, f2v0 = dS.n, sV.n

        def m3_dma2(e):
            e.wait_ge(cS.h, cS.n)
            for s in range(8):
                if s > 0:
                    e.wait_ge(sV.h, f2v0 + 4 * s)
                e.dma_start(out=wkcc[:], in_=d["cc2_out"][s]).then_inc(dS.h, 16)
        block.sync(m3_dma2)
        dS.n += 16 * 8

        def m3_v2(e):
            for s in range(8):
                e.wait_ge(dS.h, f2d0 + 16 * (s + 1))
                e.tensor_scalar(out=wkc2[:, 0:256], in0=wkcc[:, 0:256], scalar1=1.0,
                                scalar2=None, op0=OP.subtract).then_inc(sV.h, 1)
                e.wait_ge(sV.h, f2v0 + 4 * s + 1)
                e.tensor_scalar(out=wkc2[:, 0:256], in0=wkc2[:, 0:256],
                                scalar1=maskpred[:, s:s + 1], scalar2=1.0,
                                op0=OP.mult, op1=OP.add).then_inc(sV.h, 1)
                e.wait_ge(sV.h, f2v0 + 4 * s + 2)
                e.tensor_mul(out=hcolC[:], in0=hcolC[:],
                             in1=wkc2[:, 0:256]).then_inc(sV.h, 1)
                e.wait_ge(sV.h, f2v0 + 4 * s + 3)
                e.scalar_tensor_tensor(out=hcolC[:], in0=wkcc[:, 256:512],
                                       scalar=maskpred[:, s:s + 1], in1=hcolC[:],
                                       op0=OP.mult, op1=OP.add).then_inc(sV.h, 1)
        block.vector(m3_v2)
        sV.n += 4 * 8
        barrier()
        tap_now("agg2", [(hcolA[:], [128, 256], F32), (dcore[:], [128, 256], F32),
                         (hcolC[:], [128, 256], F32)])

        Pb = mt[:]
        onesb = ones128[:][:, 0:1].broadcast_to([128, TT])
        for it in range(NTILE):
            ts = slice(it * TT, (it + 1) * TT)

            def rc_dma(e, ts=ts):
                for g in range(G16):
                    e.dma_start(out=delt[:, g, :],
                                in_=d["delta_s"][g, :, ts]).then_inc(dS.h, 16)
                for g in range(G16):
                    e.dma_start(out=yac[:, g, :],
                                in_=d["y_s"][g, :, ts]).then_inc(dS.h, 16)
                e.dma_start(out=dbcsb[64:96, :], in_=d["bc_s"][:, ts]).then_inc(dS.h, 16)
            block.sync(rc_dma)
            rc_d0 = dS.n
            dS.n += 16 * (2 * G16 + 1)

            rv0 = sV.n

            def r_v(e, rv0=rv0, rc_d0=rc_d0):
                v = rv0
                for dg in range(G16):
                    if dg == 0:
                        e.wait_ge(dS.h, rc_d0 + 16 * (2 * G16 + 1))
                    if dg > 0:
                        e.wait_ge(sV.h, v)
                    e.tensor_tensor_scan(
                        out=scno[:], data0=onesb, data1=delt[:, dg, :],
                        initial=pcolC[:, dg:dg + 1],
                        op0=OP.mult, op1=OP.add).then_inc(sV.h, 1)
                    v += 1
                    e.wait_ge(sV.h, v)
                    e.tensor_copy(out=Pb[:, dg, :], in_=scno[:]).then_inc(sV.h, 1)
                    v += 1
                    e.wait_ge(sV.h, v)
                    e.tensor_copy(out=pcolC[:, dg:dg + 1],
                                  in_=scno[:, TT - 1:TT]).then_inc(sV.h, 1)
                    v += 1
            block.vector(r_v)
            sV.n += 3 * G16
            barrier()

            yt0, yv0, ys0 = sT.n, sV.n, sS.n
            PV2 = 1 + 8 * 2

            def y_t(e, yt0=yt0, yv0=yv0):
                for s in range(DST):
                    bs = s % 2
                    if s == 0:
                        e.wait_ge(dS.h, dS.n)
                    if s >= 2:
                        e.wait_ge(sV.h, yv0 + (s - 1) * PV2)
                    e.matmul(PS[6 + bs][:, 0:TT],
                             sel_sb[64:96, (16 + s) * 128:(17 + s) * 128],
                             dbcsb[64:96, :], start=True,
                             stop=True).then_inc(sT.h, 1)
            block.tensor(y_t)

            def y_s2(e, yv0=yv0):
                for s in range(DST):
                    for dg in range(G16):
                        j, h = divmod(dg, 2)
                        gj = s * 8 + j
                        if h == 0 and gj >= 2:
                            s2, j2 = divmod(gj - 2, 8)
                            e.wait_ge(sV.h, yv0 + s2 * PV2 + 1 + j2 * 2 + 1)
                        dslot = dabPR[j % 2].rearrange("p (a t) -> p a t", a=2)
                        e.activation(out=dslot[:, h, :], in_=Pb[:, dg, :],
                                     func=AF.Exp,
                                     scale=acols[:, dg * DST + s:dg * DST + s + 1]
                                     ).then_inc(sS.h, 1)
            block.scalar(y_s2)

            def y_v(e, yt0=yt0, yv0=yv0, ys0=ys0):
                v = yv0
                for s in range(DST):
                    bs = s % 2
                    e.wait_ge(sT.h, yt0 + s + 1)
                    if s >= 1:
                        e.wait_ge(sV.h, v)
                    e.tensor_copy(out=crep[:, bs, :],
                                  in_=PS[6 + bs][:, 0:TT]).then_inc(sV.h, 1)
                    v += 1
                    for j in range(8):
                        dg0, dg1 = 2 * j, 2 * j + 1
                        e.wait_ge(sS.h, ys0 + s * 16 + dg1 + 1)
                        e.wait_ge(sV.h, v)
                        dslot = dabPR[j % 2].rearrange("p (a t) -> p a t", a=2)
                        e.scalar_tensor_tensor(
                            out=wbu[:, 0, :], in0=dslot[:, 0, :],
                            scalar=hcolC[:, dg0 * DST + s:dg0 * DST + s + 1],
                            in1=crep[:, bs, :], op0=OP.mult, op1=OP.mult)
                        e.scalar_tensor_tensor(
                            out=wbu[:, 1, :], in0=dslot[:, 1, :],
                            scalar=hcolC[:, dg1 * DST + s:dg1 * DST + s + 1],
                            in1=crep[:, bs, :], op0=OP.mult,
                            op1=OP.mult).then_inc(sV.h, 1)
                        v += 1
                        e.wait_ge(sV.h, v)
                        e.tensor_add(out=yac[:, 2 * j:2 * j + 2, :],
                                     in0=yac[:, 2 * j:2 * j + 2, :],
                                     in1=wbu[:].rearrange("p a t -> p a t")
                                     ).then_inc(sV.h, 1)
                        v += 1
            block.vector(y_v)
            sT.n += DST
            sS.n += DST * G16
            sV.n += DST * PV2
            barrier()

            t4v0, t4d0, t4db = sV.n, dS.n, dG.n

            def t4_dma(e, ts=ts):
                for g in range(G16):
                    if g >= 1:
                        e.wait_ge(dS.h, t4d0 + 32 * g)
                    if g >= 2:
                        e.wait_ge(sV.h, t4v0 + 2 * (g - 1))
                    e.dma_start(out=stg[:, 0, :] if g % 2 == 0 else stg[:, 1, :],
                                in_=d["xcd_s"][g, :, ts]).then_inc(dS.h, 16)
                    e.dma_start(out=brep[:, 0, :] if g % 2 == 0 else brep[:, 1, :],
                                in_=d["sz_s"][g, :, ts]).then_inc(dS.h, 16)
            block.sync(t4_dma)

            def t4_v(e, ts=ts):
                for g in range(G16):
                    e.wait_ge(dS.h, t4d0 + 32 * (g + 1))
                    if g >= 1:
                        e.wait_ge(sV.h, t4v0 + 2 * g)
                    e.tensor_add(out=wbu[:, g % 2, :], in0=yac[:, g, :],
                                 in1=stg[:, g % 2, :]).then_inc(sV.h, 1)
                    e.wait_ge(sV.h, t4v0 + 2 * g + 1)
                    e.tensor_mul(out=ymix[:, g, :], in0=wbu[:, g % 2, :],
                                 in1=brep[:, g % 2, :]).then_inc(sV.h, 1)
            block.vector(t4_v)
            sV.n += 2 * G16
            dS.n += 16 * 2 * G16
            barrier()

            w4t0, w4d0, w4db = sT.n, dG.n, dG2.n

            def w4_dma(e):
                for kc in range(G16):
                    if kc >= 2:
                        e.wait_ge(sT.h, w4t0 + kc - 1)
                    qq = dG if kc % 2 == 0 else dG2
                    e.dma_start(out=woutb[:, kc % 2, :],
                                in_=d["wout_w"][kc]).then_inc(qq.h, 16)
            block.gpsimd(w4_dma)

            def w4_t(e):
                for kc in range(G16):
                    if kc % 2 == 0:
                        e.wait_ge(dG.h, w4d0 + 16 * (kc // 2 + 1))
                    else:
                        e.wait_ge(dG2.h, w4db + 16 * (kc // 2 + 1))
                    for g in range(G8 - 1):
                        e.matmul(PS[g][:, 0:TT], woutb[:, kc % 2, g * 128:(g + 1) * 128],
                                 ymix[:, kc, :], start=(kc == 0), stop=(kc == G16 - 1))
                    e.matmul(PS[G8 - 1][:, 0:TT],
                             woutb[:, kc % 2, (G8 - 1) * 128:G8 * 128],
                             ymix[:, kc, :], start=(kc == 0),
                             stop=(kc == G16 - 1)).then_inc(sT.h, 1)
            block.tensor(w4_t)
            dG.n += 16 * (G16 // 2)
            dG2.n += 16 * (G16 // 2)
            sT.n += G16

            l4d0 = dA.n
            l4db = dA2.n

            def l4_dma(e, ts=ts):
                for g in range(G8):
                    qa = dA if g % 2 == 0 else dA2
                    e.dma_start(out=lnh[:, g, :],
                                in_=d["lnx3_s"][g, :, ts]).then_inc(qa.h, 16)
            block.scalar(l4_dma)
            dA.n += 16 * 4
            dA2.n += 16 * 4

            h4v0 = sV.n

            def h4_v(e):
                e.wait_ge(sT.h, sT.n)
                e.wait_ge(dA.h, dA.n)
                e.wait_ge(dA2.h, dA2.n)
                for g in range(G8 - 1):
                    e.tensor_add(out=h4t[:, g, :], in0=PS[g][:, 0:TT], in1=lnh[:, g, :])
                e.tensor_add(out=h4t[:, G8 - 1, :], in0=PS[G8 - 1][:, 0:TT],
                             in1=lnh[:, G8 - 1, :]).then_inc(sV.h, 1)
            block.vector(h4_v)
            sV.n += 1
            barrier()

            ln_fm(lambda g, tss: h4t[:, g, 0:TT],
                  lambda g, tss: lnh[:, g, 0:TT], tiles=[it])
            barrier()

            r4v0 = sV.n

            def r4_v(e, ts=ts):
                for g in range(G8):
                    if g > 0:
                        e.wait_ge(sV.h, r4v0 + 2 * g)
                    e.scalar_tensor_tensor(out=resid[:, g, ts], in0=lnh[:, g, :],
                                           scalar=gamma_sb[:, g:g + 1],
                                           in1=resid[:, g, ts], op0=OP.mult,
                                           op1=OP.add).then_inc(sV.h, 1)
                    e.wait_ge(sV.h, r4v0 + 2 * g + 1)
                    e.tensor_scalar(out=resid[:, g, ts], in0=resid[:, g, ts],
                                    scalar1=beta_sb[:, g:g + 1], scalar2=None,
                                    op0=OP.add).then_inc(sV.h, 1)
            block.vector(r4_v)
            sV.n += 2 * G8
            barrier()

        tap_now("mambaout", [(resid[:].rearrange("p g t -> p (g t)"),
                              [128, G8 * TC], F32)])

    if upto >= 6:
        ffn_call("f2")

    barrier()

    def out_sync(e):
        for g in range(G8):
            e.dma_start(out=d["y_out"][g], in_=resid[:, g, :]).then_inc(dS.h, 16)
        e.wait_ge(dS.h, dS.n + 16 * G8)
    block.sync(out_sync)
    dS.n += 16 * G8

def _mk_sel32():
    import ml_dtypes
    s = np.zeros((128, 32 * 128), np.float32)
    for j in range(32):
        s[64 + j, j * 128:(j + 1) * 128] = 1.0
    return s.astype(ml_dtypes.bfloat16)


def prep_inputs(inp):
    x = np.ascontiguousarray(np.asarray(inp["L0_attn_a_x"], np.float32))
    g = lambda n: np.ascontiguousarray(np.asarray(inp[n], np.float32))

    qkvg_w = np.stack([
        g("L0_attn_a_Wq").reshape(G8, 128, 128),
        g("L0_attn_a_Wk").reshape(G8, 128, 128),
        g("L0_attn_a_Wv").reshape(G8, 128, 128),
        g("L0_attn_a_Wg").reshape(G8, 128, 128),
    ])
    qkvg_b = np.stack([
        g("L0_attn_a_bq"), g("L0_attn_a_bk"), g("L0_attn_a_bv"), g("L0_attn_a_bg")
    ], axis=1)
    ab_w = np.concatenate([g("L0_attn_a_Wa"), g("L0_attn_a_Wb")], 1).reshape(G8, 128, 2)
    ab_b = np.zeros((128, 1), np.float32)
    ab_b[0] = g("L0_attn_a_ba")[0]
    ab_b[32] = g("L0_attn_a_bb")[0]
    wo_w = g("L0_attn_a_Wo")
    bo_c = g("L0_attn_a_bo").reshape(G8, 128).T.copy()

    win_w = g("L0_attn_b_W_in").reshape(G8, 128, HID)
    convw_c = g("L0_attn_b_conv_w").reshape(G16, 128, 4).transpose(1, 0, 2).copy()
    convb_c = g("L0_attn_b_conv_b").reshape(G16, 128).T.copy()
    wx_w = g("L0_attn_b_W_x").reshape(G16, 128, 96)
    wdt_w = g("L0_attn_b_W_dt")
    bdt_c = g("L0_attn_b_b_dt").reshape(G16, 128).T.copy()
    alog_c = g("L0_attn_b_A_log").reshape(G16, 128, DST).transpose(1, 0, 2).copy()
    dd_c = g("L0_attn_b_D").reshape(G16, 128).T.copy()
    gamma_c = g("L0_attn_b_gamma").reshape(G8, 128).T.copy()
    beta_c = g("L0_attn_b_beta").reshape(G8, 128).T.copy()
    wout_w = g("L0_attn_b_W_out").reshape(G16, 128, D)

    w1t = np.ascontiguousarray(g("L0_ffn_w1").T).reshape(G8, 128, HID)
    w2t = np.ascontiguousarray(g("L0_ffn_w2").T).reshape(G32, 128, D)

    import ml_dtypes
    identf = np.eye(128, dtype=np.float32)
    identb = identf.astype(ml_dtypes.bfloat16)
    tl = np.tril(np.ones((128, 128), np.float32), -1)
    mask_sn = -tl
    mask_snT = -tl.T.copy()
    mask_incT = np.triu(np.ones((128, 128), np.float32))
    ones_k1 = np.ones((1, 128), np.float32).astype(ml_dtypes.bfloat16)
    ones128 = np.ones((128, 128), np.float32).astype(ml_dtypes.bfloat16)

    shared = dict(
        qkvg_w=qkvg_w, qkvg_b=qkvg_b, ab_w=ab_w, ab_b=ab_b, wo_w=wo_w, bo_c=bo_c,
        win_w=win_w, convw_c=convw_c, convb_c=convb_c, wx_w=wx_w, wdt_w=wdt_w,
        bdt_c=bdt_c, alog_c=alog_c, dd_c=dd_c, gamma_c=gamma_c, beta_c=beta_c,
        wout_w=wout_w, w1t=w1t, w2t=w2t, identf=identf, identb=identb,
        mask_sn=mask_sn, mask_snT=mask_snT, mask_incT=mask_incT,
        ones_k1=ones_k1, ones128=ones128,
        ones_k1f=np.ones((1, 128), np.float32),
        sel127=np.zeros((128, 128), np.float32),
        ones128f=np.ones((128, 128), np.float32),
        sel32=_mk_sel32(),
        fconst_c=np.tile(np.array(FCONSTS, np.float32), (128, 1)),
    )
    shared["sel127"][127, :] = 1.0
    in_maps = []
    for c in range(8):
        bq, p = c // 4, c % 4
        xs = x[bq, p * TC:(p + 1) * TC]
        x_fm = np.ascontiguousarray(xs.T).reshape(G8, 128, TC)
        maskpred = np.zeros((128, 8), np.float32)
        for j in range(8):
            if j // 4 == bq and j % 4 < p:
                maskpred[:, j] = 1.0
        maskhalo = np.full((128, 1), 1.0 if p > 0 else 0.0, np.float32)
        maskprev = np.zeros((128, 8), np.float32)
        if p > 0:
            maskprev[:, c - 1] = 1.0
        in_maps.append(dict(shared, x_fm=x_fm, maskpred_c=maskpred,
                            maskhalo_c=maskhalo, maskprev_c=maskprev))
    return in_maps


def kernel(**inputs):
    in_maps = prep_inputs(inputs)
    nc, b = build_program()
    res = run_bass_kernel_spmd(nc, in_maps, core_ids=list(range(8)))
    out = np.zeros((2, 8192, 1024), np.float32)
    for c in range(8):
        y = res.results[c]["y_out"]
        bq, p = c // 4, c % 4
        out[bq, p * TC:(p + 1) * TC] = y.reshape(D, TC).T
    return out


# version 14 (final)
# speedup vs baseline: 1.0365x; 1.0038x over previous
import contextlib
import math
import numpy as np

import concourse.bass as bass
import concourse.mybir as mybir
from concourse.bass_utils import run_bass_kernel_spmd

F32 = mybir.dt.float32
BF16 = mybir.dt.bfloat16
AF = mybir.ActivationFunctionType
OP = mybir.AluOpType
AX = mybir.AxisListType.X

TC = 2048
TT = 512
NTILE = TC // TT
L = 128
NCH = TC // L
D = 1024
G8 = 8
DIN = 2048
G16 = 16
DST = 16
HID = 4096
G32 = 32
DTR = 64
MAGIC = float(np.float32(1.5 * 2 ** 23))
LN_EPS = 1e-5
FCONSTS = [1e-5, -0.5, -1.0, 1.0, 0.5, 127.0, 1.0 / 127.0,
           float(np.log(127.0)), 1.5 * 2.0 ** 23]


class Ctr:

    def __init__(self, es, nc, name):
        self.h = es.enter_context(nc.semaphore(name))
        self.n = 0

    def inc(self, instr, k=1):
        instr.then_inc(self.h, k)
        self.n += k
        return self.n


class Builder:
    def __init__(self, nc, taps=None):
        self.nc = nc
        self.es = contextlib.ExitStack()
        self.taps = taps if taps is not None else []
        self.tap_tensors = {}
        self._dram = {}
        self._sb = {}

    def dram(self, name, shape, dtype=F32, kind="Internal"):
        t = self.nc.dram_tensor(name, list(shape), dtype, kind=kind)
        self._dram[name] = t
        return t.ap()

    def sb(self, name, shape, dtype):
        t = self.es.enter_context(self.nc.sbuf_tensor(name, list(shape), dtype))
        self._sb[name] = t
        return t

    def ps(self, name, shape=(128, 512), dtype=F32):
        t = self.es.enter_context(self.nc.psum_tensor(name, list(shape), dtype))
        return t

    def tap(self, name, shape, dtype=F32):
        if name in self.taps:
            ap = self.dram("tap_" + name, shape, dtype, kind="ExternalOutput")
            self.tap_tensors[name] = ap
            return ap
        return None


def build_program(taps=None, upto=99):
    nc = bass.Bass("TRN2", target_bir_lowering=False)
    b = Builder(nc, taps)
    b.upto = upto
    d = declare_io(b)
    with b.es:
        emit(b, d)
    return nc, b


def declare_io(b):
    d = {}
    I = lambda n, s, dt=F32: d.__setitem__(n, b.dram(n, s, dt, kind="ExternalInput"))
    I("x_fm", [G8, 128, TC])
    d["y_out"] = b.dram("y_out", [G8, 128, TC], F32, kind="ExternalOutput")
    I("qkvg_w", [4, G8, 128, 128])
    I("qkvg_b", [128, 4])
    I("ab_w", [G8, 128, 2])
    I("ab_b", [128, 1])
    I("wo_w", [128, D])
    I("bo_c", [128, G8])
    I("win_w", [G8, 128, HID])
    I("convw_c", [128, G16, 4])
    I("convb_c", [128, G16])
    I("wx_w", [G16, 128, 96])
    I("wdt_w", [DTR, DIN])
    I("bdt_c", [128, G16])
    I("alog_c", [128, G16, DST])
    I("dd_c", [128, G16])
    I("gamma_c", [128, G8])
    I("beta_c", [128, G8])
    I("wout_w", [G16, 128, D])
    I("w1t", [G8, 128, HID])
    I("w2t", [G32, 128, D])
    I("identf", [128, 128])
    I("identb", [128, 128], BF16)
    I("mask_sn", [128, 128])
    I("mask_snT", [128, 128])
    I("mask_incT", [128, 128])
    I("ones_k1", [1, 128], BF16)
    I("ones_k1f", [1, 128])
    I("ones128", [128, 128], BF16)
    I("sel127", [128, 128])
    I("ones128f", [128, 128])
    I("maskpred_c", [128, 8])
    I("maskprev_c", [128, 8])
    I("sel32", [128, 32 * 128], BF16)
    I("maskhalo_c", [128, 1])
    I("fconst_c", [128, len(FCONSTS)])
    d["w1q_s"] = b.dram("w1q_s", [G8, 128, HID], BF16)
    d["w2q_s"] = b.dram("w2q_s", [G32, 128, D], BF16)
    d["delta_s"] = b.dram("delta_s", [G16, 128, TC], BF16)
    d["m_s"] = b.dram("m_s", [G16, 128, TC], BF16)
    d["xcd_s"] = b.dram("xcd_s", [G16, 128, TC], BF16)
    d["sz_s"] = b.dram("sz_s", [G16, 128, TC], BF16)
    d["bc_s"] = b.dram("bc_s", [2 * DST, TC], BF16)
    d["row_s"] = b.dram("row_s", [16, TC])
    d["cc1_in"] = b.dram("cc1_in", [256, 128])
    d["cc1_out"] = b.dram("cc1_out", [8, 256, 128])
    d["cc2_in"] = b.dram("cc2_in", [128, 2 * G16 * DST])
    d["cc2_out"] = b.dram("cc2_out", [8, 128, 2 * G16 * DST])
    d["cch_in"] = b.dram("cch_in", [128, 48], BF16)
    d["cch_out"] = b.dram("cch_out", [8, 128, 48], BF16)
    d["xin_s"] = b.dram("xin_s", [G16, 128, TC], BF16)
    d["y_s"] = b.dram("y_s", [G16, 128, TC], BF16)
    d["lnx3_s"] = b.dram("lnx3_s", [G8, 128, TC], BF16)
    return d


def emit(b, d):
    nc = b.nc
    es = b.es
    upto = getattr(b, "upto", 99)
    resid = b.sb("resid", [128, G8, TC], F32)
    consts_f = b.sb("consts_f", [128, 6 * 128], F32)
    identb = b.sb("identb_sb", [128, 128], BF16)
    ones_k1 = b.sb("ones_k1_sb", [1, 128], BF16)
    ones_k1f = b.sb("ones_k1f_sb", [1, 128], F32)
    ones128 = b.sb("ones128_sb", [128, 128], BF16)
    biases = b.sb("biases", [128, 16], F32)
    gamma_sb = b.sb("gamma_sb", [128, G8], F32)
    beta_sb = b.sb("beta_sb", [128, G8], F32)
    maskpred = b.sb("maskpred_sb", [128, 8], F32)
    maskhalo = b.sb("maskhalo_sb", [128, 1], F32)
    maskprev_sb = b.sb("maskprev_sb2", [128, 8], F32)
    fconst = b.sb("fconst_sb", [128, len(FCONSTS)], F32)
    for _i, _v in enumerate(FCONSTS):
        nc.const_aps.register(F32, float(_v), fconst[:, _i:_i + 1])
    murep = b.sb("murep", [128, TT], F32)
    rsvrep = b.sb("rsvrep", [128, TT], F32)
    scr_a = b.sb("scr_a", [128, TT], F32)
    scr_c = b.sb("scr_c", [128, TT], F32)
    scr_u = b.sb("scr_u", [128, TT], F32)
    xbuf = b.sb("xbuf", [128, G8, TT], BF16)
    sqbuf = b.sb("sqbuf", [128, G8, TT], BF16)

    identf = consts_f[:, 0:128]
    mask_sn = consts_f[:, 128:256]
    mask_snT = consts_f[:, 256:384]
    mask_incT = consts_f[:, 384:512]
    sel127 = consts_f[:, 512:640]
    ones128f = consts_f[:, 640:768]

    PS = [b.ps(f"ps{i}") for i in range(8)]

    def PB(bank, slot=0):
        return PS[bank][:, slot * 64:(slot + 1) * 64].bitcast(BF16)

    sT = Ctr(es, nc, "sT")
    sV = Ctr(es, nc, "sV")
    sS = Ctr(es, nc, "sS")
    sP = Ctr(es, nc, "sP")
    dS = Ctr(es, nc, "dS")
    dG = Ctr(es, nc, "dG")
    dG2 = Ctr(es, nc, "dG2")
    dA = Ctr(es, nc, "dA")
    dA2 = Ctr(es, nc, "dA2")
    cS = Ctr(es, nc, "cS")
    ALL = [sT, sV, sS, sP, dS, dG, dG2, dA, dA2, cS]
    dW1 = Ctr(es, nc, "dW1")
    dW2 = Ctr(es, nc, "dW2")
    dW3 = Ctr(es, nc, "dW3")
    dW4 = Ctr(es, nc, "dW4")
    dW5 = Ctr(es, nc, "dW5")
    dW6 = Ctr(es, nc, "dW6")

    block = es.enter_context(nc.Block())

    def barrier():
        vals = [c.n for c in ALL]

        def mk():
            def f(e):
                for c, v in zip(ALL, vals):
                    if v:
                        e.wait_ge(c.h, v)
            return f
        block.tensor(mk())
        block.vector(mk())
        block.scalar(mk())
        block.gpsimd(mk())
        block.sync(mk())

    def tap_now(name, aps_shapes):
        if not b.taps or name not in b.taps:
            return
        barrier()
        for i, (ap, shape, dt) in enumerate(aps_shapes):
            t = b.dram(f"tap_{name}_{i}", shape, dt, kind="ExternalOutput")
            b.tap_tensors[f"{name}_{i}"] = t

            def tp(e, ap=ap, t=t):
                e.dma_start(out=t, in_=ap).then_inc(dS.h, 16)
            block.sync(tp)
            dS.n += 16
        barrier()

    def ln_fm(src_g, dst, store_stats=None, tiles=None, entry=None):
        recip_n = 1.0 / D
        tl = list(tiles) if tiles is not None else list(range(NTILE))
        t0, v0, s0 = sT.n, sV.n, sS.n
        NV, NS, NT_ = 6, 3, 1
        ew = entry or {}

        def lv(e):
            for hh, vv in ew.get("v", []):
                e.wait_ge(hh, vv)
            for it_i, itt in enumerate(tl):
                it = it_i
                ts = slice(itt * TT, (itt + 1) * TT)
                if it > 0:
                    e.wait_ge(sT.h, t0 + it)
                for g in range(G8 - 1):
                    e.tensor_copy(out=xbuf[:, g, :], in_=src_g(g, ts))
                e.tensor_copy(out=xbuf[:, G8 - 1, :],
                              in_=src_g(G8 - 1, ts)).then_inc(sV.h, 1)
                e.wait_ge(sT.h, t0 + it + 1)
                e.tensor_scalar(out=murep[:], in0=PS[6][:], scalar1=recip_n,
                                scalar2=None, op0=OP.mult)
                e.tensor_scalar(out=scr_c[:], in0=PS[7][:], scalar1=recip_n,
                                scalar2=None, op0=OP.mult).then_inc(sV.h, 1)
                e.wait_ge(sV.h, v0 + NV * it + 2)
                e.tensor_mul(out=scr_a[:], in0=murep[:],
                             in1=murep[:]).then_inc(sV.h, 1)
                e.wait_ge(sV.h, v0 + NV * it + 3)
                e.tensor_sub(out=scr_a[:], in0=scr_c[:],
                             in1=scr_a[:]).then_inc(sV.h, 1)
                e.wait_ge(sS.h, s0 + NS * it + 2)
                for g in range(G8 - 1):
                    e.tensor_sub(out=sqbuf[:, g, :], in0=src_g(g, ts), in1=murep[:])
                e.tensor_sub(out=sqbuf[:, G8 - 1, :], in0=src_g(G8 - 1, ts),
                             in1=murep[:]).then_inc(sV.h, 1)
                e.wait_ge(sS.h, s0 + NS * it + 3)
                e.wait_ge(sV.h, v0 + NV * it + 5)
                for g in range(G8 - 1):
                    e.tensor_mul(out=dst(g, ts), in0=sqbuf[:, g, :], in1=rsvrep[:])
                ii = e.tensor_mul(out=dst(G8 - 1, ts), in0=sqbuf[:, G8 - 1, :],
                                  in1=rsvrep[:])
                if store_stats is not None:
                    e.tensor_copy(out=store_stats[0](it), in_=murep[:])
                    ii = e.tensor_copy(out=store_stats[1](it), in_=rsvrep[:])
                ii.then_inc(sV.h, 1)
        block.vector(lv)

        def ls(e):
            for hh, vv in ew.get("s", []):
                e.wait_ge(hh, vv)
            for it_i, itt in enumerate(tl):
                it = it_i
                e.wait_ge(sV.h, v0 + NV * it + 1)
                if it > 0:
                    e.wait_ge(sT.h, t0 + it)
                for g in range(G8 - 1):
                    e.activation(out=sqbuf[:, g, :], in_=xbuf[:, g, :], func=AF.Square)
                e.activation(out=sqbuf[:, G8 - 1, :], in_=xbuf[:, G8 - 1, :],
                             func=AF.Square).then_inc(sS.h, 1)
                e.wait_ge(sV.h, v0 + NV * it + 4)
                e.activation(out=scr_u[:], in_=scr_a[:], func=AF.Ln,
                             bias=LN_EPS).then_inc(sS.h, 1)
                e.wait_ge(sS.h, s0 + NS * it + 2)
                e.activation(out=rsvrep[:], in_=scr_u[:], func=AF.Exp,
                             scale=-0.5).then_inc(sS.h, 1)
        block.scalar(ls)

        def lt(e):
            for hh, vv in ew.get("t", []):
                e.wait_ge(hh, vv)
            for it_i, itt in enumerate(tl):
                it = it_i
                e.wait_ge(sV.h, v0 + NV * it + 1)
                if it > 0:
                    e.wait_ge(sV.h, v0 + NV * (it - 1) + 2)
                for g in range(G8):
                    e.matmul(PS[6][:], ones128[:], xbuf[:, g, :],
                             start=(g == 0), stop=(g == G8 - 1))
                e.wait_ge(sS.h, s0 + NS * it + 1)
                for g in range(G8 - 1):
                    e.matmul(PS[7][:], ones128[:], sqbuf[:, g, :],
                             start=(g == 0), stop=False)
                e.matmul(PS[7][:], ones128[:], sqbuf[:, G8 - 1, :],
                         start=False, stop=True).then_inc(sT.h, 1)
        block.tensor(lt)

        sV.n += NV * len(tl)
        sS.n += NS * len(tl)
        sT.n += NT_ * len(tl)

    def p0_sync(e):
        for g in range(G8):
            e.dma_start(out=resid[:, g, :], in_=d["x_fm"][g]).then_inc(dS.h, 16)
        e.dma_start(out=consts_f[:, 0:128], in_=d["identf"]).then_inc(dS.h, 16)
        e.dma_start(out=consts_f[:, 128:256], in_=d["mask_sn"]).then_inc(dS.h, 16)
        e.dma_start(out=consts_f[:, 256:384], in_=d["mask_snT"]).then_inc(dS.h, 16)
        e.dma_start(out=consts_f[:, 384:512], in_=d["mask_incT"]).then_inc(dS.h, 16)
        e.dma_start(out=consts_f[:, 512:640], in_=d["sel127"]).then_inc(dS.h, 16)
        e.dma_start(out=consts_f[:, 640:768], in_=d["ones128f"]).then_inc(dS.h, 16)
        e.dma_start(out=biases[:, 5:13], in_=d["bo_c"]).then_inc(dS.h, 16)
        e.dma_start(out=identb[:], in_=d["identb"]).then_inc(dS.h, 16)
        e.dma_start(out=ones_k1[:], in_=d["ones_k1"]).then_inc(dS.h, 16)
        e.dma_start(out=ones_k1f[:], in_=d["ones_k1f"]).then_inc(dS.h, 16)
        e.dma_start(out=ones128[:], in_=d["ones128"]).then_inc(dS.h, 16)
        e.dma_start(out=biases[:, 0:4], in_=d["qkvg_b"]).then_inc(dS.h, 16)
        e.dma_start(out=biases[:, 4:5], in_=d["ab_b"]).then_inc(dS.h, 16)
        e.dma_start(out=gamma_sb[:], in_=d["gamma_c"]).then_inc(dS.h, 16)
        e.dma_start(out=beta_sb[:], in_=d["beta_c"]).then_inc(dS.h, 16)
        e.dma_start(out=maskpred[:], in_=d["maskpred_c"]).then_inc(dS.h, 16)
        e.dma_start(out=maskhalo[:], in_=d["maskhalo_c"]).then_inc(dS.h, 16)
        e.dma_start(out=maskprev_sb[:], in_=d["maskprev_c"]).then_inc(dS.h, 16)
        e.dma_start(out=fconst[:], in_=d["fconst_c"]).then_inc(dS.h, 16)
    block.sync(p0_sync)
    dS.n += 16 * (G8 + 19)

    bq_col = biases[:, 0:1]
    bk_col = biases[:, 1:2]
    bv_col = biases[:, 2:3]
    bg_col = biases[:, 3:4]
    ba_col = biases[0:1, 4:5]
    bb_col = biases[32:33, 4:5]

    barrier()

    with contextlib.ExitStack() as aes:
        asb = lambda n, s, dt: aes.enter_context(nc.sbuf_tensor(n, list(s), dt))
        rows = asb("rows", [128, TC], F32)
        qf = asb("qf", [128, TC], BF16)
        kf = asb("kf", [128, TC], BF16)
        vf = asb("vf", [128, TC], BF16)
        gatef = asb("gatef", [128, TC], BF16)
        wo_sb = asb("wo_sb", [128, D], BF16)
        pes = contextlib.ExitStack()
        psb = lambda n, s, dt: pes.enter_context(nc.sbuf_tensor(n, list(s), dt))
        lnx1 = psb("lnx1", [128, G8, TC], BF16)
        qkvg_sb = psb("qkvg_sb", [128, 4 * G8 * 128], BF16)
        ab_sb = psb("ab_sb", [128, G8 * 2], BF16)

        def pw_gpsimd(e):
            e.dma_start(out=qkvg_sb[:].rearrange("p (j g m) -> p j g m", j=4, g=G8),
                        in_=d["qkvg_w"].rearrange("j g p m -> p j g m")).then_inc(dG.h, 16)
            e.dma_start(out=ab_sb[:].rearrange("p (g m) -> p g m", g=G8),
                        in_=d["ab_w"].rearrange("g p m -> p g m")).then_inc(dG.h, 16)
            e.dma_start(out=wo_sb[:], in_=d["wo_w"]).then_inc(dG.h, 16)
        block.gpsimd(pw_gpsimd)
        dG.n += 16 * 3
        qkvg_v = qkvg_sb[:].rearrange("p (j g m) -> p j g m", j=4, g=G8)
        ab_v = ab_sb[:].rearrange("p (g m) -> p g m", g=G8)

        if upto >= 1:
            ln_fm(lambda g, ts: resid[:, g, ts], lambda g, ts: lnx1[:, g, ts])
            barrier()
            tap_now("lnx1", [(lnx1[:].rearrange("p g t -> p (g t)"),
                              [128, G8 * TC], BF16)])

        if upto >= 2:
            t0, v0, s0 = sT.n, sV.n, sS.n
            NV, NS, NT_ = 3, 5, 2

            def proj_t(e):
                for it in range(NTILE):
                    ts = slice(it * TT, (it + 1) * TT)
                    if it == 0:
                        e.wait_ge(dG.h, 16 * 3)
                    else:
                        e.wait_ge(sV.h, v0 + NV * it)
                        e.wait_ge(sS.h, s0 + NS * it)
                    for j in range(4):
                        for kc in range(G8):
                            e.matmul(PS[j][:], qkvg_v[:, j, kc, :], lnx1[:, kc, ts],
                                     start=(kc == 0), stop=(kc == G8 - 1))
                    for kc in range(G8):
                        e.matmul(PS[4][0:1, :], ab_v[:, kc, 0:1], lnx1[:, kc, ts],
                                 start=(kc == 0), stop=(kc == G8 - 1))
                    for kc in range(G8 - 1):
                        e.matmul(PS[4][32:33, :], ab_v[:, kc, 1:2], lnx1[:, kc, ts],
                                 start=(kc == 0), stop=False, tile_position=(0, 32))
                    e.matmul(PS[4][32:33, :], ab_v[:, G8 - 1, 1:2], lnx1[:, G8 - 1, ts],
                             start=False, stop=True,
                             tile_position=(0, 32)).then_inc(sT.h, 1)
                    e.wait_ge(sS.h, s0 + NS * it + 1)
                    e.matmul(PS[5][:], ones128[:], sqbuf[:, 0, :], start=True,
                             stop=True).then_inc(sT.h, 1)
            block.tensor(proj_t)

            def proj_s(e):
                for it in range(NTILE):
                    ts = slice(it * TT, (it + 1) * TT)
                    e.wait_ge(sT.h, t0 + NT_ * it + 1)
                    if it > 0:
                        e.wait_ge(sT.h, t0 + NT_ * it)
                    e.activation(out=sqbuf[:, 0, :], in_=PS[1][:], func=AF.Square,
                                 bias=bk_col)
                    e.activation(out=gatef[:, ts], in_=PS[3][:], func=AF.Sigmoid,
                                 bias=bg_col)
                    e.activation(out=rows[0:1, ts], in_=PS[4][0:1, :], func=AF.Sigmoid,
                                 bias=ba_col)
                    e.activation(out=rows[32:33, ts], in_=PS[4][32:33, :],
                                 func=AF.Sigmoid, bias=bb_col).then_inc(sS.h, 1)
                    e.wait_ge(sS.h, s0 + NS * it + 1)
                    e.activation(out=rows[64:65, ts], in_=rows[0:1, ts],
                                 func=AF.Ln).then_inc(sS.h, 1)
                    e.wait_ge(sT.h, t0 + NT_ * it + 2)
                    if it > 0:
                        e.wait_ge(sV.h, v0 + NV * it)
                    e.activation(out=scr_a[:], in_=PS[5][:],
                                 func=AF.Sqrt).then_inc(sS.h, 1)
                    e.wait_ge(sV.h, v0 + NV * it + 2)
                    e.activation(out=scr_c[:], in_=scr_a[:],
                                 func=AF.Ln).then_inc(sS.h, 1)
                    e.wait_ge(sS.h, s0 + NS * it + 4)
                    e.activation(out=scr_c[:], in_=scr_c[:], func=AF.Exp,
                                 scale=-1.0).then_inc(sS.h, 1)
            block.scalar(proj_s)

            def proj_v(e):
                for it in range(NTILE):
                    ts = slice(it * TT, (it + 1) * TT)
                    e.wait_ge(sT.h, t0 + NT_ * it + 1)
                    e.tensor_scalar(out=qf[:, ts], in0=PS[0][:], scalar1=bq_col,
                                    scalar2=None, op0=OP.add)
                    e.tensor_scalar(out=vf[:, ts], in0=PS[2][:], scalar1=bv_col,
                                    scalar2=None, op0=OP.add).then_inc(sV.h, 1)
                    e.wait_ge(sS.h, s0 + NS * it + 3)
                    e.tensor_scalar(out=scr_a[:], in0=scr_a[:], scalar1=1e-12,
                                    scalar2=None, op0=OP.max).then_inc(sV.h, 1)
                    e.wait_ge(sS.h, s0 + NS * it + 5)
                    e.scalar_tensor_tensor(out=kf[:, ts], in0=PS[1][:], scalar=bk_col,
                                           in1=scr_c[:], op0=OP.add,
                                           op1=OP.mult).then_inc(sV.h, 1)
            block.vector(proj_v)

            sT.n += NT_ * NTILE
            sS.n += NS * NTILE
            sV.n += NV * NTILE
            barrier()

            tap_now("proj", [
                (qf[:], [128, TC], BF16), (kf[:], [128, TC], BF16),
                (vf[:], [128, TC], BF16), (gatef[:], [128, TC], BF16),
                (rows[:, :], [128, TC], F32),
            ])
        pes.close()

        if upto >= 3:
            lgch = asb("lgch", [16, 128], F32)
            lgcum = asb("lgcum", [16, 128], F32)
            onesf16 = asb("onesf16", [16, 128], F32)
            dncols = asb("dncols", [128, 96], F32)
            lgrow = asb("lgrow", [1, TC], F32)
            grow = asb("grow", [1, TC], F32)
            brow = asb("brow", [1, TC], F32)
            lgc = dncols[:, 0:16]
            lgLb = dncols[:, 16:32]
            rc = dncols[:, 32:48]
            bcolv = dncols[:, 48:64]
            gLb = dncols[:, 64:80]
            gc = dncols[:, 80:96]

            def aux_sync(e):
                e.dma_start(out=d["row_s"][0:1, :], in_=rows[64:65, :]).then_inc(dS.h, 16)
                e.dma_start(out=d["row_s"][1:2, :], in_=rows[32:33, :]).then_inc(dS.h, 16)
                e.wait_ge(dS.h, dS.n + 32)
                e.dma_start(out=lgch[:],
                            in_=d["row_s"][0:1, :].rearrange("a (c p) -> (a c) p", c=16)
                            ).then_inc(dS.h, 16)
                e.dma_start(out=bcolv,
                            in_=d["row_s"][1:2, :].rearrange("a (c p) -> (a p) c", c=16),
                            allow_slow_non_contiguous=True).then_inc(dS.h, 16)
                e.dma_start(out=brow[:], in_=d["row_s"][1:2, :]).then_inc(dS.h, 16)
            block.sync(aux_sync)
            dS.n += 16 * 5
            aux_d0 = dS.n
            av0, as0, at0 = sV.n, sS.n, sT.n

            def aux_v(e):
                e.wait_ge(dS.h, aux_d0)
                e.memset(onesf16[:], 1.0).then_inc(sV.h, 1)
                e.wait_ge(sV.h, av0 + 1)
                e.tensor_tensor_scan(out=lgcum[:], data0=onesf16[:], data1=lgch[:],
                                     initial=0.0, op0=OP.mult,
                                     op1=OP.add).then_inc(sV.h, 1)
                e.wait_ge(sT.h, at0 + 1)
                e.tensor_copy(out=lgc, in_=PS[0][:, 0:16]).then_inc(sV.h, 1)
                e.wait_ge(sT.h, at0 + 2)
                e.tensor_copy(out=lgLb, in_=PS[1][:, 0:16]).then_inc(sV.h, 1)
                e.wait_ge(sV.h, av0 + 4)
                e.tensor_sub(out=rc, in0=lgLb, in1=lgc).then_inc(sV.h, 1)
            block.vector(aux_v)

            def aux_t(e):
                e.wait_ge(sV.h, av0 + 2)
                e.transpose(PS[0][:, 0:16], lgcum[:],
                            identf[0:16, 0:16]).then_inc(sT.h, 1)
                e.wait_ge(sV.h, av0 + 3)
                e.matmul(PS[1][:, 0:16], sel127, lgc, start=True,
                         stop=True).then_inc(sT.h, 1)
            block.tensor(aux_t)

            def aux_s(e):
                e.wait_ge(sV.h, av0 + 5)
                e.activation(out=rc, in_=rc, func=AF.Exp).then_inc(sS.h, 1)
                e.activation(out=gLb, in_=lgLb, func=AF.Exp)
                e.activation(out=gc, in_=lgc, func=AF.Exp).then_inc(sS.h, 1)
            block.scalar(aux_s)
            sV.n += 5
            sT.n += 2
            sS.n += 2

            def aux2_sync(e):
                e.wait_ge(sV.h, av0 + 2)
                e.dma_start(out=d["row_s"][2:3, :].rearrange("a (c p) -> (a c) p", c=16),
                            in_=lgcum[:]).then_inc(dS.h, 16)
                e.wait_ge(dS.h, dS.n + 16)
                e.dma_start(out=lgrow[:], in_=d["row_s"][2:3, :]).then_inc(dS.h, 16)
            block.sync(aux2_sync)
            dS.n += 32
            aux_d1 = dS.n

            def aux2_s(e):
                e.wait_ge(dS.h, aux_d1)
                e.activation(out=grow[:], in_=lgrow[:], func=AF.Exp).then_inc(sS.h, 1)
            block.scalar(aux2_s)
            sS.n += 1
            barrier()
            tap_now("aux", [(dncols[:], [128, 96], F32), (lgrow[:], [1, TC], F32)])

            chx = asb("chx", [128, NCH, 256], BF16)
            wpt = asb("wpt", [128, NCH, 128], BF16)
            kt = asb("ktb", [128, NCH, 128], BF16)
            gtin = asb("gtin", [128, NCH, 128], BF16)
            wpr = asb("wpr", [128, NCH, 128], BF16)
            qg = asb("qg", [128, TC], BF16)
            pcur = asb("pcur", [128, NCH, 128], BF16)
            tcur = asb("tcur", [128, NCH, 128], BF16)
            ogf = asb("ogf", [128, TC], BF16)
            wkd = asb("wkd", [128, 2, 128], F32)
            wkm = asb("wkm", [128, 2, 128], F32)
            wkmi = asb("wkmi", [128, 2, 128], F32)
            wkgT = asb("wkgT", [128, 2, 128], F32)
            wkA = asb("wkA", [128, 2, 128], BF16)
            gmF = asb("gmF", [128, 2, 128], F32)
            gmTm = asb("gmTm", [128, 2, 128], F32)
            wk1 = asb("wk1", [128, 2, 128], F32)
            wk2 = asb("wk2", [128, 2, 128], F32)
            Zf = asb("Zf", [128, 128], F32)
            Zb = asb("Zbb", [128, 128], BF16)
            S0b = asb("S0b", [128, 128], BF16)
            Baccb = asb("Baccb", [128, 128], BF16)
            BaccTb = asb("BaccTb", [128, 128], BF16)
            Baccf = asb("Baccf", [128, 128], F32)
            wkv = asb("wkv", [128, 128], BF16)
            wkv2 = asb("wkv2", [128, 128], BF16)
            wkv3 = asb("wkv3", [128, 128], BF16)
            wkf = asb("wkf", [128, 256], F32)

            pt0, pv0, ps0 = sT.n, sV.n, sS.n
            PNV = 4

            def prep_t(e):
                for j in range(NCH):
                    cs = slice(j * L, (j + 1) * L)
                    w = j % 2
                    if j >= 2:
                        e.wait_ge(sV.h, pv0 + PNV * (j - 1))
                    e.matmul(PS[0 + w][:, 0:128], kf[:, cs], kf[:, cs],
                             start=True, stop=True)
                    e.matmul(PS[2 + w][:, 0:128], ones_k1f[:], lgrow[0:1, cs],
                             start=True, stop=True)
                    e.matmul(PS[2 + w][:, 128:256], ones_k1f[:], brow[0:1, cs],
                             start=True, stop=True)
                    e.transpose(PB(4 + w, 0), kf[:, cs], identb[:])
                    e.transpose(PB(4 + w, 1), vf[:, cs],
                                identb[:]).then_inc(sT.h, 1)
            block.tensor(prep_t)

            def prep_v1(e):
                for j in range(NCH):
                    w = j % 2
                    e.wait_ge(sT.h, pt0 + j + 1)
                    if j >= 2:
                        e.wait_ge(sS.h, ps0 + (j - 1))
                    e.tensor_copy(out=wkA[:, w, :], in_=PS[0 + w][:, 0:128])
                    e.tensor_scalar(out=wkd[:, w, :], in0=PS[2 + w][:, 0:128],
                                    scalar1=lgc[:, j:j + 1], scalar2=None,
                                    op0=OP.subtract).then_inc(sV.h, 1)
                    e.wait_ge(sV.h, pv0 + PNV * j + 1)
                    e.tensor_scalar(out=wkm[:, w, :], in0=wkd[:, w, :], scalar1=0.0,
                                    scalar2=None, op0=OP.max)
                    e.tensor_scalar(out=wkmi[:, w, :], in0=wkd[:, w, :], scalar1=0.0,
                                    scalar2=None, op0=OP.min).then_inc(sV.h, 1)
                    e.wait_ge(sS.h, ps0 + j + 1)
                    e.tensor_mul(out=gtin[:, j, :], in0=wkgT[:, w, :], in1=mask_incT)
                    e.tensor_mul(out=gmTm[:, w, :], in0=wkgT[:, w, :], in1=mask_snT)
                    e.scalar_tensor_tensor(out=wk1[:, w, :], in0=wkA[:, w, :],
                                           scalar=bcolv[:, j:j + 1], in1=gmF[:, w, :],
                                           op0=OP.mult, op1=OP.mult)
                    e.tensor_mul(out=wk2[:, w, :], in0=wkA[:, w, :],
                                 in1=PS[2 + w][:, 128:256])
                    e.tensor_scalar(out=chx[:, j, 0:128], in0=PB(4 + w, 1),
                                    scalar1=bcolv[:, j:j + 1], scalar2=None,
                                    op0=OP.mult)
                    e.tensor_scalar(out=chx[:, j, 128:256], in0=PB(4 + w, 0),
                                    scalar1=bcolv[:, j:j + 1], scalar2=gc[:, j:j + 1],
                                    op0=OP.mult, op1=OP.mult)
                    e.tensor_copy(out=kt[:, j, :],
                                  in_=PB(4 + w, 0)).then_inc(sV.h, 1)
                    e.wait_ge(sV.h, pv0 + PNV * j + 3)
                    e.tensor_mul(out=pcur[:, j, :], in0=wk1[:, w, :], in1=mask_sn)
                    e.tensor_mul(out=tcur[:, j, :], in0=wk2[:, w, :],
                                 in1=gmTm[:, w, :]).then_inc(sV.h, 1)
            block.vector(prep_v1)

            def prep_s1(e):
                for j in range(NCH):
                    w = j % 2
                    e.wait_ge(sV.h, pv0 + PNV * j + 2)
                    e.activation(out=gmF[:, w, :], in_=wkm[:, w, :], func=AF.Exp,
                                 scale=-1.0)
                    e.activation(out=wkgT[:, w, :], in_=wkmi[:, w, :],
                                 func=AF.Exp).then_inc(sS.h, 1)
            block.scalar(prep_s1)

            sT.n += NCH
            sV.n += PNV * NCH
            sS.n += NCH
            barrier()

            NLVL = 7
            st0, sv0 = sT.n, sV.n

            def solve_t(e):
                for lvl in range(NLVL):
                    for j in range(NCH):
                        w = j % 2
                        if lvl > 0 and j < 2:
                            e.wait_ge(sV.h, sv0 + lvl * NCH)
                        if j >= 2:
                            e.wait_ge(sV.h, sv0 + lvl * NCH + (j - 1))
                        e.matmul(PS[0 + w][:, 0:256], identb[:], chx[:, j, :],
                                 start=True, stop=False)
                        if lvl < NLVL - 1:
                            e.matmul(PS[0 + w][:, 0:256], tcur[:, j, :], chx[:, j, :],
                                     start=False, stop=True)
                            e.matmul(PS[2 + w][:, 0:128], tcur[:, j, :], pcur[:, j, :],
                                     start=True, stop=True)
                            e.matmul(PS[2 + w][:, 128:256], pcur[:, j, :],
                                     tcur[:, j, :],
                                     start=True, stop=True).then_inc(sT.h, 1)
                        else:
                            e.matmul(PS[0 + w][:, 0:256], tcur[:, j, :], chx[:, j, :],
                                     start=False, stop=True).then_inc(sT.h, 1)
            block.tensor(solve_t)

            def solve_v(e):
                for lvl in range(NLVL):
                    for j in range(NCH):
                        w = j % 2
                        e.wait_ge(sT.h, st0 + lvl * NCH + j + 1)
                        ii = e.tensor_copy(out=chx[:, j, :], in_=PS[0 + w][:, 0:256])
                        if lvl < NLVL - 1:
                            e.tensor_copy(out=pcur[:, j, :], in_=PS[2 + w][:, 0:128])
                            ii = e.tensor_copy(out=tcur[:, j, :],
                                               in_=PS[2 + w][:, 128:256])
                        ii.then_inc(sV.h, 1)
            block.vector(solve_v)
            sT.n += NLVL * NCH
            sV.n += NLVL * NCH
            barrier()

            qt0, qv0 = sT.n, sV.n

            def post_t(e):
                for j in range(NCH):
                    w = j % 2
                    if j >= 2:
                        e.wait_ge(sV.h, qv0 + (j - 1))
                    e.transpose(PB(0 + w, 0), chx[:, j, 128:256],
                                identb[:]).then_inc(sT.h, 1)
                for it in range(NTILE):
                    ts = slice(it * TT, (it + 1) * TT)
                    if it >= 1:
                        e.wait_ge(sV.h, qv0 + NCH + it)
                    e.matmul(PS[4][:, 0:TT], ones_k1f[:], grow[0:1, ts],
                             start=True, stop=True).then_inc(sT.h, 1)
            block.tensor(post_t)

            def post_v(e):
                for j in range(NCH):
                    w = j % 2
                    e.wait_ge(sT.h, qt0 + j + 1)
                    e.tensor_copy(out=wpt[:, j, :], in_=PB(0 + w, 0))
                    e.tensor_scalar(out=wpr[:, j, :], in0=chx[:, j, 128:256],
                                    scalar1=rc[:, j:j + 1], scalar2=None,
                                    op0=OP.mult).then_inc(sV.h, 1)
                for it in range(NTILE):
                    ts = slice(it * TT, (it + 1) * TT)
                    e.wait_ge(sT.h, qt0 + NCH + it + 1)
                    e.tensor_mul(out=qg[:, ts], in0=qf[:, ts],
                                 in1=PS[4][:, 0:TT]).then_inc(sV.h, 1)
            block.vector(post_v)
            sT.n += NCH + NTILE
            sV.n += NCH + NTILE
            barrier()
            tap_now("solve", [
                (chx[:].rearrange("p c x -> p (c x)"), [128, NCH * 256], BF16),
                (kt[:].rearrange("p c x -> p (c x)"), [128, NCH * 128], BF16),
                (wpt[:].rearrange("p c x -> p (c x)"), [128, NCH * 128], BF16),
                (gtin[:].rearrange("p c x -> p (c x)"), [128, NCH * 128], BF16),
            ])

            def zs_v_init(e):
                e.memset(Zf[:], 0.0)
                e.memset(Zb[:], 0.0)
                e.memset(S0b[:], 0.0)
                e.tensor_copy(out=Baccb[:], in_=identb[:])
                e.tensor_copy(out=BaccTb[:], in_=identb[:])
                e.tensor_copy(out=Baccf[:], in_=identf).then_inc(sV.h, 1)
            block.vector(zs_v_init)
            sV.n += 1
            zt0, zv0 = sT.n, sV.n
            ZT, ZV = 4, 6

            def zs_t(e):
                for j in range(NCH):
                    e.wait_ge(sV.h, zv0 + ZV * j)
                    e.matmul(PS[0][:, 0:128], wpt[:, j, :], Zb[:],
                             start=True, stop=True).then_inc(sT.h, 1)
                    e.wait_ge(sV.h, zv0 + ZV * j + 2)
                    e.matmul(PS[1][:, 0:128], kt[:, j, :], wkv2[:],
                             start=True, stop=True)
                    e.matmul(PS[2][:, 0:128], kt[:, j, :], wpr[:, j, :],
                             start=True, stop=True).then_inc(sT.h, 1)
                    e.wait_ge(sV.h, zv0 + ZV * j + 4)
                    e.transpose(PB(3, 0), wkv[:], identb[:])
                    e.transpose(PB(4, 0), Zb[:], identb[:]).then_inc(sT.h, 1)
                    e.wait_ge(sV.h, zv0 + ZV * j + 5)
                    e.matmul(PS[5][:, 0:128], BaccTb[:], wkv3[:],
                             start=True, stop=True)
                    e.matmul(PS[6][:, 0:128], wkv3[:], BaccTb[:],
                             start=True, stop=True).then_inc(sT.h, 1)
            block.tensor(zs_t)

            def zs_v(e):
                for j in range(NCH):
                    e.wait_ge(sT.h, zt0 + ZT * j + 1)
                    e.tensor_sub(out=wkv2[:], in0=chx[:, j, 0:128],
                                 in1=PS[0][:, 0:128]).then_inc(sV.h, 1)
                    e.wait_ge(sV.h, zv0 + ZV * j + 1)
                    e.tensor_scalar(out=wkv2[:], in0=wkv2[:], scalar1=rc[:, j:j + 1],
                                    scalar2=None, op0=OP.mult).then_inc(sV.h, 1)
                    e.wait_ge(sT.h, zt0 + ZT * j + 2)
                    e.scalar_tensor_tensor(out=Zf[:], in0=Zf[:], scalar=gLb[:, j:j + 1],
                                           in1=PS[1][:, 0:128], op0=OP.mult,
                                           op1=OP.add)
                    e.scalar_tensor_tensor(out=wkv[:], in0=identb[:],
                                           scalar=gLb[:, j:j + 1], in1=PS[2][:, 0:128],
                                           op0=OP.mult,
                                           op1=OP.subtract).then_inc(sV.h, 1)
                    e.wait_ge(sV.h, zv0 + ZV * j + 3)
                    e.tensor_copy(out=Zb[:], in_=Zf[:]).then_inc(sV.h, 1)
                    e.wait_ge(sT.h, zt0 + ZT * j + 3)
                    e.tensor_copy(out=wkv3[:], in_=PB(3, 0))
                    e.tensor_copy(out=S0b[:], in_=PB(4, 0)).then_inc(sV.h, 1)
                    e.wait_ge(sT.h, zt0 + ZT * j + 4)
                    e.tensor_copy(out=Baccb[:], in_=PS[5][:, 0:128])
                    e.tensor_copy(out=Baccf[:], in_=PS[5][:, 0:128])
                    e.tensor_copy(out=BaccTb[:], in_=PS[6][:, 0:128]).then_inc(sV.h, 1)
            block.vector(zs_v)
            sT.n += ZT * NCH
            sV.n += ZV * NCH
            barrier()
            tap_now("agg", [(Zf[:], [128, 128], F32), (Baccf[:], [128, 128], F32)])

            def cc1_sync(e):
                e.dma_start(out=d["cc1_in"][0:128, :], in_=Zf[:]).then_inc(dS.h, 16)
                e.dma_start(out=d["cc1_in"][128:256, :], in_=Baccf[:]).then_inc(dS.h, 16)
            block.sync(cc1_sync)
            dS.n += 32
            cc1_d = dS.n

            def cc1_g(e):
                e.wait_ge(dS.h, cc1_d)
                e.collective_compute(
                    "AllGather", OP.bypass,
                    replica_groups=[list(range(8))],
                    ins=[d["cc1_in"]],
                    outs=[d["cc1_out"].rearrange("r a b -> (r a) b")],
                ).then_inc(cS.h, 1)
            block.gpsimd(cc1_g)
            cS.n += 1

            def fold_v_init(e):
                e.wait_ge(cS.h, cS.n)
                e.memset(Zf[:], 0.0)
                e.memset(Zb[:], 0.0).then_inc(sV.h, 1)
            block.vector(fold_v_init)
            sV.n += 1
            ft0, fv0, fd0 = sT.n, sV.n, dS.n

            def fold_sync(e):
                e.wait_ge(cS.h, cS.n)
                for s in range(8):
                    if s > 0:
                        e.wait_ge(sV.h, fv0 + 4 * s - 1)
                        e.wait_ge(dS.h, fd0 + 32 * s)
                    e.dma_start(out=wkf[:, 0:128],
                                in_=d["cc1_out"][s, 0:128, :]).then_inc(dS.h, 16)
                    e.dma_start(out=wkf[:, 128:256],
                                in_=d["cc1_out"][s, 128:256, :]).then_inc(dS.h, 16)
            block.sync(fold_sync)
            dS.n += 16 * 16

            def fold_v(e):
                for s in range(8):
                    e.wait_ge(dS.h, fd0 + 32 * (s + 1))
                    e.tensor_sub(out=wk1[:, 0, :], in0=wkf[:, 128:256],
                                 in1=identf).then_inc(sV.h, 1)
                    e.wait_ge(sV.h, fv0 + 4 * s + 1)
                    e.scalar_tensor_tensor(out=wkv[:], in0=wk1[:, 0, :],
                                           scalar=maskpred[:, s:s + 1], in1=identf,
                                           op0=OP.mult,
                                           op1=OP.add).then_inc(sV.h, 1)
                    e.wait_ge(sT.h, ft0 + s + 1)
                    e.scalar_tensor_tensor(out=Zf[:], in0=wkf[:, 0:128],
                                           scalar=maskpred[:, s:s + 1],
                                           in1=PS[0][:, 0:128], op0=OP.mult,
                                           op1=OP.add).then_inc(sV.h, 1)
                    e.wait_ge(sV.h, fv0 + 4 * s + 3)
                    e.tensor_copy(out=Zb[:], in_=Zf[:]).then_inc(sV.h, 1)
            block.vector(fold_v)

            def fold_t(e):
                for s in range(8):
                    e.wait_ge(sV.h, fv0 + 4 * s + 2)
                    e.matmul(PS[0][:, 0:128], wkv[:], Zb[:], start=True,
                             stop=True).then_inc(sT.h, 1)
            block.tensor(fold_t)
            sV.n += 4 * 8
            sT.n += 8
            barrier()

            def fold_fix_t(e):
                e.transpose(PB(1, 0), Zb[:], identb[:]).then_inc(sT.h, 1)
            block.tensor(fold_fix_t)
            sT.n += 1
            fxt = sT.n

            def fold_fix_v(e):
                e.wait_ge(sT.h, fxt)
                e.tensor_copy(out=S0b[:], in_=PB(1, 0)).then_inc(sV.h, 1)
            block.vector(fold_fix_v)
            sV.n += 1
            barrier()
            tap_now("zin", [(Zf[:], [128, 128], F32)])

            ot0, ov0 = sT.n, sV.n
            OT, OV = 5, 7

            def os_t(e):
                for j in range(NCH):
                    cs = slice(j * L, (j + 1) * L)
                    e.wait_ge(sV.h, ov0 + OV * j)
                    e.matmul(PS[0][:, 0:128], wpt[:, j, :], Zb[:],
                             start=True, stop=True).then_inc(sT.h, 1)
                    e.wait_ge(sV.h, ov0 + OV * j + 2)
                    e.transpose(PB(1, 0), wkv[:], identb[:]).then_inc(sT.h, 1)
                    e.wait_ge(sV.h, ov0 + OV * j + 3)
                    e.matmul(PS[2][:, 0:128], wkv3[:], qf[:, cs],
                             start=True, stop=True).then_inc(sT.h, 1)
                    e.wait_ge(sV.h, ov0 + OV * j + 4)
                    e.matmul(PS[3][:, 0:128], kt[:, j, :], wkv[:],
                             start=True, stop=False)
                    e.matmul(PS[3][:, 0:128], S0b[:], qg[:, cs],
                             start=False, stop=True)
                    e.matmul(PS[4][:, 0:128], kt[:, j, :], wkv2[:],
                             start=True, stop=True).then_inc(sT.h, 1)
                    e.wait_ge(sV.h, ov0 + OV * j + 6)
                    e.transpose(PB(5, 0), Zb[:], identb[:]).then_inc(sT.h, 1)
            block.tensor(os_t)

            def os_v(e):
                for j in range(NCH):
                    cs = slice(j * L, (j + 1) * L)
                    e.wait_ge(sT.h, ot0 + OT * j + 1)
                    e.tensor_sub(out=wkv[:], in0=chx[:, j, 0:128],
                                 in1=PS[0][:, 0:128]).then_inc(sV.h, 1)
                    e.wait_ge(sV.h, ov0 + OV * j + 1)
                    e.tensor_scalar(out=wkv2[:], in0=wkv[:], scalar1=rc[:, j:j + 1],
                                    scalar2=None, op0=OP.mult).then_inc(sV.h, 1)
                    e.wait_ge(sT.h, ot0 + OT * j + 2)
                    e.tensor_copy(out=wkv3[:], in_=PB(1, 0)).then_inc(sV.h, 1)
                    e.wait_ge(sT.h, ot0 + OT * j + 3)
                    e.tensor_mul(out=wkv[:], in0=PS[2][:, 0:128],
                                 in1=gtin[:, j, :]).then_inc(sV.h, 1)
                    e.wait_ge(sT.h, ot0 + OT * j + 4)
                    e.tensor_mul(out=ogf[:, cs], in0=PS[3][:, 0:128], in1=gatef[:, cs])
                    e.scalar_tensor_tensor(out=Zf[:], in0=Zf[:], scalar=gLb[:, j:j + 1],
                                           in1=PS[4][:, 0:128], op0=OP.mult,
                                           op1=OP.add).then_inc(sV.h, 1)
                    e.wait_ge(sV.h, ov0 + OV * j + 5)
                    e.tensor_copy(out=Zb[:], in_=Zf[:]).then_inc(sV.h, 1)
                    e.wait_ge(sT.h, ot0 + OT * j + 5)
                    e.tensor_copy(out=S0b[:], in_=PB(5, 0)).then_inc(sV.h, 1)
            block.vector(os_v)
            sT.n += OT * NCH
            sV.n += OV * NCH
            barrier()

            wt0, wv0 = sT.n, sV.n

            def wo_t(e):
                for it in range(NTILE):
                    ts = slice(it * TT, (it + 1) * TT)
                    for g in range(G8):
                        bk = 6 + (g % 2)
                        if it > 0 or g >= 2:
                            e.wait_ge(sV.h, wv0 + it * G8 + g - 1)
                        e.matmul(PS[bk][:], wo_sb[:, g * 128:(g + 1) * 128],
                                 ogf[:, ts], start=True, stop=True).then_inc(sT.h, 1)
            block.tensor(wo_t)

            def wo_v(e):
                for it in range(NTILE):
                    ts = slice(it * TT, (it + 1) * TT)
                    for g in range(G8):
                        bk = 6 + (g % 2)
                        e.wait_ge(sT.h, wt0 + it * G8 + g + 1)
                        e.scalar_tensor_tensor(out=resid[:, g, ts], in0=PS[bk][:],
                                               scalar=biases[:, 5 + g:6 + g],
                                               in1=resid[:, g, ts], op0=OP.add,
                                               op1=OP.add).then_inc(sV.h, 1)
            block.vector(wo_v)
            sT.n += NTILE * G8
            sV.n += NTILE * G8
            barrier()
            tap_now("attnout", [(resid[:].rearrange("p g t -> p (g t)"),
                                 [128, G8 * TC], F32)])

    LN127 = float(np.log(127.0))

    def emit_wquant():
        wes = contextlib.ExitStack()
        wsb = lambda n, s, dt: wes.enter_context(nc.sbuf_tensor(n, list(s), dt))
        wwk = wsb("wwk", [128, 2, HID], F32)
        wab = wsb("wab", [128, 2, HID], BF16)
        wq1 = wsb("wq1", [128, HID], F32)
        wqo = wsb("wqo", [128, 2, HID], BF16)
        acc = wsb("acc", [128, 48], F32)
        srow = wsb("srow", [1, 4], F32)
        scol = wsb("scol", [128, 4], F32)
        CH_ALL = [("w1t", "w1q_s", G8, HID), ("w2t", "w2q_s", G32, D)]
        NCHW = G8 + G32
        d0, v0, s0, t0 = dS.n, sV.n, sS.n, sT.n
        d0b = dG2.n

        def wq_dma1(e, par=0):
            i = 0
            for (src, dst, n, wd) in CH_ALL:
                for g in range(n):
                    if i % 2 == par:
                        if i >= 2:
                            e.wait_ge(sS.h, s0 + i - 1)
                            if par == 0:
                                e.wait_ge(dS.h, d0 + 16 * (i // 2))
                        qq = dS if i % 2 == 0 else dG2
                        e.dma_start(out=wwk[:, i % 2, 0:wd],
                                    in_=d[src][g]).then_inc(qq.h, 16)
                    i += 1
        block.sync(lambda e: wq_dma1(e, 0))
        block.gpsimd(lambda e: wq_dma1(e, 1))
        dS.n += 16 * ((NCHW + 1) // 2)
        dG2.n += 16 * (NCHW // 2)

        def wq_s1(e):
            i = 0
            for (src, dst, n, wd) in CH_ALL:
                for g in range(n):
                    qq, qb = (dS, d0) if i % 2 == 0 else (dG2, d0b)
                    e.wait_ge(qq.h, qb + 16 * (i // 2 + 1))
                    if i >= 2:
                        e.wait_ge(sS.h, s0 + i - 1)
                    e.activation(out=wab[:, i % 2, 0:wd], in_=wwk[:, i % 2, 0:wd],
                                 func=AF.Abs,
                                 accum_out=acc[:, i:i + 1]).then_inc(sS.h, 1)
                    i += 1
        block.scalar(wq_s1)
        sS.n += NCHW

        def wq_v1(e):
            e.wait_ge(sS.h, s0 + NCHW)
            e.tensor_reduce(out=acc[:, 40:41], in_=acc[:, 0:G8], axis=AX, op=OP.add)
            e.tensor_reduce(out=acc[:, 41:42], in_=acc[:, G8:NCHW], axis=AX,
                            op=OP.add).then_inc(sV.h, 1)
        block.vector(wq_v1)

        def wq_t1(e):
            e.wait_ge(sV.h, v0 + 1)
            e.matmul(PS[0][0:1, 0:2], ones128f[:, 0:1], acc[:, 40:42], start=True,
                     stop=True).then_inc(sT.h, 1)
        block.tensor(wq_t1)

        def wq_v2(e):
            e.wait_ge(sT.h, t0 + 1)
            e.tensor_scalar(out=srow[0:1, 0:2], in0=PS[0][0:1, 0:2],
                            scalar1=1.0 / 4194304.0, scalar2=1e-5, op0=OP.mult,
                            op1=OP.max).then_inc(sV.h, 1)
            e.wait_ge(sV.h, v0 + 2)
            e.reciprocal(out=srow[0:1, 2:4], in_=srow[0:1, 0:2]).then_inc(sV.h, 1)
        block.vector(wq_v2)

        def wq_t2(e):
            e.wait_ge(sV.h, v0 + 3)
            e.matmul(PS[1][:, 0:4], ones_k1f[:], srow[0:1, 0:4], start=True,
                     stop=True).then_inc(sT.h, 1)
        block.tensor(wq_t2)

        def wq_v3(e):
            e.wait_ge(sT.h, t0 + 2)
            e.tensor_copy(out=scol[:], in_=PS[1][:, 0:4]).then_inc(sV.h, 1)
        block.vector(wq_v3)
        sV.n += 4
        sT.n += 2
        vq0 = sV.n
        dq0 = dS.n
        dq0b = dG2.n
        daq0 = dA.n
        daq0b = dA2.n

        def wq_dma2(e, par=0):
            i = 0
            for (src, dst, n, wd) in CH_ALL:
                for g in range(n):
                    if i % 2 == par:
                        e.wait_ge(sV.h, vq0)
                        if i >= 2:
                            e.wait_ge(sV.h, vq0 + 3 * (i - 1))
                            if par == 0:
                                e.wait_ge(dS.h, dq0 + 16 * (i // 2))
                        qq = dS if i % 2 == 0 else dG2
                        e.dma_start(out=wwk[:, i % 2, 0:wd],
                                    in_=d[src][g]).then_inc(qq.h, 16)
                    i += 1
        block.sync(lambda e: wq_dma2(e, 0))
        block.gpsimd(lambda e: wq_dma2(e, 1))
        dS.n += 16 * ((NCHW + 1) // 2)
        dG2.n += 16 * (NCHW // 2)

        def wq_v4(e):
            i = 0
            for (src, dst, n, wd) in CH_ALL:
                si = 0 if src == "w1t" else 1
                for g in range(n):
                    qq, qb = (dS, dq0) if i % 2 == 0 else (dG2, dq0b)
                    e.wait_ge(qq.h, qb + 16 * (i // 2 + 1))
                    if i >= 2:
                        qa, qab = (dA, daq0) if i % 2 == 0 else (dA2, daq0b)
                        e.wait_ge(qa.h, qab + 16 * ((i - 2) // 2 + 1))
                    if i >= 1:
                        e.wait_ge(sV.h, vq0 + 3 * i)
                    e.tensor_scalar(out=wq1[:, 0:wd], in0=wwk[:, i % 2, 0:wd],
                                    scalar1=scol[:, 2 + si:3 + si], scalar2=MAGIC,
                                    op0=OP.mult, op1=OP.add).then_inc(sV.h, 1)
                    e.wait_ge(sV.h, vq0 + 3 * i + 1)
                    e.tensor_scalar(out=wq1[:, 0:wd], in0=wq1[:, 0:wd], scalar1=MAGIC,
                                    scalar2=1.0, op0=OP.subtract,
                                    op1=OP.min).then_inc(sV.h, 1)
                    e.wait_ge(sV.h, vq0 + 3 * i + 2)
                    e.tensor_scalar(out=wqo[:, i % 2, 0:wd], in0=wq1[:, 0:wd],
                                    scalar1=-1.0, scalar2=scol[:, si:si + 1],
                                    op0=OP.max, op1=OP.mult).then_inc(sV.h, 1)
                    i += 1
        block.vector(wq_v4)
        sV.n += 3 * NCHW

        def wq_dma3(e):
            i = 0
            for (src, dst, n, wd) in CH_ALL:
                for g in range(n):
                    e.wait_ge(sV.h, vq0 + 3 * i + 3)
                    qa = dA if i % 2 == 0 else dA2
                    e.dma_start(out=d[dst][g], in_=wqo[:, i % 2, 0:wd]).then_inc(qa.h, 16)
                    i += 1
        block.scalar(wq_dma3)
        dA.n += 16 * ((NCHW + 1) // 2)
        dA2.n += 16 * (NCHW // 2)
        barrier()
        wes.close()

    def ffn_call(tag, tapname=None):
        fes = contextlib.ExitStack()
        fsb = lambda n, s, dt: fes.enter_context(nc.sbuf_tensor(tag + n, list(s), dt))
        lnxt_b = fsb("lnxt", [128, G8, TT], BF16)
        xqfm = fsb("xqfm", [128, G8, TT], BF16)
        qf32 = fsb("qf32", [128, 1024], F32)
        w1b = fsb("w1b", [128, 2, G8, 512], BF16)
        htm = fsb("htm", [128, 4, HID], BF16)
        hqtm = fsb("hqtm", [128, HID], BF16)
        hqfm = fsb("hqfm", [128, G32, 512], BF16)
        w2b = fsb("w2b", [128, 2, D], BF16)
        qcols = fsb("qcols", [128, 8], F32)
        lnxT4 = lambda sub: hqtm[:][:, sub * 1024:(sub + 1) * 1024]
        W2S = [dW3, dW4, dW5, dW6]
        tb = {}
        itb = {}

        def emit_AB(j):
            A_T, A_V, A_S = sT.n, sV.n, sS.n
            entry = None
            if j > 0:
                entry = {"v": [(sT.h, tb[j - 1]["A_T"] + 1),
                               (sS.h, tb[j - 1]["A_S"] + 1)],
                         "t": [(sT.h, A_T)]}
            ln_fm(lambda g, tss: resid[:, g, tss],
                  lambda g, tss: lnxt_b[:, g, tss.start - j * TT:tss.stop - j * TT],
                  tiles=[j], entry=entry)
            for sub in range(4):
                bt0 = sT.n

                def b_t(e, sub=sub, A_V=A_V):
                    e.wait_ge(sV.h, A_V + 6 + (sub if sub else 0))
                    for g in range(G8 - 1):
                        e.transpose(PB(6, g), lnxt_b[:, g, sub * 128:(sub + 1) * 128],
                                    identb[:])
                    e.transpose(PB(6, G8 - 1),
                                lnxt_b[:, G8 - 1, sub * 128:(sub + 1) * 128],
                                identb[:]).then_inc(sT.h, 1)
                def b_v1(e, sub=sub, bt0=bt0):
                    e.wait_ge(sT.h, bt0 + 1)
                    for g in range(G8 - 1):
                        e.tensor_copy(out=lnxT4(sub)[:, g * 128:(g + 1) * 128],
                                      in_=PB(6, g))
                    e.tensor_copy(out=lnxT4(sub)[:, (G8 - 1) * 128:G8 * 128],
                                  in_=PB(6, G8 - 1)).then_inc(sV.h, 1)
                block.tensor(b_t)
                block.vector(b_v1)
                sT.n += 1
            tb[j] = dict(A_T=A_T, A_V=A_V, A_S=A_S, B1_end=A_V + 10)
            sV.n += 4

        def emit_B2(j):
            A_V, A_S = tb[j]["A_V"], tb[j]["A_S"]
            for sub in range(4):
                bv0, bs0 = sV.n, sS.n

                def b_v2(e, sub=sub, bv0=bv0, bs0=bs0, A_V=A_V):
                    e.wait_ge(sV.h, A_V + 7 + sub)
                    if sub:
                        e.wait_ge(sV.h, bv0)
                    e.tensor_reduce(out=qcols[:, 0:1], in_=lnxT4(sub), axis=AX,
                                    op=OP.max,
                                    apply_absolute_value=True).then_inc(sV.h, 1)
                    e.wait_ge(sV.h, bv0 + 1)
                    e.tensor_scalar(out=qcols[:, 1:2], in0=qcols[:, 0:1], scalar1=1e-5,
                                    scalar2=None, op0=OP.max).then_inc(sV.h, 1)
                    e.wait_ge(sS.h, bs0 + 2)
                    e.tensor_scalar(out=qcols[:, 3:4], in0=qcols[:, 1:2],
                                    scalar1=1.0 / 127.0, scalar2=None,
                                    op0=OP.mult).then_inc(sV.h, 1)
                    e.wait_ge(sS.h, bs0 + 3)
                    e.wait_ge(sV.h, bv0 + 3)
                    e.tensor_scalar(out=lnxT4(sub), in0=qf32[:, 0:D], scalar1=MAGIC,
                                    scalar2=qcols[:, 3:4], op0=OP.subtract,
                                    op1=OP.mult).then_inc(sV.h, 1)
                def b_s(e, sub=sub, bv0=bv0, bs0=bs0):
                    e.wait_ge(sV.h, bv0 + 2)
                    e.activation(out=qcols[:, 4:5], in_=qcols[:, 1:2],
                                 func=AF.Ln).then_inc(sS.h, 1)
                    e.wait_ge(sS.h, bs0 + 1)
                    e.activation(out=qcols[:, 2:3], in_=qcols[:, 4:5], func=AF.Exp,
                                 scale=-1.0, bias=LN127).then_inc(sS.h, 1)
                    e.wait_ge(sS.h, bs0 + 2)
                    e.activation(out=qf32[:, 0:D], in_=lnxT4(sub), func=AF.Copy,
                                 scale=qcols[:, 2:3], bias=MAGIC).then_inc(sS.h, 1)
                block.vector(b_v2)
                block.scalar(b_s)
                sV.n += 4
                sS.n += 3
            tb[j]["B2_end"] = sV.n

        def emit_B3(j, FV):
            XQ = []
            VB = sV.n
            for sub in range(4):
                bt0 = sT.n

                def b_t2(e, sub=sub, FV=FV, VB=VB):
                    e.wait_ge(sV.h, FV if sub == 0 else VB + sub)
                    for g in range(G8 - 1):
                        e.transpose(PB(5, g), lnxT4(sub)[:, g * 128:(g + 1) * 128],
                                    identb[:])
                    e.transpose(PB(5, G8 - 1),
                                lnxT4(sub)[:, (G8 - 1) * 128:G8 * 128],
                                identb[:]).then_inc(sT.h, 1)
                def b_v3(e, sub=sub, bt0=bt0):
                    e.wait_ge(sT.h, bt0 + 1)
                    for g in range(G8 - 1):
                        e.tensor_copy(out=xqfm[:, g, sub * 128:(sub + 1) * 128],
                                      in_=PB(5, g))
                    e.tensor_copy(out=xqfm[:, G8 - 1, sub * 128:(sub + 1) * 128],
                                  in_=PB(5, G8 - 1)).then_inc(sV.h, 1)
                block.tensor(b_t2)
                block.vector(b_v3)
                sT.n += 1
                XQ.append(VB + 1 + sub)
            sV.n += 4
            tb[j]["XQ"] = XQ

        emit_AB(0)
        emit_B2(0)
        emit_B3(0, sV.n)

        for it in range(NTILE):
            ts = slice(it * TT, (it + 1) * TT)
            IT_T, IT_V, IT_S = sT.n, sV.n, sS.n
            last = (it == NTILE - 1)
            mm1_T0 = IT_T
            gelu_S0 = IT_S
            mm2_T0 = IT_T + (48 if last else 53)
            dlast_V = IT_V + 44
            itb[it] = dict(IT_T=IT_T, IT_V=IT_V, mm1_T0=mm1_T0,
                           mm2_T0=mm2_T0)

            w1d0 = [dW1.n, dW2.n]
            w2d0 = [c.n for c in W2S]
            pv = itb.get(it - 1)

            def w_dma(e, mm1_T0=mm1_T0, mm2_T0=mm2_T0, pv=pv):
                for q in range(8):
                    if q >= 2:
                        e.wait_ge(sT.h, mm1_T0 + 4 * q - 4)
                    elif pv is not None:
                        e.wait_ge(sT.h, pv["mm1_T0"] + 4 * (q + 6) + 4)
                    qq = dW1 if q % 2 == 0 else dW2
                    e.dma_start(
                        out=w1b[:, q % 2, :, :],
                        in_=d["w1q_s"].rearrange("g p h -> p g h")[
                            :, :, q * 512:(q + 1) * 512]).then_inc(qq.h, 16)
                for hc in range(G32):
                    if hc >= 2:
                        e.wait_ge(sT.h, mm2_T0 + hc - 1)
                    elif pv is not None:
                        e.wait_ge(sT.h, pv["mm2_T0"] + 31 + hc)
                    e.dma_start(out=w2b[:, hc % 2, :],
                                in_=d["w2q_s"][hc]).then_inc(W2S[hc % 4].h, 16)
            block.gpsimd(w_dma)
            dW1.n += 16 * 4
            dW2.n += 16 * 4
            for c in W2S:
                c.n += 16 * 8

            ct0, cs0 = sT.n, sS.n
            assert ct0 == mm1_T0 and cs0 == gelu_S0
            XQ = tb[it]["XQ"]

            def c_t(e, ct0=ct0, cs0=cs0, w1d0=w1d0, XQ=XQ):
                for q in range(8):
                    qq, qb = (dW1, w1d0[0]) if q % 2 == 0 else (dW2, w1d0[1])
                    e.wait_ge(qq.h, qb + 16 * (q // 2 + 1))
                    for sub in range(4):
                        i = q * 4 + sub
                        bk = i % 2
                        if q == 0:
                            e.wait_ge(sV.h, XQ[sub])
                        if i >= 2:
                            e.wait_ge(sS.h, cs0 + i - 1)
                        for kc in range(G8 - 1):
                            e.matmul(PS[bk][:],
                                     xqfm[:, kc, sub * 128:(sub + 1) * 128],
                                     w1b[:, q % 2, kc, :],
                                     start=(kc == 0), stop=False)
                        e.matmul(PS[bk][:],
                                 xqfm[:, G8 - 1, sub * 128:(sub + 1) * 128],
                                 w1b[:, q % 2, G8 - 1, :],
                                 start=False, stop=True).then_inc(sT.h, 1)
            def c_s(e, ct0=ct0):
                i = 0
                for q in range(8):
                    for sub in range(4):
                        bk = i % 2
                        e.wait_ge(sT.h, ct0 + i + 1)
                        e.activation(out=htm[:, sub, q * 512:(q + 1) * 512],
                                     in_=PS[bk][:],
                                     func=AF.Gelu).then_inc(sS.h, 1)
                        i += 1
            block.tensor(c_t)
            block.scalar(c_s)
            sT.n += 32
            sS.n += 32

            D_T0 = sT.n
            assert D_T0 == IT_T + 32
            for sub in range(4):
                dt0, dv0, ds0 = sT.n, sV.n, sS.n

                def d_v(e, sub=sub, dv0=dv0, ds0=ds0, dt0=dt0, gelu_S0=gelu_S0):
                    e.wait_ge(sS.h, gelu_S0 + 29 if sub == 0
                              else gelu_S0 + 32 + 6 * sub)
                    if sub:
                        e.wait_ge(sT.h, dt0)
                    e.tensor_reduce(out=qcols[:, 4:5], in_=htm[:, sub, :], axis=AX,
                                    op=OP.max,
                                    apply_absolute_value=True).then_inc(sV.h, 1)
                    e.wait_ge(sV.h, dv0 + 1)
                    e.tensor_scalar(out=qcols[:, 5:6], in0=qcols[:, 4:5], scalar1=1e-5,
                                    scalar2=None, op0=OP.max).then_inc(sV.h, 1)
                    e.wait_ge(sS.h, ds0 + 2)
                    e.tensor_scalar(out=qcols[:, 7:8], in0=qcols[:, 5:6],
                                    scalar1=1.0 / 127.0, scalar2=None,
                                    op0=OP.mult).then_inc(sV.h, 1)
                    for hh in range(4):
                        hs = slice(hh * 1024, (hh + 1) * 1024)
                        e.wait_ge(sS.h, ds0 + 3 + hh)
                        e.wait_ge(sV.h, dv0 + 3 + hh)
                        e.tensor_scalar(out=hqtm[:, hs], in0=qf32[:], scalar1=MAGIC,
                                        scalar2=qcols[:, 7:8], op0=OP.subtract,
                                        op1=OP.mult).then_inc(sV.h, 1)
                    for blkz in range(4):
                        e.wait_ge(sT.h, dt0 + blkz + 1)
                        bk = 4 + (blkz % 2)
                        kk = None
                        for k in range(8):
                            hc = blkz * 8 + k
                            kk = e.tensor_copy(
                                out=hqfm[:, hc, sub * 128:(sub + 1) * 128],
                                in_=PB(bk, k))
                        kk.then_inc(sV.h, 1)
                def d_s(e, sub=sub, dv0=dv0, ds0=ds0):
                    e.wait_ge(sV.h, dv0 + 2)
                    e.activation(out=qcols[:, 6:7], in_=qcols[:, 5:6],
                                 func=AF.Ln).then_inc(sS.h, 1)
                    e.wait_ge(sS.h, ds0 + 1)
                    e.activation(out=qcols[:, 6:7], in_=qcols[:, 6:7], func=AF.Exp,
                                 scale=-1.0, bias=LN127).then_inc(sS.h, 1)
                    for hh in range(4):
                        hs = slice(hh * 1024, (hh + 1) * 1024)
                        e.wait_ge(sS.h, ds0 + 2 + hh)
                        if hh:
                            e.wait_ge(sV.h, dv0 + 3 + hh)
                        e.activation(out=qf32[:], in_=htm[:, sub, hs], func=AF.Copy,
                                     scale=qcols[:, 6:7],
                                     bias=MAGIC).then_inc(sS.h, 1)
                def d_t(e, dv0=dv0, dt0=dt0):
                    for blkz in range(4):
                        bk = 4 + (blkz % 2)
                        e.wait_ge(sV.h, dv0 + 4 + blkz)
                        if blkz >= 2:
                            e.wait_ge(sV.h, dv0 + 6 + blkz)
                        for k in range(8 - 1):
                            hc = blkz * 8 + k
                            e.transpose(PB(bk, k), hqtm[:, hc * 128:(hc + 1) * 128],
                                        identb[:])
                        hc = blkz * 8 + 7
                        e.transpose(PB(bk, 7), hqtm[:, hc * 128:(hc + 1) * 128],
                                    identb[:]).then_inc(sT.h, 1)
                block.vector(d_v)
                block.scalar(d_s)
                block.tensor(d_t)
                sV.n += 11
                sS.n += 6
                sT.n += 4
            assert sV.n == dlast_V

            if not last:
                emit_AB(it + 1)
                emit_B2(it + 1)

            ft0, fv0 = sT.n, sV.n
            assert ft0 == mm2_T0
            fwait = dlast_V if last else tb[it + 1]["B1_end"]

            def f_t(e, ft0=ft0, w2d0=w2d0, fwait=fwait):
                e.wait_ge(sV.h, fwait)
                for hc in range(G32):
                    e.wait_ge(W2S[hc % 4].h, w2d0[hc % 4] + 16 * (hc // 4 + 1))
                    sl = hc % 2
                    for g in range(G8 - 1):
                        e.matmul(PS[g][:], w2b[:, sl, g * 128:(g + 1) * 128],
                                 hqfm[:, hc, :], start=(hc == 0),
                                 stop=(hc == G32 - 1))
                    e.matmul(PS[G8 - 1][:], w2b[:, sl, (G8 - 1) * 128:G8 * 128],
                             hqfm[:, hc, :], start=(hc == 0),
                             stop=(hc == G32 - 1)).then_inc(sT.h, 1)
            def f_v(e, ts=ts, ft0=ft0):
                e.wait_ge(sT.h, ft0 + G32)
                for g in range(G8 - 1):
                    e.tensor_add(out=resid[:, g, ts], in0=PS[g][:],
                                 in1=resid[:, g, ts])
                e.tensor_add(out=resid[:, G8 - 1, ts], in0=PS[G8 - 1][:],
                             in1=resid[:, G8 - 1, ts]).then_inc(sV.h, 1)
            block.tensor(f_t)
            block.vector(f_v)
            sT.n += G32
            sV.n += 1
            if not last:
                emit_B3(it + 1, sV.n)
            assert sT.n == IT_T + (80 if last else 89)
            assert sV.n == IT_V + (45 if last else 75)

        barrier()
        fes.close()
        if tapname is not None:
            tap_now(tapname, [(resid[:].rearrange("p g t -> p (g t)"),
                               [128, G8 * TC], F32)])

    if upto >= 4:
        emit_wquant()
        if b.taps and "wq" in b.taps:
            t1 = b.dram("tap_wq_0", [128, HID], BF16, kind="ExternalOutput")
            t2 = b.dram("tap_wq_1", [128, D], BF16, kind="ExternalOutput")
            b.tap_tensors["wq_0"] = t1
            b.tap_tensors["wq_1"] = t2

            def tpw(e):
                e.dma_start(out=t1, in_=d["w1q_s"][0]).then_inc(dS.h, 16)
                e.dma_start(out=t2, in_=d["w2q_s"][0]).then_inc(dS.h, 16)
            block.sync(tpw)
            dS.n += 32
            barrier()
        ffn_call("f1", tapname="ffn1out")

    if upto >= 5:
      with contextlib.ExitStack() as mes:
        msb = lambda n, s, dt: mes.enter_context(nc.sbuf_tensor(n, list(s), dt))
        haloxin = msb("haloxin", [128, G16, 4], BF16)

        def mh_init(e):
            e.memset(haloxin[:], 0.0).then_inc(sV.h, 1)
        block.vector(mh_init)
        sV.n += 1
        barrier()

        with contextlib.ExitStack() as m1es:
            lnx3 = m1es.enter_context(nc.sbuf_tensor("lnx3", [128, G8, TC], BF16))
            win_sb = m1es.enter_context(nc.sbuf_tensor("win_sb", [128, G8, HID], BF16))
            xzev = m1es.enter_context(nc.sbuf_tensor("xzev", [128, 2, TT], BF16))
            chal = m1es.enter_context(nc.sbuf_tensor("chal", [128, 48], BF16))

            ln_fm(lambda g, ts: resid[:, g, ts], lambda g, ts: lnx3[:, g, ts])

            def m1_wdma(e):
                for g in range(G8):
                    qq = dG if g % 2 == 0 else dG2
                    e.dma_start(out=win_sb[:, g, :], in_=d["win_w"][g]).then_inc(qq.h, 16)
            block.gpsimd(m1_wdma)
            dG.n += 16 * 4
            dG2.n += 16 * 4
            barrier()

            def m1_lndma(e):
                for g in range(G8):
                    qa = dA if g % 2 == 0 else dA2
                    e.dma_start(out=d["lnx3_s"][g], in_=lnx3[:, g, :]).then_inc(qa.h, 16)
            block.scalar(m1_lndma)
            dA.n += 16 * 4
            dA2.n += 16 * 4

            m1t0, m1v0, m1s0, m1d0 = sT.n, sV.n, sS.n, dS.n
            m1d0b = dA2.n

            def _cntx(i):
                it_, mc_ = divmod(i, 32)
                return it_ * 16 + min(mc_, 16)

            def _cntz(i):
                it_, mc_ = divmod(i, 32)
                return it_ * 16 + max(mc_ - 16, 0)

            def m1_t(e):
                i = 0
                for it in range(NTILE):
                    ts = slice(it * TT, (it + 1) * TT)
                    for mc in range(32):
                        bk = i % 2
                        if i >= 2:
                            if (i - 2) % 32 < 16:
                                e.wait_ge(sV.h, m1v0 + _cntx(i - 1))
                            else:
                                e.wait_ge(sS.h, m1s0 + _cntz(i - 1))
                        for kc in range(G8 - 1):
                            e.matmul(PS[bk][:], win_sb[:, kc, mc * 128:(mc + 1) * 128],
                                     lnx3[:, kc, ts], start=(kc == 0), stop=False)
                        e.matmul(PS[bk][:], win_sb[:, G8 - 1, mc * 128:(mc + 1) * 128],
                                 lnx3[:, G8 - 1, ts], start=False,
                                 stop=True).then_inc(sT.h, 1)
                        i += 1
                for mc in range(16):
                    for kc in range(G8 - 1):
                        e.matmul(PS[2][:, mc * 3:(mc + 1) * 3],
                                 win_sb[:, kc, mc * 128:(mc + 1) * 128],
                                 lnx3[:, kc, TC - 3:TC], start=(kc == 0), stop=False)
                    e.matmul(PS[2][:, mc * 3:(mc + 1) * 3],
                             win_sb[:, G8 - 1, mc * 128:(mc + 1) * 128],
                             lnx3[:, G8 - 1, TC - 3:TC], start=False, stop=True)
                e.matmul(PS[3][0:1, 0:1], identb[:, 0:1], identb[:, 0:1],
                         start=True, stop=True).then_inc(sT.h, 1)
            block.tensor(m1_t)

            def m1_v(e):
                i = 0
                for it in range(NTILE):
                    for mc in range(32):
                        bk = i % 2
                        if mc < 16:
                            e.wait_ge(sT.h, m1t0 + i + 1)
                            if i >= 2:
                                qq, qb = (dS, m1d0) if bk == 0 else (dA2, m1d0b)
                                e.wait_ge(qq.h, qb + 16 * ((i - 2) // 2 + 1))
                            e.tensor_copy(out=xzev[:, bk, :],
                                          in_=PS[bk][:]).then_inc(sV.h, 1)
                        i += 1
            block.vector(m1_v)

            def m1_sil(e):
                i = 0
                for it in range(NTILE):
                    for mc in range(32):
                        bk = i % 2
                        if mc >= 16:
                            e.wait_ge(sT.h, m1t0 + i + 1)
                            if i >= 2:
                                qq, qb = (dS, m1d0) if bk == 0 else (dA2, m1d0b)
                                e.wait_ge(qq.h, qb + 16 * ((i - 2) // 2 + 1))
                            e.activation(out=xzev[:, bk, :], in_=PS[bk][:],
                                         func=AF.Silu).then_inc(sS.h, 1)
                        i += 1
            block.scalar(m1_sil)

            def m1_odma(e):
                i = 0
                for it in range(NTILE):
                    ts = slice(it * TT, (it + 1) * TT)
                    for mc in range(32):
                        bk = i % 2
                        qq, qb = (dS, m1d0) if bk == 0 else (dA2, m1d0b)
                        if i >= 2:
                            e.wait_ge(qq.h, qb + 16 * ((i - 2) // 2 + 1))
                        if mc < 16:
                            e.wait_ge(sV.h, m1v0 + _cntx(i) + 1)
                            e.dma_start(out=d["xin_s"][mc, :, ts],
                                        in_=xzev[:, bk, :]).then_inc(qq.h, 16)
                        else:
                            e.wait_ge(sS.h, m1s0 + _cntz(i) + 1)
                            e.dma_start(out=d["sz_s"][mc - 16, :, ts],
                                        in_=xzev[:, bk, :]).then_inc(qq.h, 16)
                        i += 1
            block.sync(m1_odma)
            sT.n += NTILE * 32 + 1
            sV.n += NTILE * 16
            sS.n += NTILE * 16
            dS.n += 16 * NTILE * 16
            dA2.n += 16 * NTILE * 16
            barrier()

            def mh_v(e):
                e.tensor_copy(out=chal[:], in_=PS[2][:, 0:48]).then_inc(sV.h, 1)
            block.vector(mh_v)
            sV.n += 1
            mhv_done = sV.n

            def mh_dma(e):
                e.wait_ge(sV.h, mhv_done)
                e.dma_start(out=d["cch_in"], in_=chal[:]).then_inc(dS.h, 16)
            block.sync(mh_dma)
            dS.n += 16
            barrier()

            def mh_cc(e):
                e.collective_compute(
                    "AllGather", OP.bypass,
                    replica_groups=[list(range(8))],
                    ins=[d["cch_in"]],
                    outs=[d["cch_out"].rearrange("r a b -> (r a) b")],
                ).then_inc(cS.h, 1)
            block.gpsimd(mh_cc)
            cS.n += 1

            mhd0, mhv0 = dS.n, sV.n

            def mh_dma2(e):
                e.wait_ge(cS.h, cS.n)
                for s in range(8):
                    if s > 0:
                        e.wait_ge(sV.h, mhv0 + s)
                    e.dma_start(out=chal[:], in_=d["cch_out"][s]).then_inc(dS.h, 16)
            block.sync(mh_dma2)
            dS.n += 16 * 8

            def mh_v2(e):
                for s in range(8):
                    e.wait_ge(dS.h, mhd0 + 16 * (s + 1))
                    e.scalar_tensor_tensor(
                        out=haloxin[:, :, 0:3],
                        in0=chal[:].rearrange("p (g r) -> p g r", g=G16),
                        scalar=maskprev_sb[:, s:s + 1],
                        in1=haloxin[:, :, 0:3],
                        op0=OP.mult, op1=OP.add).then_inc(sV.h, 1)
            block.vector(mh_v2)
            sV.n += 8
            barrier()
            tap_now("halo", [(haloxin[:].rearrange("p g r -> p (g r)"),
                              [128, G16 * 4], BF16),
                             (chal[:], [128, 48], BF16)])

        acols = msb("acols", [128, G16 * DST], F32)
        cwc = msb("cwc", [128, G16 * 4], F32)
        cbc = msb("cbc", [128, G16], F32)
        bdtc = msb("bdtc", [128, G16], F32)
        ddc = msb("ddc", [128, G16], F32)
        sel_sb = msb("sel_sb", [128, 32 * 128], BF16)
        wx_sb = msb("wx_sb", [128, G16 * 96], BF16)
        wdt_sb = msb("wdt_sb", [64, DIN], BF16)
        hcolA = msb("hcolA", [128, G16 * DST], F32)
        hcolC = msb("hcolC", [128, G16 * DST], F32)
        pcolC = msb("pcolC", [128, G16], F32)
        sumdp = msb("sumdp", [128, G16 * NTILE], F32)
        dcore = msb("dcore", [128, G16 * DST], F32)
        wkcc = msb("wkcc", [128, 512], F32)
        wkc2 = msb("wkc2", [128, 512], F32)

        def m0_dma(e):
            e.dma_start(out=cwc[:].rearrange("p (g k) -> p g k", g=G16),
                        in_=d["convw_c"]).then_inc(dS.h, 16)
            e.dma_start(out=cbc[:], in_=d["convb_c"]).then_inc(dS.h, 16)
            e.dma_start(out=bdtc[:], in_=d["bdt_c"]).then_inc(dS.h, 16)
            e.dma_start(out=ddc[:], in_=d["dd_c"]).then_inc(dS.h, 16)
            e.dma_start(out=sel_sb[:], in_=d["sel32"]).then_inc(dS.h, 16)
            e.dma_start(out=acols[:].rearrange("p (g s) -> p g s", g=G16),
                        in_=d["alog_c"]).then_inc(dS.h, 16)
        block.sync(m0_dma)
        dS.n += 16 * 6

        def m0_gdma(e):
            e.dma_start(out=wx_sb[:].rearrange("p (g w) -> p g w", g=G16),
                        in_=d["wx_w"].rearrange("g p w -> p g w")).then_inc(dG.h, 16)
            e.dma_start(out=wdt_sb[:], in_=d["wdt_w"]).then_inc(dG.h, 16)
        block.gpsimd(m0_gdma)
        dG.n += 32
        barrier()

        m0v, m0s = sV.n, sS.n

        def m0_s(e):
            e.activation(out=acols[:], in_=acols[:], func=AF.Exp).then_inc(sS.h, 1)
        block.scalar(m0_s)
        sS.n += 1

        def m0_v(e):
            e.wait_ge(sS.h, m0s + 1)
            e.tensor_scalar(out=acols[:], in0=acols[:], scalar1=-1.0, scalar2=None,
                            op0=OP.mult).then_inc(sV.h, 1)
            e.memset(hcolA[:], 0.0)
            e.memset(hcolC[:], 0.0)
            e.memset(pcolC[:], 0.0)
            e.memset(sumdp[:], 0.0).then_inc(sV.h, 1)
        block.vector(m0_v)
        sV.n += 2
        barrier()

        cvin = msb("cvin", [128, G16, 520], BF16)
        xct = msb("xct", [128, G16, TT], BF16)
        delt = msb("delt", [128, G16, TT], BF16)
        mt = msb("mt", [128, G16, TT], BF16)

        dbcsb = msb("dbcsb", [128, TT], BF16)
        stg = msb("stg", [128, 2, TT], BF16)
        brep = msb("brep", [128, 2, TT], BF16)
        crep = msb("crep", [128, 2, TT], BF16)
        dab = msb("dab", [128, 2, TT], BF16)
        wbu = msb("wbu", [128, 2, TT], BF16)
        scno = msb("scno", [128, TT], F32)
        wbuP = msb("wbuP", [128, 2, TT], BF16)
        scnoP = msb("scnoP", [128, 2, TT], F32)
        accv2 = lambda g: scnoP[:][:, 0, :] if g % 2 == 0 else scno[:]
        sps = lambda mc: scno[:] if mc % 2 == 0 else scnoP[:][:, 1, :]
        woutb = msb("woutb", [128, 2, D], BF16)
        lnh = xct[:][:, 0:G8, :]
        yac = cvin[:].rearrange("p g r -> p (g r)")[:, 0:G16 * TT].rearrange(
            "p (g t) -> p g t", g=G16)
        ymix = mt[:]
        h4t = delt[:].rearrange("p g t -> p (g t)").bitcast(F32).rearrange(
            "p (g t) -> p g t", g=G8)

        def emit_tile_prep(it):
            ts = slice(it * TT, (it + 1) * TT)
            pd0 = dS.n
            do_tap = (it == 0)

            def p_dma(e, pd0=pd0):
                if pd0:
                    e.wait_ge(dS.h, pd0)
                for g in range(G16):
                    e.dma_start(out=cvin[:, g, 3:515],
                                in_=d["xin_s"][g, :, ts]).then_inc(dS.h, 16)
                if it > 0:
                    for g in range(G16):
                        e.dma_start(out=cvin[:, g, 0:3],
                                    in_=d["xin_s"][g, :, it * TT - 3:it * TT]
                                    ).then_inc(dS.h, 16)
            block.sync(p_dma)
            dS.n += 16 * (G16 * (2 if it > 0 else 1))

            if it == 0:
                vh0 = sV.n

                def p_v0(e):
                    e.wait_ge(dS.h, dS.n)
                    e.tensor_copy(out=cvin[:, :, 0:3],
                                  in_=haloxin[:, :, 0:3]).then_inc(sV.h, 1)
                block.vector(p_v0)
                sV.n += 1
            barrier()

            pv1, ps1, pa1, pa1b = sV.n, sS.n, dA.n, dA2.n

            def p_cv(e):
                for g in range(G16):
                    if g >= 2:
                        e.wait_ge(sS.h, ps1 + g - 1)
                    e.tensor_scalar(out=accv2(g), in0=cvin[:, g, 3:515],
                                    scalar1=cwc[:, g * 4 + 3:g * 4 + 4],
                                    scalar2=cbc[:, g:g + 1], op0=OP.mult,
                                    op1=OP.add).then_inc(sV.h, 1)
                    for k in range(3):
                        e.wait_ge(sV.h, pv1 + 4 * g + 1 + k)
                        e.scalar_tensor_tensor(out=accv2(g), in0=cvin[:, g, k:512 + k],
                                               scalar=cwc[:, g * 4 + k:g * 4 + k + 1],
                                               in1=accv2(g), op0=OP.mult,
                                               op1=OP.add).then_inc(sV.h, 1)
            block.vector(p_cv)

            def p_cs(e):
                for g in range(G16):
                    e.wait_ge(sV.h, pv1 + 4 * g + 4)
                    e.activation(out=xct[:, g, :], in_=accv2(g),
                                 func=AF.Silu).then_inc(sS.h, 1)
            block.scalar(p_cs)
            sV.n += 4 * G16
            sS.n += G16
            pv2 = sV.n

            def p_cd(e):
                for g in range(G16):
                    e.wait_ge(sS.h, ps1 + g + 1)
                    if g >= 2:
                        qa, qb = (dA, pa1) if g % 2 == 0 else (dA2, pa1b)
                        e.wait_ge(qa.h, qb + 16 * ((g - 2) // 2 + 1))
                    e.tensor_scalar(out=stg[:, g % 2, :], in0=xct[:, g, :],
                                    scalar1=ddc[:, g:g + 1], scalar2=None,
                                    op0=OP.mult).then_inc(sV.h, 1)
            block.vector(p_cd)

            def p_cddma(e):
                for g in range(G16):
                    e.wait_ge(sV.h, pv2 + g + 1)
                    qa, qb2 = (dA, pa1) if g % 2 == 0 else (dA2, pa1b)
                    if g >= 2:
                        e.wait_ge(qa.h, qb2 + 16 * ((g - 2) // 2 + 1))
                    e.dma_start(out=d["xcd_s"][g, :, ts],
                                in_=stg[:, g % 2, :]).then_inc(qa.h, 16)
            block.scalar(p_cddma)
            sV.n += G16
            dA.n += 16 * (G16 // 2)
            dA2.n += 16 * (G16 // 2)
            barrier()

            qt0, qv0 = sT.n, sV.n

            def p_qt(e):
                for kc in range(G16 - 1):
                    e.matmul(PS[0][0:96, 0:TT], wx_sb[:, kc * 96:(kc + 1) * 96],
                             xct[:, kc, :], start=(kc == 0), stop=False)
                e.matmul(PS[0][0:96, 0:TT], wx_sb[:, (G16 - 1) * 96:G16 * 96],
                         xct[:, G16 - 1, :], start=False, stop=True).then_inc(sT.h, 1)
            block.tensor(p_qt)

            def p_qv(e):
                e.wait_ge(sT.h, qt0 + 1)
                e.tensor_copy(out=dbcsb[0:96, :],
                              in_=PS[0][0:96, 0:TT]).then_inc(sV.h, 1)
            block.vector(p_qv)

            def p_qdma(e):
                e.wait_ge(sV.h, qv0 + 1)
                e.dma_start(out=d["bc_s"][:, ts], in_=dbcsb[64:96, :]).then_inc(dS.h, 16)
            block.sync(p_qdma)
            sT.n += 1
            sV.n += 1
            dS.n += 16
            barrier()

            rt0, rv0, rs0 = sT.n, sV.n, sS.n

            def p_rt(e):
                for mc in range(G16):
                    bk = mc % 2
                    if mc >= 2:
                        e.wait_ge(sS.h, rs0 + 2 * (mc - 1))
                    e.matmul(PS[bk][:, 0:TT], wdt_sb[:, mc * 128:(mc + 1) * 128],
                             dbcsb[0:64, :], start=True, stop=True).then_inc(sT.h, 1)
            block.tensor(p_rt)

            def p_rs(e):
                for mc in range(G16):
                    bk = mc % 2
                    e.wait_ge(sT.h, rt0 + mc + 1)
                    if mc >= 1:
                        e.wait_ge(sV.h, rv0 + 2 * mc - 1)
                    e.activation(out=sps(mc), in_=PS[bk][:, 0:TT], func=AF.Exp,
                                 bias=bdtc[:, mc:mc + 1]).then_inc(sS.h, 1)
                    e.wait_ge(sV.h, rv0 + 2 * mc + 1)
                    e.activation(out=delt[:, mc, :], in_=sps(mc), func=AF.Ln,
                                 accum_out=sumdp[:, it * G16 + mc:it * G16 + mc + 1]
                                 ).then_inc(sS.h, 1)
            block.scalar(p_rs)

            def p_rv(e):
                for mc in range(G16):
                    e.wait_ge(sS.h, rs0 + 2 * mc + 1)
                    e.tensor_scalar(out=sps(mc), in0=sps(mc),
                                    scalar1=1.0, scalar2=None,
                                    op0=OP.add).then_inc(sV.h, 1)
                    e.wait_ge(sS.h, rs0 + 2 * mc + 2)
                    e.tensor_mul(out=mt[:, mc, :], in0=delt[:, mc, :],
                                 in1=xct[:, mc, :]).then_inc(sV.h, 1)
            block.vector(p_rv)
            sT.n += G16
            sS.n += 2 * G16
            sV.n += 2 * G16
            barrier()

            if do_tap:
                tap_now("xct0", [(xct[:].rearrange("p g t -> p (g t)"),
                                  [128, G16 * TT], BF16),
                                 (cvin[:].rearrange("p g t -> p (g t)"),
                                  [128, G16 * 520], BF16),
                                 (delt[:].rearrange("p g t -> p (g t)"),
                                  [128, G16 * TT], BF16)])

            dd0 = dS.n

            def p_sdma(e):
                for g in range(G16):
                    e.dma_start(out=d["delta_s"][g, :, ts],
                                in_=delt[:, g, :]).then_inc(dS.h, 16)
            block.sync(p_sdma)
            dS.n += 16 * G16
            barrier()

        dabPR = [dab[:].rearrange("p a t -> p (a t)"),
                 wbuP[:].rearrange("p a t -> p (a t)")]
        wPR = [wbu[:].rearrange("p a t -> p (a t)"),
               stg[:].rearrange("p a t -> p (a t)")]
        scnoF = scnoP[:].rearrange("p a t -> p (a t)")
        NPAIR = G16 // 2
        NVP = 6

        def emit_scan_tile(it, passC, hcol_ov=None):
            assert passC
            ts = slice(it * TT, (it + 1) * TT)
            hcol = hcol_ov if hcol_ov is not None else hcolC
            hview = hcol.rearrange("p (g d) -> p g d", g=G16)
            st0, sv0, ss0 = sT.n, sV.n, sS.n
            PV = 1 + NPAIR * NVP

            def scan_pos(s, j):
                return sv0 + s * PV + 1 + j * NVP + 4

            def s_t(e):
                for s in range(DST):
                    bs = s % 2
                    if s >= 2:
                        e.wait_ge(sV.h, sv0 + (s - 1) * PV)
                    e.matmul(PS[4 + bs][:, 0:TT],
                             sel_sb[64:96, s * 128:(s + 1) * 128],
                             dbcsb[64:96, :], start=True, stop=True)
                    e.matmul(PS[6 + bs][:, 0:TT],
                             sel_sb[64:96, (16 + s) * 128:(17 + s) * 128],
                             dbcsb[64:96, :], start=True,
                             stop=True).then_inc(sT.h, 1)
            block.tensor(s_t)

            def s_s(e):
                for s in range(DST):
                    for dg in range(G16):
                        j, h = divmod(dg, 2)
                        if h == 0 and (j >= 2 or s >= 1):
                            s2, j2 = (s, j - 2) if j >= 2 else (s - 1, j + NPAIR - 2)
                            e.wait_ge(sV.h, scan_pos(s2, j2))
                        dslot = dabPR[j % 2].rearrange("p (a t) -> p a t", a=2)
                        e.activation(out=dslot[:, h, :], in_=delt[:, dg, :],
                                     func=AF.Exp,
                                     scale=acols[:, dg * DST + s:dg * DST + s + 1]
                                     ).then_inc(sS.h, 1)
            block.scalar(s_s)

            def s_v(e):
                v = sv0
                for s in range(DST):
                    bs = s % 2
                    e.wait_ge(sT.h, st0 + s + 1)
                    if s >= 1:
                        e.wait_ge(sV.h, v)
                    e.tensor_copy(out=brep[:, bs, :], in_=PS[4 + bs][:, 0:TT])
                    e.tensor_copy(out=crep[:, bs, :],
                                  in_=PS[6 + bs][:, 0:TT]).then_inc(sV.h, 1)
                    v += 1
                    for j in range(NPAIR):
                        dg0, dg1 = 2 * j, 2 * j + 1
                        dslot = dabPR[j % 2].rearrange("p (a t) -> p a t", a=2)
                        wslot = wPR[j % 2].rearrange("p (a t) -> p a t", a=2)
                        e.wait_ge(sS.h, ss0 + s * 16 + dg1 + 1)
                        e.wait_ge(sV.h, v)
                        e.tensor_mul(out=wslot[:, 0, :], in0=mt[:, dg0, :],
                                     in1=brep[:, bs, :])
                        e.tensor_mul(out=wslot[:, 1, :], in0=mt[:, dg1, :],
                                     in1=brep[:, bs, :]).then_inc(sV.h, 1)
                        v += 1
                        e.wait_ge(sV.h, v)
                        e.scalar_tensor_tensor(
                            out=wslot[:, 1, 0:1], in0=dslot[:, 1, 0:1],
                            scalar=hview[:, dg1, s:s + 1], in1=wslot[:, 1, 0:1],
                            op0=OP.mult, op1=OP.add).then_inc(sV.h, 1)
                        v += 1
                        e.wait_ge(sV.h, v)
                        e.tensor_scalar(out=dslot[:, 1, 0:1], in0=dslot[:, 1, 0:1],
                                        scalar1=0.0, scalar2=None,
                                        op0=OP.mult).then_inc(sV.h, 1)
                        v += 1
                        e.wait_ge(sV.h, v)
                        e.tensor_tensor_scan(
                            out=scnoF, data0=dabPR[j % 2], data1=wPR[j % 2],
                            initial=hview[:, dg0, s:s + 1],
                            op0=OP.mult, op1=OP.add).then_inc(sV.h, 1)
                        v += 1
                        e.wait_ge(sV.h, v)
                        e.tensor_copy(out=hview[:, 2 * j:2 * j + 2, s:s + 1],
                                      in_=scnoP[:][:, :, TT - 1:TT])
                        e.tensor_mul(
                            out=wslot[:],
                            in0=scnoP[:],
                            in1=crep[:, bs:bs + 1, :].broadcast_to([128, 2, TT])
                            ).then_inc(sV.h, 1)
                        v += 1
                        e.wait_ge(sV.h, v)
                        if s == 0:
                            e.tensor_copy(out=yac[:, 2 * j:2 * j + 2, :],
                                          in_=wslot[:]).then_inc(sV.h, 1)
                        else:
                            e.tensor_add(out=yac[:, 2 * j:2 * j + 2, :],
                                         in0=yac[:, 2 * j:2 * j + 2, :],
                                         in1=wslot[:]).then_inc(sV.h, 1)
                        v += 1
            block.vector(s_v)
            sT.n += DST
            sS.n += DST * G16
            sV.n += DST * PV
            barrier()

        for it in range(NTILE):
            emit_tile_prep(it)
            emit_scan_tile(it, passC=True, hcol_ov=hcolA)
            ts_z = slice(it * TT, (it + 1) * TT)

            def yz_dma(e, ts=ts_z):
                for g in range(G16):
                    e.dma_start(out=d["y_s"][g, :, ts],
                                in_=yac[:, g, :]).then_inc(dS.h, 16)
            block.sync(yz_dma)
            dS.n += 16 * G16

        g3v0, g3s0 = sV.n, sS.n

        def m3_v(e):
            for dg in range(G16):
                if dg > 0:
                    e.wait_ge(sV.h, g3v0 + 2 * dg)
                e.tensor_reduce(out=wkcc[:, dg:dg + 1],
                                in_=sumdp[:].rearrange("p (t g) -> p t g", g=G16)[:, :, dg],
                                axis=AX, op=OP.add).then_inc(sV.h, 1)
                e.wait_ge(sV.h, g3v0 + 2 * dg + 1)
                e.tensor_scalar(out=dcore[:, dg * DST:(dg + 1) * DST],
                                in0=acols[:, dg * DST:(dg + 1) * DST],
                                scalar1=wkcc[:, dg:dg + 1], scalar2=None,
                                op0=OP.mult).then_inc(sV.h, 1)
        block.vector(m3_v)
        sV.n += 2 * G16

        def m3_s(e):
            e.wait_ge(sV.h, g3v0 + 2 * G16)
            e.activation(out=dcore[:], in_=dcore[:], func=AF.Exp).then_inc(sS.h, 1)
        block.scalar(m3_s)
        sS.n += 1
        barrier()

        def m3_dma(e):
            e.dma_start(out=d["cc2_in"][:, 0:256], in_=dcore[:]).then_inc(dS.h, 16)
            e.dma_start(out=d["cc2_in"][:, 256:512], in_=hcolA[:]).then_inc(dS.h, 16)
        block.sync(m3_dma)
        dS.n += 32
        barrier()

        def m3_cc(e):
            e.collective_compute(
                "AllGather", OP.bypass,
                replica_groups=[list(range(8))],
                ins=[d["cc2_in"]],
                outs=[d["cc2_out"].rearrange("r a b -> (r a) b")],
            ).then_inc(cS.h, 1)
        block.gpsimd(m3_cc)
        cS.n += 1

        f2d0, f2v0 = dS.n, sV.n

        def m3_dma2(e):
            e.wait_ge(cS.h, cS.n)
            for s in range(8):
                if s > 0:
                    e.wait_ge(sV.h, f2v0 + 4 * s)
                e.dma_start(out=wkcc[:], in_=d["cc2_out"][s]).then_inc(dS.h, 16)
        block.sync(m3_dma2)
        dS.n += 16 * 8

        def m3_v2(e):
            for s in range(8):
                e.wait_ge(dS.h, f2d0 + 16 * (s + 1))
                e.tensor_scalar(out=wkc2[:, 0:256], in0=wkcc[:, 0:256], scalar1=1.0,
                                scalar2=None, op0=OP.subtract).then_inc(sV.h, 1)
                e.wait_ge(sV.h, f2v0 + 4 * s + 1)
                e.tensor_scalar(out=wkc2[:, 0:256], in0=wkc2[:, 0:256],
                                scalar1=maskpred[:, s:s + 1], scalar2=1.0,
                                op0=OP.mult, op1=OP.add).then_inc(sV.h, 1)
                e.wait_ge(sV.h, f2v0 + 4 * s + 2)
                e.tensor_mul(out=hcolC[:], in0=hcolC[:],
                             in1=wkc2[:, 0:256]).then_inc(sV.h, 1)
                e.wait_ge(sV.h, f2v0 + 4 * s + 3)
                e.scalar_tensor_tensor(out=hcolC[:], in0=wkcc[:, 256:512],
                                       scalar=maskpred[:, s:s + 1], in1=hcolC[:],
                                       op0=OP.mult, op1=OP.add).then_inc(sV.h, 1)
        block.vector(m3_v2)
        sV.n += 4 * 8
        barrier()
        tap_now("agg2", [(hcolA[:], [128, 256], F32), (dcore[:], [128, 256], F32),
                         (hcolC[:], [128, 256], F32)])

        Pb = mt[:]
        onesb = ones128[:][:, 0:1].broadcast_to([128, TT])
        for it in range(NTILE):
            ts = slice(it * TT, (it + 1) * TT)

            def rc_dma(e, ts=ts):
                for g in range(G16):
                    e.dma_start(out=delt[:, g, :],
                                in_=d["delta_s"][g, :, ts]).then_inc(dS.h, 16)
                for g in range(G16):
                    e.dma_start(out=yac[:, g, :],
                                in_=d["y_s"][g, :, ts]).then_inc(dS.h, 16)
                e.dma_start(out=dbcsb[64:96, :], in_=d["bc_s"][:, ts]).then_inc(dS.h, 16)
            block.sync(rc_dma)
            rc_d0 = dS.n
            dS.n += 16 * (2 * G16 + 1)

            rv0 = sV.n

            def r_v(e, rv0=rv0, rc_d0=rc_d0):
                v = rv0
                for dg in range(G16):
                    if dg == 0:
                        e.wait_ge(dS.h, rc_d0 + 16 * (2 * G16 + 1))
                    if dg > 0:
                        e.wait_ge(sV.h, v)
                    e.tensor_tensor_scan(
                        out=scno[:], data0=onesb, data1=delt[:, dg, :],
                        initial=pcolC[:, dg:dg + 1],
                        op0=OP.mult, op1=OP.add).then_inc(sV.h, 1)
                    v += 1
                    e.wait_ge(sV.h, v)
                    e.tensor_copy(out=Pb[:, dg, :], in_=scno[:]).then_inc(sV.h, 1)
                    v += 1
                    e.wait_ge(sV.h, v)
                    e.tensor_copy(out=pcolC[:, dg:dg + 1],
                                  in_=scno[:, TT - 1:TT]).then_inc(sV.h, 1)
                    v += 1
            block.vector(r_v)
            sV.n += 3 * G16
            barrier()

            yt0, yv0, ys0 = sT.n, sV.n, sS.n
            PV2 = 1 + 8 * 2

            def y_t(e, yt0=yt0, yv0=yv0):
                for s in range(DST):
                    bs = s % 2
                    if s == 0:
                        e.wait_ge(dS.h, dS.n)
                    if s >= 2:
                        e.wait_ge(sV.h, yv0 + (s - 1) * PV2)
                    e.matmul(PS[6 + bs][:, 0:TT],
                             sel_sb[64:96, (16 + s) * 128:(17 + s) * 128],
                             dbcsb[64:96, :], start=True,
                             stop=True).then_inc(sT.h, 1)
            block.tensor(y_t)

            def y_s2(e, yv0=yv0):
                for s in range(DST):
                    for dg in range(G16):
                        j, h = divmod(dg, 2)
                        gj = s * 8 + j
                        if h == 0 and gj >= 2:
                            s2, j2 = divmod(gj - 2, 8)
                            e.wait_ge(sV.h, yv0 + s2 * PV2 + 1 + j2 * 2 + 1)
                        dslot = dabPR[j % 2].rearrange("p (a t) -> p a t", a=2)
                        e.activation(out=dslot[:, h, :], in_=Pb[:, dg, :],
                                     func=AF.Exp,
                                     scale=acols[:, dg * DST + s:dg * DST + s + 1]
                                     ).then_inc(sS.h, 1)
            block.scalar(y_s2)

            def y_v(e, yt0=yt0, yv0=yv0, ys0=ys0):
                v = yv0
                for s in range(DST):
                    bs = s % 2
                    e.wait_ge(sT.h, yt0 + s + 1)
                    if s >= 1:
                        e.wait_ge(sV.h, v)
                    e.tensor_copy(out=crep[:, bs, :],
                                  in_=PS[6 + bs][:, 0:TT]).then_inc(sV.h, 1)
                    v += 1
                    for j in range(8):
                        dg0, dg1 = 2 * j, 2 * j + 1
                        e.wait_ge(sS.h, ys0 + s * 16 + dg1 + 1)
                        e.wait_ge(sV.h, v)
                        dslot = dabPR[j % 2].rearrange("p (a t) -> p a t", a=2)
                        e.scalar_tensor_tensor(
                            out=wbu[:, 0, :], in0=dslot[:, 0, :],
                            scalar=hcolC[:, dg0 * DST + s:dg0 * DST + s + 1],
                            in1=crep[:, bs, :], op0=OP.mult, op1=OP.mult)
                        e.scalar_tensor_tensor(
                            out=wbu[:, 1, :], in0=dslot[:, 1, :],
                            scalar=hcolC[:, dg1 * DST + s:dg1 * DST + s + 1],
                            in1=crep[:, bs, :], op0=OP.mult,
                            op1=OP.mult).then_inc(sV.h, 1)
                        v += 1
                        e.wait_ge(sV.h, v)
                        e.tensor_add(out=yac[:, 2 * j:2 * j + 2, :],
                                     in0=yac[:, 2 * j:2 * j + 2, :],
                                     in1=wbu[:].rearrange("p a t -> p a t")
                                     ).then_inc(sV.h, 1)
                        v += 1
            block.vector(y_v)
            sT.n += DST
            sS.n += DST * G16
            sV.n += DST * PV2
            barrier()

            t4v0, t4d0, t4db = sV.n, dS.n, dG.n

            def t4_dma(e, ts=ts):
                for g in range(G16):
                    if g >= 1:
                        e.wait_ge(dS.h, t4d0 + 32 * g)
                    if g >= 2:
                        e.wait_ge(sV.h, t4v0 + 2 * (g - 1))
                    e.dma_start(out=stg[:, 0, :] if g % 2 == 0 else stg[:, 1, :],
                                in_=d["xcd_s"][g, :, ts]).then_inc(dS.h, 16)
                    e.dma_start(out=brep[:, 0, :] if g % 2 == 0 else brep[:, 1, :],
                                in_=d["sz_s"][g, :, ts]).then_inc(dS.h, 16)
            block.sync(t4_dma)

            def t4_v(e, ts=ts):
                for g in range(G16):
                    e.wait_ge(dS.h, t4d0 + 32 * (g + 1))
                    if g >= 1:
                        e.wait_ge(sV.h, t4v0 + 2 * g)
                    e.tensor_add(out=wbu[:, g % 2, :], in0=yac[:, g, :],
                                 in1=stg[:, g % 2, :]).then_inc(sV.h, 1)
                    e.wait_ge(sV.h, t4v0 + 2 * g + 1)
                    e.tensor_mul(out=ymix[:, g, :], in0=wbu[:, g % 2, :],
                                 in1=brep[:, g % 2, :]).then_inc(sV.h, 1)
            block.vector(t4_v)
            sV.n += 2 * G16
            dS.n += 16 * 2 * G16
            barrier()

            w4t0, w4d0, w4db = sT.n, dG.n, dG2.n

            def w4_dma(e):
                for kc in range(G16):
                    if kc >= 2:
                        e.wait_ge(sT.h, w4t0 + kc - 1)
                    qq = dG if kc % 2 == 0 else dG2
                    e.dma_start(out=woutb[:, kc % 2, :],
                                in_=d["wout_w"][kc]).then_inc(qq.h, 16)
            block.gpsimd(w4_dma)

            def w4_t(e):
                for kc in range(G16):
                    if kc % 2 == 0:
                        e.wait_ge(dG.h, w4d0 + 16 * (kc // 2 + 1))
                    else:
                        e.wait_ge(dG2.h, w4db + 16 * (kc // 2 + 1))
                    for g in range(G8 - 1):
                        e.matmul(PS[g][:, 0:TT], woutb[:, kc % 2, g * 128:(g + 1) * 128],
                                 ymix[:, kc, :], start=(kc == 0), stop=(kc == G16 - 1))
                    e.matmul(PS[G8 - 1][:, 0:TT],
                             woutb[:, kc % 2, (G8 - 1) * 128:G8 * 128],
                             ymix[:, kc, :], start=(kc == 0),
                             stop=(kc == G16 - 1)).then_inc(sT.h, 1)
            block.tensor(w4_t)
            dG.n += 16 * (G16 // 2)
            dG2.n += 16 * (G16 // 2)
            sT.n += G16

            l4d0 = dA.n
            l4db = dA2.n

            def l4_dma(e, ts=ts):
                for g in range(G8):
                    qa = dA if g % 2 == 0 else dA2
                    e.dma_start(out=lnh[:, g, :],
                                in_=d["lnx3_s"][g, :, ts]).then_inc(qa.h, 16)
            block.scalar(l4_dma)
            dA.n += 16 * 4
            dA2.n += 16 * 4

            h4v0 = sV.n

            def h4_v(e):
                e.wait_ge(sT.h, sT.n)
                e.wait_ge(dA.h, dA.n)
                e.wait_ge(dA2.h, dA2.n)
                for g in range(G8 - 1):
                    e.tensor_add(out=h4t[:, g, :], in0=PS[g][:, 0:TT], in1=lnh[:, g, :])
                e.tensor_add(out=h4t[:, G8 - 1, :], in0=PS[G8 - 1][:, 0:TT],
                             in1=lnh[:, G8 - 1, :]).then_inc(sV.h, 1)
            block.vector(h4_v)
            sV.n += 1
            barrier()

            ln_fm(lambda g, tss: h4t[:, g, 0:TT],
                  lambda g, tss: lnh[:, g, 0:TT], tiles=[it])
            barrier()

            r4v0 = sV.n

            def r4_v(e, ts=ts):
                for g in range(G8):
                    if g > 0:
                        e.wait_ge(sV.h, r4v0 + 2 * g)
                    e.scalar_tensor_tensor(out=resid[:, g, ts], in0=lnh[:, g, :],
                                           scalar=gamma_sb[:, g:g + 1],
                                           in1=resid[:, g, ts], op0=OP.mult,
                                           op1=OP.add).then_inc(sV.h, 1)
                    e.wait_ge(sV.h, r4v0 + 2 * g + 1)
                    e.tensor_scalar(out=resid[:, g, ts], in0=resid[:, g, ts],
                                    scalar1=beta_sb[:, g:g + 1], scalar2=None,
                                    op0=OP.add).then_inc(sV.h, 1)
            block.vector(r4_v)
            sV.n += 2 * G8
            barrier()

        tap_now("mambaout", [(resid[:].rearrange("p g t -> p (g t)"),
                              [128, G8 * TC], F32)])

    if upto >= 6:
        ffn_call("f2")

    barrier()

    def out_sync(e):
        for g in range(G8):
            e.dma_start(out=d["y_out"][g], in_=resid[:, g, :]).then_inc(dS.h, 16)
        e.wait_ge(dS.h, dS.n + 16 * G8)
    block.sync(out_sync)
    dS.n += 16 * G8

def _mk_sel32():
    import ml_dtypes
    s = np.zeros((128, 32 * 128), np.float32)
    for j in range(32):
        s[64 + j, j * 128:(j + 1) * 128] = 1.0
    return s.astype(ml_dtypes.bfloat16)


def prep_inputs(inp):
    x = np.ascontiguousarray(np.asarray(inp["L0_attn_a_x"], np.float32))
    g = lambda n: np.ascontiguousarray(np.asarray(inp[n], np.float32))

    qkvg_w = np.stack([
        g("L0_attn_a_Wq").reshape(G8, 128, 128),
        g("L0_attn_a_Wk").reshape(G8, 128, 128),
        g("L0_attn_a_Wv").reshape(G8, 128, 128),
        g("L0_attn_a_Wg").reshape(G8, 128, 128),
    ])
    qkvg_b = np.stack([
        g("L0_attn_a_bq"), g("L0_attn_a_bk"), g("L0_attn_a_bv"), g("L0_attn_a_bg")
    ], axis=1)
    ab_w = np.concatenate([g("L0_attn_a_Wa"), g("L0_attn_a_Wb")], 1).reshape(G8, 128, 2)
    ab_b = np.zeros((128, 1), np.float32)
    ab_b[0] = g("L0_attn_a_ba")[0]
    ab_b[32] = g("L0_attn_a_bb")[0]
    wo_w = g("L0_attn_a_Wo")
    bo_c = g("L0_attn_a_bo").reshape(G8, 128).T.copy()

    win_w = g("L0_attn_b_W_in").reshape(G8, 128, HID)
    convw_c = g("L0_attn_b_conv_w").reshape(G16, 128, 4).transpose(1, 0, 2).copy()
    convb_c = g("L0_attn_b_conv_b").reshape(G16, 128).T.copy()
    wx_w = g("L0_attn_b_W_x").reshape(G16, 128, 96)
    wdt_w = g("L0_attn_b_W_dt")
    bdt_c = g("L0_attn_b_b_dt").reshape(G16, 128).T.copy()
    alog_c = g("L0_attn_b_A_log").reshape(G16, 128, DST).transpose(1, 0, 2).copy()
    dd_c = g("L0_attn_b_D").reshape(G16, 128).T.copy()
    gamma_c = g("L0_attn_b_gamma").reshape(G8, 128).T.copy()
    beta_c = g("L0_attn_b_beta").reshape(G8, 128).T.copy()
    wout_w = g("L0_attn_b_W_out").reshape(G16, 128, D)

    w1t = np.ascontiguousarray(g("L0_ffn_w1").T).reshape(G8, 128, HID)
    w2t = np.ascontiguousarray(g("L0_ffn_w2").T).reshape(G32, 128, D)

    import ml_dtypes
    identf = np.eye(128, dtype=np.float32)
    identb = identf.astype(ml_dtypes.bfloat16)
    tl = np.tril(np.ones((128, 128), np.float32), -1)
    mask_sn = -tl
    mask_snT = -tl.T.copy()
    mask_incT = np.triu(np.ones((128, 128), np.float32))
    ones_k1 = np.ones((1, 128), np.float32).astype(ml_dtypes.bfloat16)
    ones128 = np.ones((128, 128), np.float32).astype(ml_dtypes.bfloat16)

    shared = dict(
        qkvg_w=qkvg_w, qkvg_b=qkvg_b, ab_w=ab_w, ab_b=ab_b, wo_w=wo_w, bo_c=bo_c,
        win_w=win_w, convw_c=convw_c, convb_c=convb_c, wx_w=wx_w, wdt_w=wdt_w,
        bdt_c=bdt_c, alog_c=alog_c, dd_c=dd_c, gamma_c=gamma_c, beta_c=beta_c,
        wout_w=wout_w, w1t=w1t, w2t=w2t, identf=identf, identb=identb,
        mask_sn=mask_sn, mask_snT=mask_snT, mask_incT=mask_incT,
        ones_k1=ones_k1, ones128=ones128,
        ones_k1f=np.ones((1, 128), np.float32),
        sel127=np.zeros((128, 128), np.float32),
        ones128f=np.ones((128, 128), np.float32),
        sel32=_mk_sel32(),
        fconst_c=np.tile(np.array(FCONSTS, np.float32), (128, 1)),
    )
    shared["sel127"][127, :] = 1.0
    in_maps = []
    for c in range(8):
        bq, p = c // 4, c % 4
        xs = x[bq, p * TC:(p + 1) * TC]
        x_fm = np.ascontiguousarray(xs.T).reshape(G8, 128, TC)
        maskpred = np.zeros((128, 8), np.float32)
        for j in range(8):
            if j // 4 == bq and j % 4 < p:
                maskpred[:, j] = 1.0
        maskhalo = np.full((128, 1), 1.0 if p > 0 else 0.0, np.float32)
        maskprev = np.zeros((128, 8), np.float32)
        if p > 0:
            maskprev[:, c - 1] = 1.0
        in_maps.append(dict(shared, x_fm=x_fm, maskpred_c=maskpred,
                            maskhalo_c=maskhalo, maskprev_c=maskprev))
    return in_maps


def kernel(**inputs):
    in_maps = prep_inputs(inputs)
    nc, b = build_program()
    res = run_bass_kernel_spmd(nc, in_maps, core_ids=list(range(8)))
    out = np.zeros((2, 8192, 1024), np.float32)
    for c in range(8):
        y = res.results[c]["y_out"]
        bq, p = c // 4, c % 4
        out[bq, p * TC:(p + 1) * TC] = y.reshape(D, TC).T
    return out
